# version 12
# baseline (speedup 1.0000x reference)
import os
import numpy as np
import ml_dtypes
import concourse.bass as bass
import concourse.mybir as mybir
from concourse.bass_utils import run_bass_kernel_spmd
from contextlib import ExitStack

F32 = mybir.dt.float32
BF16 = mybir.dt.bfloat16
I32 = mybir.dt.int32
AF = mybir.ActivationFunctionType
ALU = mybir.AluOpType
AX = mybir.AxisListType

B, T, D = 8, 4096, 1024
MAGIC = 12582912.0
ENG = ("y", "t", "v", "s", "g")


class Sem:
    __slots__ = ("h", "n")

    def __init__(self, handle):
        self.h = handle
        self.n = 0


class P:

    def __init__(self, nc):
        self.nc = nc
        self.es = ExitStack()
        self.q = {e: [] for e in ENG}
        self.sems = {}
        self._perm = []
        self._cvecs = {}
        self._uid = 0
        self.scope_stack = []
        self.sem_prefix = ""
        self._sem_cms = {}
        self.dma_sems = set()
        self._cva = self.sb("cvarena", [128, 32])

    def uid(self):
        self._uid += 1
        return self._uid

    def sb(self, name, shape, dt=F32, scope=None):
        cm = self.nc.sbuf_tensor(f"{name}{self.uid()}", shape, dt)
        t = cm.__enter__()
        if scope is None and self.scope_stack:
            scope = self.scope_stack[-1]
        if scope is not None:
            scope.append(cm)
        else:
            self._perm.append(cm)
        return t

    def pt(self, name, shape, dt=F32, scope=None):
        cm = self.nc.psum_tensor(f"{name}{self.uid()}", shape, dt)
        t = cm.__enter__()
        if scope is None and self.scope_stack:
            scope = self.scope_stack[-1]
        if scope is not None:
            scope.append(cm)
        else:
            self._perm.append(cm)
        return t

    def close_scope(self, scope):
        for cm in reversed(scope):
            cm.__exit__(None, None, None)
        scope.clear()

    def sem(self, name):
        if not (name.startswith("prog_") or name.startswith("bar_")):
            name = self.sem_prefix + name
        if name not in self.sems:
            cm = self.nc.semaphore(name)
            h = cm.__enter__()
            self.sems[name] = Sem(h)
            self._sem_cms[name] = cm
        return self.sems[name]

    def release_sems(self, prefix):
        names = [n for n in self.sems if n.startswith(prefix)]
        for n in names:
            sm = self.sems[n]
            self.op("g", lambda eng, h=sm.h: eng.sem_clear(h))
        self.barrier("ytvsg")
        for n in names:
            self.dma_sems.discard(self.sems[n])
            self._sem_cms[n].__exit__(None, None, None)
            del self.sems[n]
            del self._sem_cms[n]

    def op(self, e, fn, sem=None, k=1, kind="i"):
        if kind == "i" and e in "vsg":
            assert sem is None
            ps = self.sem("prog_" + e)
            if ps.n > 0:
                self.q[e].append({"fn": (lambda eng, s=ps, t=ps.n:
                                         eng.wait_ge(s.h, t)),
                                  "sem": None, "k": 0, "kind": "w"})
            ps.n += 1
            self.q[e].append({"fn": fn, "sem": ps, "k": 1, "kind": "i"})
            return ps.n
        ent = {"fn": fn, "sem": sem, "k": k, "kind": kind}
        self.q[e].append(ent)
        if sem is not None:
            sem.n += k
            return sem.n
        return None

    def wait(self, e, sem, tgt=None):
        tv = sem.n if tgt is None else tgt
        if tv > 0:
            self.q[e].append({"fn": (lambda eng, s=sem, t=tv:
                                     eng.wait_ge(s.h, t)),
                              "sem": None, "k": 0, "kind": "w"})
        return tv

    def marker(self, e):
        s = self.sem("prog_" + e)
        if e in "vsg":
            if s.n == 0:
                s.n += 1
                self.q[e].append({"fn": lambda eng: eng.nop(), "sem": s,
                                  "k": 1, "kind": "i"})
            return s, s.n
        q = self.q[e]
        pre = s.n
        if q and q[-1]["kind"] == "i" and q[-1]["sem"] is None:
            if pre > 0:
                q.insert(len(q) - 1,
                         {"fn": (lambda eng, ss=s, t=pre:
                                 eng.wait_ge(ss.h, t)),
                          "sem": None, "k": 0, "kind": "w"})
            q[-1]["sem"] = s
            q[-1]["k"] = 1
            s.n += 1
            return s, s.n
        if pre > 0:
            q.append({"fn": (lambda eng, ss=s, t=pre: eng.wait_ge(ss.h, t)),
                      "sem": None, "k": 0, "kind": "w"})
        s.n += 1
        tv = s.n
        q.append({"fn": lambda eng: eng.nop(), "sem": s, "k": 1, "kind": "i"})
        return s, tv

    def cvec(self, val):
        key = float(val)
        if key not in self._cvecs:
            j = len(self._cvecs)
            assert j < 32, "cvec arena full"
            t = self._cva[:, j:j + 1]
            self.op("v", lambda eng, t=t, v=key: eng.memset(t, v))
            mark = self.marker("v")
            self._cvecs[key] = (t, mark)
        return self._cvecs[key]

    def barrier(self, engines="ytvsg"):
        e0 = engines[0]
        for ds in sorted(self.dma_sems, key=lambda x: id(x)):
            if ds.n > 0:
                self.wait(e0, ds, ds.n)
        marks = [self.marker(e) for e in engines]
        for e in engines:
            for s, tgt in marks:
                self.wait(e, s, tgt)

    def finalize(self):
        nc = self.nc
        with nc.Block() as block:
            for e in ENG:
                dec = dict(y=block.sync, t=block.tensor, v=block.vector,
                           s=block.scalar, g=block.gpsimd)[e]
                ops = self.q[e]

                def mk(ops=ops):
                    def _(eng):
                        for ent in ops:
                            r = ent["fn"](eng)
                            if ent["sem"] is not None:
                                r.then_inc(ent["sem"].h, ent["k"])
                    return _
                dec(mk())
        for cm in reversed(self._perm):
            cm.__exit__(None, None, None)
        self.es.close()
        return nc


def e_dma(p, e, dst, src, sem=None):
    sem = sem or p.sem("dma_" + e)
    p.dma_sems.add(sem)
    tgt = p.op(e, lambda eng: eng.dma_start(out=dst, in_=src), sem, 16,
               kind="d")
    return sem, tgt


def e_act(p, out, in_, func=AF.Identity, bias=0.0, scale=1.0, accum=None,
          sem=None):
    if isinstance(bias, float) and bias != 0.0:
        t, mark = p.cvec(bias)
        p.wait("s", mark[0], mark[1])
        bias = t[0:out.shape[0], :]
    def f(eng):
        kw = {}
        if accum is not None:
            kw["accum_out"] = accum
        return eng.activation(out=out, in_=in_, func=func, bias=bias,
                              scale=scale, **kw)
    return p.op("s", f, sem)


def e_tt(p, out, in0, in1, alu, e="v", sem=None):
    return p.op(e, lambda eng: eng.tensor_tensor(out=out, in0=in0, in1=in1,
                                                 op=alu), sem)


def e_ts(p, out, in0, s1, s2=None, op0=ALU.mult, op1=None, e="v", sem=None,
         accum=None):
    def f(eng):
        kw = dict(out=out, in0=in0, scalar1=s1, scalar2=s2, op0=op0)
        if op1 is not None:
            kw["op1"] = op1
        if accum is not None:
            kw["accum_out"] = accum
        return eng.tensor_scalar(**kw)
    return p.op(e, f, sem)


def e_stt(p, out, in0, scal, in1, op0, op1, e="v", sem=None):
    return p.op(e, lambda eng: eng.scalar_tensor_tensor(
        out=out, in0=in0, scalar=scal, in1=in1, op0=op0, op1=op1), sem)


def e_mm(p, out, lhsT, rhs, start=True, stop=True, sem=None):
    from concourse.bass_types import AP as _AP
    assert isinstance(out, _AP), f"out {type(out)}"
    assert isinstance(lhsT, _AP), f"lhsT {type(lhsT)}"
    assert isinstance(rhs, _AP), f"rhs {type(rhs)}"
    return p.op("t", lambda eng: eng.matmul(out, lhsT, rhs, start=start,
                                            stop=stop), sem)


def e_copy(p, out, in_, e="v", sem=None):
    return p.op(e, lambda eng: eng.tensor_copy(out=out, in_=in_), sem)


def e_memset(p, ap, val, e="v", sem=None):
    return p.op(e, lambda eng: eng.memset(ap, val), sem)


def e_softplus(p, out, in_, tmp, n0=0, nw=None, tmp0=False, in0=False):
    sl = (slice(None), slice(n0, (n0 + nw) if nw is not None else None))
    tl = (slice(None), slice(0, nw)) if tmp0 else sl
    il = tl if in0 else sl
    e_act(p, tmp[tl], in_[il], AF.Abs)
    e_act(p, tmp[tl], tmp[tl], AF.Exp, scale=-1.0)
    e_act(p, tmp[tl], tmp[tl], AF.Ln, bias=1.0)
    sm = p.marker("s")
    e_ts(p, out[sl], in_[il], 0.0, None, op0=ALU.max)
    p.wait("v", sm[0], sm[1])
    e_tt(p, out[sl], out[sl], tmp[tl], ALU.add)
    vm = p.marker("v")
    p.wait("s", vm[0], vm[1])


def lin_pool(p, max_nk, scope=None, np_ps=2):
    return dict(
        wbuf=[p.sb("wb", [128, max_nk * 128], BF16, scope=scope) for _ in range(2)],
        ps=[p.pt("lp", [128, 512], scope=scope) for _ in range(np_ps)],
        wsem=[p.sem("wlda"), p.sem("wldb")],
        prev=[None] * np_ps,
        wprev=[None, None],
        np_ps=np_ps,
    )


def linear_fm(p, pool, w_dram, x_tiles, K, M, evac, n_free=None, extra_k=None,
              w_fn=None):
    n_free = n_free or T
    """psum[(mb,tc)] = sum_k W[k-blk, m-blk]^T @ x[k][:, t-chunk].

    evac(mb, tc, t0, tw, psum_ap, tmark) is called at record time right
    after the unit's matmuls; must wait on tmark before touching psum and
    return a marker gating psum reuse.
    w_fn: optional override(mb, wsem)->wtarget recording the weight load for
    m-block mb into pool['wbuf'][mb%2] (for on-chip dequant etc).
    """
    nk = (K + 127) // 128
    nm = (M + 127) // 128
    assert nk * 128 <= pool["wbuf"][0].shape[-1]
    ntc = (n_free + 511) // 512
    wsems = pool["wsem"]
    wbuf, ps = pool["wbuf"], pool["ps"]
    emarks = list(pool["prev"])
    wmarks = list(pool["wprev"])
    wtargets = []

    def load_w(mb):
        wsem = wsems[mb % 2]
        if wmarks[mb % 2] is not None:
            p.wait("g", wmarks[mb % 2][0], wmarks[mb % 2][1])
        if w_fn is not None:
            wtargets.append(w_fn(mb, wsem))
            return
        m0 = mb * 128
        mw = min(128, M - m0)
        if K % 128 == 0:
            src = w_dram[:, m0:m0 + mw].rearrange("(kc p) m -> p kc m", p=128)
            dst = wbuf[mb % 2][:, 0:nk * mw].rearrange("p (kc m) -> p kc m",
                                                       kc=nk)
            _, wt = e_dma(p, "g", dst, src, wsem)
        else:
            for k in range(nk):
                k0 = k * 128
                kw = min(128, K - k0)
                _, wt = e_dma(p, "g", wbuf[mb % 2][0:kw, k * mw:k * mw + mw],
                              w_dram[k0:k0 + kw, m0:m0 + mw], wsem)
        wtargets.append(wt)

    load_w(0)
    if nm > 1:
        load_w(1)
    for mb in range(nm):
        m0 = mb * 128
        mw = min(128, M - m0)
        p.wait("t", wsems[mb % 2], wtargets[mb])
        for tc in range(ntc):
            t0 = tc * 512
            tw = min(512, n_free - t0)
            u = mb * ntc + tc
            nps = pool.get("np_ps", 2)
            pp = ps[u % nps]
            if emarks[u % nps] is not None:
                p.wait("t", emarks[u % nps][0], emarks[u % nps][1])
            for k in range(nk):
                kw = min(128, K - k * 128)
                e_mm(p, pp[0:mw, 0:tw], wbuf[mb % 2][0:kw, k * mw:k * mw + mw],
                     x_tiles[k][0:kw, t0:t0 + tw], start=(k == 0),
                     stop=(k == nk - 1 and extra_k is None))
            if extra_k is not None:
                extra_k(mb, tc, pp[0:mw, 0:tw], t0, tw, mw)
            tmark = p.marker("t")
            wmarks[mb % 2] = tmark
            em = evac(mb, tc, t0, tw, pp[0:mw, 0:tw], tmark)
            emarks[u % nps] = em
        if mb + 2 < nm:
            load_w(mb + 2)
    pool["prev"] = emarks
    pool["wprev"] = wmarks
    return emarks


def evac_to_tiles(p, dst_tiles, func=AF.Identity, bias=None, scale=1.0,
                  col0=0):
    def evac(mb, tc, t0, tw, psum, tmark):
        p.wait("s", tmark[0], tmark[1])
        mw = psum.shape[0]
        b = bias(mb) if bias is not None else 0.0
        sc = scale(mb) if callable(scale) else scale
        e_act(p, dst_tiles[mb][0:mw, col0 + t0:col0 + t0 + tw], psum, func,
              bias=b, scale=sc)
        return p.marker("s")
    return evac


def linear_tm(p, pool, w_tiles, x_tiles, K, N, evac, bias_row=None,
              ones_row=None, n_free=None):
    n_free = n_free or T
    """out[(tt),(0:N)] = sum_k x[k][:, tt*128..]^T w[k] (+ ones^T bias_row).
    Uses pool['ps'] (128,512) psums; N <= 512."""
    nk = (K + 127) // 128
    ntt = n_free // 128
    ps = pool["ps"]
    emarks = list(pool["prev"])
    for tt in range(ntt):
        t0 = tt * 128
        pp = ps[tt % 2]
        if emarks[tt % 2] is not None:
            p.wait("t", emarks[tt % 2][0], emarks[tt % 2][1])
        for k in range(nk):
            kw = min(128, K - k * 128)
            e_mm(p, pp[0:128, 0:N], x_tiles[k][0:kw, t0:t0 + 128], w_tiles[k][:, 0:N],
                 start=(k == 0), stop=(k == nk - 1 and bias_row is None))
        if bias_row is not None:
            e_mm(p, pp[0:128, 0:N], ones_row[0:1, t0:t0 + 128], bias_row[0:1, 0:N],
                 start=False, stop=True)
        tmark = p.marker("t")
        emarks[tt % 2] = evac(tt, pp[0:128, 0:N], tmark)
    pool["prev"] = emarks
    return emarks


def norm_pool(p, scope=None, slim=False):
    d = dict(
        sq=p.sb("nsq", [128, 512], scope=scope),
        stat=p.sb("nstat", [1, 512], scope=scope),
        mu=p.sb("nmu", [1, 512], scope=scope),
        onesc=p.sb("nonesc", [128, 1], scope=scope),
        onescb=p.sb("nonescb", [128, 1], BF16, scope=scope),
        onesr=p.sb("nonesr", [1, 128], scope=scope),
        ps_q=p.pt("npq", [1, 512], scope=scope),
        ps_rs=p.pt("npr", [128, 512], scope=scope),
        init=False,
    )
    if slim:
        d["ps_s"] = None
        d["ps_mu"] = None
    else:
        d["ps_s"] = p.pt("nps", [1, 512], scope=scope)
        d["ps_mu"] = p.pt("npm", [128, 512], scope=scope)
    return d


def norm_feat(p, np_, src, apply_cb, eps, nd=8, mean_center=True, n_free=None,
              n_feat=None, xch=None):
    n_free = n_free or T
    """src: ('dram', ap) (streams chunks into xch pair) or ('sb', tiles).
    apply_cb(d, tc, t0, tw, xa_fp32_ap, mu_bc_or_None, rstd_bc)."""
    Nf = n_feat or nd * 128
    ntc = (n_free + 511) // 512
    sq, stat, mu = np_["sq"], np_["stat"], np_["mu"]
    onesc, onesr = np_["onesc"], np_["onesr"]
    ps_s, ps_q, ps_mu, ps_rs = np_["ps_s"], np_["ps_q"], np_["ps_mu"], np_["ps_rs"]
    if not np_["init"]:
        e_memset(p, onesc[:], 1.0)
        e_memset(p, np_["onescb"][:], 1.0)
        e_memset(p, onesr[:], 1.0)
        vready = p.marker("v")
        p.wait("t", vready[0], vready[1])
        np_["init"] = True
    streaming = src[0] == "dram"
    if streaming:
        x_dram = src[1]
        lsems = [p.sem("nldxa"), p.sem("nldxb")]
        ltargets = {}

        def load_chunk(tc):
            t0, tw = tc * 512, min(512, n_free - tc * 512)
            for d in range(nd):
                _, lt = e_dma(p, "y", xch[tc % 2][:, d * 512:d * 512 + tw],
                              x_dram[d * 128:(d + 1) * 128, t0:t0 + tw],
                              lsems[tc % 2])
            ltargets[tc] = lt

        load_chunk(0)
        if ntc > 1:
            load_chunk(1)
    else:
        tiles = src[1]

    for tc in range(ntc):
        t0, tw = tc * 512, min(512, n_free - tc * 512)
        if streaming:
            xa = lambda d: xch[tc % 2][:, d * 512:d * 512 + tw]
            p.wait("s", lsems[tc % 2], ltargets[tc])
            p.wait("t", lsems[tc % 2], ltargets[tc])
        else:
            xa = lambda d: tiles[d][:, t0:t0 + tw]
        for d in range(nd):
            e_act(p, sq[:, 0:tw], xa(d), AF.Square)
            sm = p.marker("s")
            p.wait("t", sm[0], sm[1])
            if mean_center:
                onesx = (np_["onescb"] if xa(d).dtype == BF16 else onesc)
                e_mm(p, ps_s[:, 0:tw], onesx[:, :], xa(d),
                     start=(d == 0), stop=(d == nd - 1))
            e_mm(p, ps_q[:, 0:tw], onesc[:, :], sq[:, 0:tw],
                 start=(d == 0), stop=(d == nd - 1))
            tm = p.marker("t")
            p.wait("s", tm[0], tm[1])
        p.wait("v", tm[0], tm[1])
        if mean_center:
            e_ts(p, mu[:, 0:tw], ps_s[:, 0:tw], 1.0 / Nf)
            e_ts(p, stat[:, 0:tw], ps_q[:, 0:tw], 1.0 / Nf)
            e_tt(p, sq[0:1, 0:tw], mu[:, 0:tw], mu[:, 0:tw], ALU.mult)
            e_tt(p, stat[:, 0:tw], stat[:, 0:tw], sq[0:1, 0:tw], ALU.subtract)
        else:
            e_ts(p, stat[:, 0:tw], ps_q[:, 0:tw], 1.0 / Nf)
        vm = p.marker("v")
        p.wait("s", vm[0], vm[1])
        e_act(p, stat[:, 0:tw], stat[:, 0:tw], AF.Ln, bias=float(eps))
        e_act(p, stat[:, 0:tw], stat[:, 0:tw], AF.Exp, scale=-0.5)
        sm2 = p.marker("s")
        p.wait("t", sm2[0], sm2[1])
        if mean_center:
            e_mm(p, ps_mu[:, 0:tw], onesr[:, :], mu[:, 0:tw])
        e_mm(p, ps_rs[:, 0:tw], onesr[:, :], stat[:, 0:tw])
        tb = p.marker("t")
        p.wait("v", tb[0], tb[1])
        for d in range(nd):
            apply_cb(d, tc, t0, tw, xa(d),
                     ps_mu[:, 0:tw] if mean_center else None, ps_rs[:, 0:tw])
        va = p.marker("v")
        p.wait("t", va[0], va[1])
        p.wait("s", va[0], va[1])
        if streaming and tc + 2 < ntc:
            p.wait("y", va[0], va[1])
            load_chunk(tc + 2)
    p.barrier("ytvsg")


def load_bias_cols(p, dram_1d, n, name="bc", scope=None):
    t = p.sb(name, [128, n], scope=scope)
    sem = p.sem("blc")
    for j in range(n):
        e_dma(p, "y", t[:, j:j + 1],
              dram_1d[j * 128:(j + 1) * 128].unsqueeze(1), sem)
    return t, sem, sem.n


def final_wait_all(p):
    for name, s in list(p.sems.items()):
        if s.n > 0:
            p.wait("y", s, s.n)




def _dram(nc, io, name, shape, dt, kind):
    if io is None:
        if kind == "Internal":
            return nc.dram_tensor(name, shape, dt).ap()
        return nc.dram_tensor(name, shape, dt, kind=kind).ap()
    tag = io["__tag__"]
    if kind == "ExternalInput":
        if name in io:
            return io[name]
        ap = nc.dram_tensor(tag + name, shape, dt, kind="ExternalInput").ap()
        io.setdefault("__ins__", []).append((tag + name, name))
        return ap
    if kind == "ExternalOutput":
        if name in io:
            return io[name]
        ap = nc.dram_tensor(tag + name, shape, dt).ap()
        io[name] = ap
        return ap
    return nc.dram_tensor(tag + name, shape, dt).ap()


def attn_head(p, sc_chunks, get_vchunks, o_cb, state, n_free=None, tgsz=1024,
              s_tiles=None, causal_sbs=None, mask_cb=None, exp_scale=1.0):
    n_free = n_free or T
    s_tiles = s_tiles or (T // 128)
    """One head's attention, scores^T orientation, no max-subtraction
    (inputs are normalised and projection weights are ~N(0, 0.02^2), so
    |scores| stays far below the fp32 exp overflow threshold of 88).

    sc_chunks: list of (k_fn(s0)->lhsT(kc,128), q_fn(g0,gw)->rhs(kc,gw)).
    get_vchunks(sb)->list of (lhsT_ap, psum_idx, rows); ov_ps[psum_idx][sub]
    accumulates (rows, 512).
    o_cb(tg, g0, gw): normalise/write, psums reused after.
    causal_sbs: fn(tg)->list of s-tile indices; mask_cb(sb_, tg, pbuf_ap):
    records masking of the exp'd tile (on vector), returns marker or None.
    """
    nsub = tgsz // 512
    sc_ps, pbuf, ov_ps = state["sc_ps"], state["pbuf"], state["ov_ps"]
    ntg = n_free // tgsz
    for tg in range(ntg):
        g0 = tg * tgsz
        exp_marks = [None, None]
        pv_marks = [None, None]
        sbs = list(range(s_tiles)) if causal_sbs is None else causal_sbs(tg)
        for idx, sb_ in enumerate(sbs):
            s0 = sb_ * 128
            pp = sc_ps[idx % 2]
            if exp_marks[idx % 2] is not None:
                p.wait("t", exp_marks[idx % 2][0], exp_marks[idx % 2][1])
            for sub in range(nsub):
                for ic, (kfn, qfn) in enumerate(sc_chunks):
                    e_mm(p, pp[:, sub * 512:(sub + 1) * 512], kfn(s0),
                         qfn(g0 + sub * 512, 512), start=(ic == 0),
                         stop=(ic == len(sc_chunks) - 1))
            tm = p.marker("t")
            p.wait("s", tm[0], tm[1])
            if pv_marks[idx % 2] is not None:
                p.wait("s", pv_marks[idx % 2][0], pv_marks[idx % 2][1])
            e_act(p, pbuf[idx % 2][:, 0:tgsz], pp[:, 0:tgsz], AF.Exp,
                  scale=exp_scale)
            em = p.marker("s")
            if mask_cb is not None:
                mm_ = mask_cb(sb_, tg, pbuf[idx % 2], em)
                if mm_ is not None:
                    em = mm_
            exp_marks[idx % 2] = em
            p.wait("t", em[0], em[1])
            for sub in range(nsub):
                for (vap, pidx, rows) in get_vchunks(sb_):
                    e_mm(p, ov_ps[pidx][sub][0:rows, :], vap,
                         pbuf[idx % 2][:, sub * 512:(sub + 1) * 512],
                         start=(idx == 0), stop=(idx == len(sbs) - 1))
            pvm = p.marker("t")
            pv_marks[idx % 2] = pvm
        for m_ in pv_marks:
            if m_ is not None:
                p.wait("s", m_[0], m_[1])
                p.wait("v", m_[0], m_[1])
        o_cb(tg, g0, tgsz)


def build_L3a(ctx=None):
    own = ctx is None
    if own:
        nc = bass.Bass("TRN2", target_bir_lowering=False)
        p = P(nc)
        io = None
    else:
        nc, p, io = ctx
    xd = _dram(nc, io, "x", [D, T], F32, "ExternalInput")
    wqkv = _dram(nc, io, "w_qkv", [D, 3 * D], F32, "ExternalInput")
    bqkv = _dram(nc, io, "b_qkv", [3 * D], F32, "ExternalInput")
    wproj = _dram(nc, io, "w_proj", [D, D], F32, "ExternalInput")
    bproj = _dram(nc, io, "b_proj", [D], F32, "ExternalInput")
    xn_out = _dram(nc, io, "xn", [D, T], F32, "ExternalOutput")
    s_out = _dram(nc, io, "s", [D, T], F32, "ExternalOutput")
    o_dram = _dram(nc, io, "o_scr", [D, T], BF16, "Internal")

    xnb = [p.sb("xnb", [128, T], BF16) for _ in range(8)]

    sc1 = []
    np1 = norm_pool(p, scope=sc1)
    xch = [p.sb("xch", [128, 512 * 8], scope=sc1) for _ in range(2)]
    xnf_ch = [p.sb("xnf", [128, 512 * 8], scope=sc1) for _ in range(2)]
    stx = [p.sem("stxa"), p.sem("stxb")]
    st_marks = [None, None]

    def ln_apply(d, tc, t0, tw, xa, mu_bc, rstd_bc):
        ch = xnf_ch[tc % 2]
        if d == 0 and st_marks[tc % 2] is not None:
            p.wait("v", stx[tc % 2], st_marks[tc % 2])
        e_tt(p, ch[:, d * 512:d * 512 + tw], xa, mu_bc, ALU.subtract)
        e_tt(p, ch[:, d * 512:d * 512 + tw], ch[:, d * 512:d * 512 + tw],
             rstd_bc, ALU.mult)
        e_copy(p, xnb[d][:, t0:t0 + tw], ch[:, d * 512:d * 512 + tw])
        if d == 7:
            vm = p.marker("v")
            p.wait("y", vm[0], vm[1])
            for dd in range(8):
                e_dma(p, "y", xn_out[dd * 128:(dd + 1) * 128, t0:t0 + tw],
                      ch[:, dd * 512:dd * 512 + tw], stx[tc % 2])
            st_marks[tc % 2] = stx[tc % 2].n

    norm_feat(p, np1, ("dram", xd), ln_apply, 1e-5, xch=xch)
    p.close_scope(sc1)

    sc2 = []
    bq_sb, _, _ = load_bias_cols(p, bqkv, 24, "bqk", scope=sc2)
    ones_t = p.sb("onest", [1, T], BF16, scope=sc2)
    vbias = p.sb("vbias", [1, 256], BF16, scope=sc2)
    onesq = p.sb("onesq", [1, 64], scope=sc2)
    e_memset(p, ones_t[:], 1.0)
    e_memset(p, onesq[:], 1.0)
    qt = [p.sb("qt", [128, T], BF16, scope=sc2) for _ in range(2)]
    kt = [p.sb("kt", [128, T], BF16, scope=sc2) for _ in range(2)]
    ot = [p.sb("ot", [128, T], BF16, scope=sc2) for _ in range(2)]
    vt = p.sb("vt", [128, 32 * 260], BF16, scope=sc2)
    wv_sb = [p.sb("wv", [128, 256], BF16, scope=sc2) for _ in range(8)]
    o_scr = p.sb("oscr", [65, 1024], scope=sc2)
    rec = p.sb("rec", [1, 1024], scope=sc2)
    lp = lin_pool(p, 8, scope=sc2)
    state = dict(
        sc_ps=[p.pt("scps", [128, 1024], scope=sc2) for _ in range(2)],
        pbuf=[p.sb("pb", [128, 1024], BF16, scope=sc2) for _ in range(2)],
        ov_ps=[[p.pt("ovps", [65, 512], scope=sc2) for _ in range(2)]],
    )
    p.barrier("ytvsg")

    for q in range(4):
        ev_q = evac_to_tiles(p, qt, AF.Identity,
                             bias=lambda mb, q=q: bq_sb[:, 2 * q + mb:2 * q + mb + 1])
        linear_fm(p, lp, wqkv[:, 256 * q:256 * q + 256], xnb, D, 256, ev_q)
        ev_k = evac_to_tiles(p, kt, AF.Identity,
                             bias=lambda mb, q=q: bq_sb[:, 8 + 2 * q + mb:8 + 2 * q + mb + 1])
        linear_fm(p, lp, wqkv[:, D + 256 * q:D + 256 * q + 256], xnb, D, 256,
                  ev_k)
        wsem = p.sem("wvs")
        for k in range(8):
            e_dma(p, "g", wv_sb[k][:],
                  wqkv[k * 128:(k + 1) * 128,
                       2 * D + 256 * q:2 * D + 256 * q + 256], wsem)
        e_dma(p, "g", vbias[:],
              bqkv[2 * D + 256 * q:2 * D + 256 * q + 256].unsqueeze(0), wsem)
        p.wait("t", wsem, wsem.n)
        for tt in range(T // 128):
            for h in range(4):
                e_memset(p, vt[:, tt * 260 + h * 65 + 64:tt * 260 + h * 65 + 65],
                         1.0)
        vm0 = p.marker("v")
        p.wait("s", vm0[0], vm0[1])

        def ev_v(tt, psum, tmark):
            p.wait("s", tmark[0], tmark[1])
            for h in range(4):
                e_act(p, vt[:, tt * 260 + h * 65:tt * 260 + h * 65 + 64],
                      psum[:, h * 64:h * 64 + 64], AF.Identity)
            return p.marker("s")
        linear_tm(p, lp, wv_sb, xnb, D, 256, ev_v, bias_row=vbias,
                  ones_row=ones_t)
        p.barrier("ytvsg")

        for hl in range(4):
            koff = (hl % 2) * 64
            kk, qq = kt[hl // 2], qt[hl // 2]
            orow = (hl % 2) * 64

            def o_cb(tg, g0, gw, hl=hl, orow=orow):
                ov = state["ov_ps"][0]
                for sub in range(2):
                    e_act(p, o_scr[:, sub * 512:(sub + 1) * 512],
                          ov[sub][:, :], AF.Identity)
                sm = p.marker("s")
                p.wait("v", sm[0], sm[1])
                p.op("v", lambda eng: eng.reciprocal(out=rec[:, 0:1024],
                                                     in_=o_scr[64:65, 0:1024]))
                vm = p.marker("v")
                p.wait("t", vm[0], vm[1])
                for sub in range(2):
                    e_mm(p, ov[sub][0:64, :], onesq[:, 0:64],
                         rec[:, sub * 512:(sub + 1) * 512])
                tb = p.marker("t")
                p.wait("v", tb[0], tb[1])
                for sub in range(2):
                    e_tt(p, ot[hl // 2][orow:orow + 64,
                                        g0 + sub * 512:g0 + (sub + 1) * 512],
                         o_scr[0:64, sub * 512:(sub + 1) * 512],
                         ov[sub][0:64, :], ALU.mult)
                vo = p.marker("v")
                p.wait("t", vo[0], vo[1])
                p.wait("s", vo[0], vo[1])

            attn_head(
                p,
                sc_chunks=[(lambda s0, kk=kk, koff=koff:
                            kk[koff:koff + 64, s0:s0 + 128],
                            lambda g0, gw, qq=qq, koff=koff:
                            qq[koff:koff + 64, g0:g0 + gw])],
                get_vchunks=lambda sb_, hl=hl: [
                    (vt[:, sb_ * 260 + hl * 65:sb_ * 260 + hl * 65 + 65], 0, 65)],
                o_cb=o_cb, state=state, exp_scale=0.125)

        vm = p.marker("v")
        p.wait("y", vm[0], vm[1])
        som = p.sem("so")
        for i in range(2):
            e_dma(p, "y", o_dram[256 * q + i * 128:256 * q + (i + 1) * 128, :],
                  ot[i][:], som)
        p.wait("t", som, som.n)
        p.wait("v", som, som.n)
        p.wait("s", som, som.n)

    p.barrier("ytvsg")
    p.close_scope(sc2)

    sc3 = []
    bp_sb, bps, bpt = load_bias_cols(p, bproj, 8, "bpj", scope=sc3)
    p.wait("s", bps, bpt)
    obt = [p.sb("obt", [128, T], BF16, scope=sc3) for _ in range(8)]
    xrs = [p.sb("xrs", [128, T], scope=sc3) for _ in range(2)]
    sch2 = [p.sb("sch", [128, T], scope=sc3) for _ in range(2)]
    lp3 = lin_pool(p, 8, scope=sc3)
    olsem = p.sem("obl")
    for k in range(8):
        e_dma(p, "y", obt[k][:], o_dram[k * 128:(k + 1) * 128, :], olsem)
    p.wait("t", olsem, olsem.n)
    xlsem2 = [p.sem("xla"), p.sem("xlb")]
    ostm2 = [p.sem("osta"), p.sem("ostb")]
    xl_t = {}
    ev_done = {}
    st_done = {}

    def ev_proj(mb, tc, t0, tw, psum, tmark):
        if tc == 0:
            if mb == 0:
                for m2 in (0, 1):
                    _, xl_t[m2] = e_dma(p, "y", xrs[m2 % 2][:],
                                        xd[m2 * 128:(m2 + 1) * 128, :],
                                        xlsem2[m2 % 2])
            elif mb + 1 < 8:
                p.wait("y", ev_done[mb - 1][0], ev_done[mb - 1][1])
                _, xl_t[mb + 1] = e_dma(p, "y", xrs[(mb + 1) % 2][:],
                                        xd[(mb + 1) * 128:(mb + 2) * 128, :],
                                        xlsem2[(mb + 1) % 2])
        sch = sch2[mb % 2]
        p.wait("s", tmark[0], tmark[1])
        if tc == 0:
            p.wait("s", xlsem2[mb % 2], xl_t[mb])
            if mb >= 2:
                p.wait("s", ostm2[mb % 2], st_done[mb - 2])
        e_act(p, sch[:, t0:t0 + tw], psum, AF.Identity,
              bias=bp_sb[:, mb:mb + 1])
        sm = p.marker("s")
        p.wait("v", sm[0], sm[1])
        e_tt(p, sch[:, t0:t0 + tw], sch[:, t0:t0 + tw],
             xrs[mb % 2][:, t0:t0 + tw], ALU.add)
        vm = p.marker("v")
        ev_done[mb] = vm
        p.wait("y", vm[0], vm[1])
        e_dma(p, "y", s_out[mb * 128:(mb + 1) * 128, t0:t0 + tw],
              sch[:, t0:t0 + tw], ostm2[mb % 2])
        st_done[mb] = ostm2[mb % 2].n
        return vm

    linear_fm(p, lp3, wproj, obt, D, D, ev_proj)
    p.close_scope(sc3)
    if own:
        final_wait_all(p)
        return p.finalize()


def build_L3f(ctx=None):
    own = ctx is None
    if own:
        nc = bass.Bass("TRN2", target_bir_lowering=False)
        p = P(nc)
        io = None
    else:
        nc, p, io = ctx
    xn = _dram(nc, io, "xn", [D, T], F32, "ExternalInput")
    sIn = _dram(nc, io, "sin", [D, T], F32, "ExternalInput")
    qT = _dram(nc, io, "qT", [6, D, D], I32, "ExternalInput")
    scT = _dram(nc, io, "scT", [6, D, D], F32, "ExternalInput")
    biases = _dram(nc, io, "biases", [6, D], F32, "ExternalInput")
    laT = _dram(nc, io, "laT", [6, D, 32], F32, "ExternalInput")
    lbT = _dram(nc, io, "lbT", [6, 32, D], F32, "ExternalInput")
    lng = _dram(nc, io, "lng", [3, D], F32, "ExternalInput")
    lnb = _dram(nc, io, "lnb", [3, D], F32, "ExternalInput")
    x_out = _dram(nc, io, "x4", [D, T], F32, "ExternalOutput")
    h_dram = _dram(nc, io, "hscr", [D, T], F32, "Internal")

    hcp = p.sem("hcp")
    for dd in range(8):
        e_dma(p, "y", h_dram[dd * 128:(dd + 1) * 128, :],
              xn[dd * 128:(dd + 1) * 128, :], hcp)

    A_b16 = [p.sb("Ab", [128, T], BF16) for _ in range(8)]
    Y_b16 = [p.sb("Yb", [128, T], BF16) for _ in range(8)]
    lam_b = p.sb("lamb", [32, T], BF16)
    lam_f = p.sb("lamf", [32, 512])
    lbT_sb = p.sb("lbTs", [32, D], BF16)
    qstage = [p.sb("qst", [128, 1024], I32) for _ in range(2)]
    sstage = [p.sb("sst", [128, 1024]) for _ in range(2)]
    gvec = p.sb("gv", [128, 8])
    bvec = p.sb("bv", [128, 8])
    bias_sb = p.sb("bias", [128, 8])
    tchunk = p.sb("tch", [128, 512])
    np_ = norm_pool(p)
    lp = lin_pool(p, 8)
    dqsem = [p.sem("dqa"), p.sem("dqb")]

    def dequant_wfn2(j):
        def w_fn(mb, wsem):
            m0 = mb * 128
            sl = qstage[mb % 2]
            ss = sstage[mb % 2]
            dsem = dqsem[mb % 2]
            _, t1 = e_dma(p, "g", sl[:].rearrange("p (kc m) -> p kc m", kc=8),
                          qT[j, :, m0:m0 + 128].rearrange(
                              "(kc p) m -> p kc m", p=128), dsem)
            _, t2 = e_dma(p, "g", ss[:].rearrange("p (kc m) -> p kc m", kc=8),
                          scT[j, :, m0:m0 + 128].rearrange(
                              "(kc p) m -> p kc m", p=128), dsem)
            p.wait("v", dsem, t2)
            e_ts(p, sl[:].bitcast(F32), sl[:], 1.0 / 7.5, -1.0, op0=ALU.mult,
                 op1=ALU.add)
            e_tt(p, lp["wbuf"][mb % 2][:, 0:1024], sl[:].bitcast(F32), ss[:],
                 ALU.mult)
            vm = p.marker("v")
            wsem.n += 0
            p.wait("t", vm[0], vm[1])
            return 0
        return w_fn

    lnsem = p.sem("lnld")
    accsem2 = [p.sem("acca"), p.sem("accb")]
    ybounce = [p.sb("ybnc", [128, 512]) for _ in range(2)]

    for blk in range(3):
        j0, j1 = 2 * blk, 2 * blk + 1
        p.barrier("ytvsg")
        ald = p.sem("ald")
        for dd in range(8):
            e_dma(p, "g", A_b16[dd][:], h_dram[dd * 128:(dd + 1) * 128, :],
                  ald)
        a_t = ald.n
        for c in range(8):
            e_dma(p, "y", gvec[:, c:c + 1],
                  lng[blk, c * 128:(c + 1) * 128].unsqueeze(1), lnsem)
            _, g2 = e_dma(p, "y", bvec[:, c:c + 1],
                          lnb[blk, c * 128:(c + 1) * 128].unsqueeze(1), lnsem)
        p.wait("t", ald, a_t)
        p.wait("v", lnsem, g2)

        bsem = p.sem("bsx")
        for c in range(8):
            _, bt = e_dma(p, "y", bias_sb[:, c:c + 1],
                          biases[j0, c * 128:(c + 1) * 128].unsqueeze(1), bsem)
        lamsem = p.sem("lamx")
        la_sb = p.sb("lasb%d" % blk, [128, 8 * 32], BF16)
        _, lt = e_dma(p, "g", la_sb[:].rearrange("p (kc m) -> p kc m", kc=8),
                      laT[j0].rearrange("(kc p) m -> p kc m", p=128), lamsem)
        p.wait("t", lamsem, lt)

        def ev_lam(mb, tc, t0, tw, psum, tmark):
            p.wait("s", tmark[0], tmark[1])
            e_act(p, lam_b[0:32, t0:t0 + tw], psum, AF.Identity)
            return p.marker("s")
        lam_marks = []
        for tc in range(T // 512):
            t0 = tc * 512
            pp = lp["ps"][tc % 2]
            if lp["prev"][tc % 2] is not None:
                p.wait("t", lp["prev"][tc % 2][0], lp["prev"][tc % 2][1])
            for k in range(8):
                e_mm(p, pp[0:32, :], la_sb[:, k * 32:(k + 1) * 32],
                     A_b16[k][:, t0:t0 + 512], start=(k == 0), stop=(k == 7))
            tm = p.marker("t")
            lp["prev"][tc % 2] = ev_lam(None, tc, t0, 512, pp[0:32, :], tm)
        lbsem = p.sem("lbx")
        _, lbt = e_dma(p, "g", lbT_sb[:], lbT[j0], lbsem)
        p.wait("t", lbsem, lbt)
        p.wait("t", bsem, bt)

        def extra_lora(mb, tc, psum, t0, tw, mw):
            e_mm(p, psum, lbT_sb[0:32, mb * 128:mb * 128 + mw],
                 lam_b[0:32, t0:t0 + tw], start=False, stop=True)

        ev_y1 = evac_to_tiles(p, Y_b16, AF.Identity,
                              bias=lambda mb: bias_sb[:, mb:mb + 1])
        linear_fm(p, lp, None, A_b16, D, D, ev_y1, extra_k=extra_lora,
                  w_fn=dequant_wfn2(j0))
        p.barrier("ytvsg")

        def ln_apply(d, tc, t0, tw, xa, mu_bc, rstd_bc):
            e_tt(p, tchunk[:, 0:tw], xa, mu_bc, ALU.subtract)
            e_tt(p, tchunk[:, 0:tw], tchunk[:, 0:tw], rstd_bc, ALU.mult)
            e_ts(p, tchunk[:, 0:tw], tchunk[:, 0:tw], gvec[:, d:d + 1],
                 bvec[:, d:d + 1], op0=ALU.mult, op1=ALU.add)
            vm = p.marker("v")
            p.wait("s", vm[0], vm[1])
            e_act(p, A_b16[d][:, t0:t0 + tw], tchunk[:, 0:tw], AF.Gelu)
            sm = p.marker("s")
            p.wait("v", sm[0], sm[1])

        norm_feat(p, np_, ("sb", Y_b16), ln_apply, 1e-5)

        for c in range(8):
            _, bt2 = e_dma(p, "y", bias_sb[:, c:c + 1],
                          biases[j1, c * 128:(c + 1) * 128].unsqueeze(1), bsem)
        _, lt2 = e_dma(p, "g", la_sb[:].rearrange("p (kc m) -> p kc m", kc=8),
                       laT[j1].rearrange("(kc p) m -> p kc m", p=128), lamsem)
        p.wait("t", lamsem, lt2)
        for tc in range(T // 512):
            t0 = tc * 512
            pp = lp["ps"][tc % 2]
            if lp["prev"][tc % 2] is not None:
                p.wait("t", lp["prev"][tc % 2][0], lp["prev"][tc % 2][1])
            for k in range(8):
                e_mm(p, pp[0:32, :], la_sb[:, k * 32:(k + 1) * 32],
                     A_b16[k][:, t0:t0 + 512], start=(k == 0), stop=(k == 7))
            tm = p.marker("t")
            lp["prev"][tc % 2] = ev_lam(None, tc, t0, 512, pp[0:32, :], tm)
        _, lbt2 = e_dma(p, "g", lbT_sb[:], lbT[j1], lbsem)
        p.wait("t", lbsem, lbt2)
        p.wait("t", bsem, bt2)

        st_marks = [None, None]

        def ev_y2(mb, tc, t0, tw, psum, tmark):
            u = (mb * (T // 512) + tc) % 2
            p.wait("s", tmark[0], tmark[1])
            if st_marks[u] is not None:
                p.wait("s", accsem2[u], st_marks[u])
            e_act(p, ybounce[u][:, 0:tw], psum, AF.Identity,
                  bias=bias_sb[:, mb:mb + 1])
            sm = p.marker("s")
            p.wait("g", sm[0], sm[1])
            p.op("g", lambda eng, mb=mb, t0=t0, tw=tw, u=u: eng.dma_start(
                out=h_dram[mb * 128:(mb + 1) * 128, t0:t0 + tw],
                in_=ybounce[u][:, 0:tw], accum_op=ALU.add), accsem2[u], 16,
                kind="d")
            st_marks[u] = accsem2[u].n
            return sm
        p.dma_sems.add(accsem2[0])
        p.dma_sems.add(accsem2[1])
        linear_fm(p, lp, None, A_b16, D, D, ev_y2, extra_k=extra_lora,
                  w_fn=dequant_wfn2(j1))
        p.barrier("ytvsg")

    fsem = p.sem("fin")
    fst2 = [p.sem("osta"), p.sem("ostb")]
    sch = [p.sb("fch", [128, 1024]) for _ in range(2)]
    hch = [p.sb("fhh", [128, 1024]) for _ in range(2)]
    fl = {}
    fs_t = {}
    nhalf = T // 1024
    for dd in range(8):
        for half in range(nhalf):
            u = (dd * nhalf + half) % 2
            c0 = half * 1024
            key = dd * nhalf + half
            if key >= 2:
                pk = key - 2
                p.wait("y", fl[pk][0], fl[pk][1])
                p.wait("y", fst2[u], fs_t[pk])
            _, t1 = e_dma(p, "y", sch[u][:], sIn[dd * 128:(dd + 1) * 128,
                                                 c0:c0 + 1024], fsem)
            _, t2 = e_dma(p, "y", hch[u][:], h_dram[dd * 128:(dd + 1) * 128,
                                                    c0:c0 + 1024], fsem)
            p.wait("v", fsem, t2)
            e_tt(p, sch[u][:], sch[u][:], hch[u][:], ALU.add)
            vm = p.marker("v")
            p.wait("y", vm[0], vm[1])
            _, ft = e_dma(p, "y", x_out[dd * 128:(dd + 1) * 128, c0:c0 + 1024],
                          sch[u][:], fst2[u])
            fl[key] = vm
            fs_t[key] = ft
    if own:
        final_wait_all(p)
        return p.finalize()


def build_L0f(ctx=None):
    QB = 128.0
    own = ctx is None
    if own:
        nc = bass.Bass("TRN2", target_bir_lowering=False)
        p = P(nc)
        io = None
    else:
        nc, p, io = ctx
    xn = _dram(nc, io, "xn", [D, T], F32, "ExternalInput")
    sIn = _dram(nc, io, "sin", [D, T], F32, "ExternalInput")
    wd = _dram(nc, io, "weight", [D, D], F32, "ExternalInput")
    lng = _dram(nc, io, "lng", [D], F32, "ExternalInput")
    lnb = _dram(nc, io, "lnb", [D], F32, "ExternalInput")
    x_out = _dram(nc, io, "x1", [D, T], F32, "ExternalOutput")
    scal_d = _dram(nc, io, "scal", [1, 4], F32, "Internal")

    gvec, _, _ = load_bias_cols(p, lng, 8, "gv")
    bvec, bs, bt = load_bias_cols(p, lnb, 8, "bv")
    p.wait("v", bs, bt)

    xn2d = _dram(nc, io, "xn2scr", [D, T], F32, "Internal")
    xq = [p.sb("xq", [128, T], BF16) for _ in range(8)]
    mx = p.sb("mx", [128, 8])
    mxc = p.sb("mxc", [128, 1])
    mx1 = p.sb("mx1", [1, 1])
    svec = p.sb("svec", [128, 4])
    wsums = p.sb("wsums", [1, 1024])
    wtot = p.sb("wtot", [1, 1])
    wstage = [p.sb("wst", [128, 1024]) for _ in range(2)]
    np_ = norm_pool(p)
    sc0 = []
    xch = [p.sb("xchf", [128, 512 * 8], scope=sc0) for _ in range(2)]
    e_memset(p, mx[:], 0.0)

    x2ch = [p.sb("x2ch", [128, 512 * 8], scope=sc0) for _ in range(2)]
    stx2 = [p.sem("sxa"), p.sem("sxb")]
    st_m2 = [None, None]

    def ln_apply(d, tc, t0, tw, xa, mu_bc, rstd_bc):
        ch = x2ch[tc % 2]
        if d == 0 and st_m2[tc % 2] is not None:
            p.wait("v", stx2[tc % 2], st_m2[tc % 2])
        dst = ch[:, d * 512:d * 512 + tw]
        e_tt(p, dst, xa, mu_bc, ALU.subtract)
        e_tt(p, dst, dst, rstd_bc, ALU.mult)
        e_ts(p, dst, dst, gvec[:, d:d + 1], bvec[:, d:d + 1], op0=ALU.mult,
             op1=ALU.add)
        p.op("v", lambda eng, dst=dst: eng.tensor_reduce(
            out=mxc[:], in_=dst, axis=AX.X, op=ALU.max,
            apply_absolute_value=True))
        e_tt(p, mx[:, d:d + 1], mx[:, d:d + 1], mxc[:], ALU.max)
        if d == 7:
            vm = p.marker("v")
            p.wait("y", vm[0], vm[1])
            for dd in range(8):
                e_dma(p, "y", xn2d[dd * 128:(dd + 1) * 128, t0:t0 + tw],
                      ch[:, dd * 512:dd * 512 + tw], stx2[tc % 2])
            st_m2[tc % 2] = stx2[tc % 2].n

    norm_feat(p, np_, ("dram", xn), ln_apply, 1e-3, xch=xch)

    p.op("v", lambda eng: eng.tensor_reduce(out=mxc[:], in_=mx[:], axis=AX.X,
                                            op=ALU.max))
    vm = p.marker("v")
    p.wait("g", vm[0], vm[1])
    p.op("g", lambda eng: eng.tensor_reduce(out=mx1[:], in_=mxc[:], axis=AX.C,
                                            op=ALU.max))
    gm = p.marker("g")

    wls = [p.sem("wla"), p.sem("wlb")]
    wlt = {}
    for mb in range(2):
        _, wlt[mb] = e_dma(p, "y", wstage[mb % 2][:].rearrange(
            "p (kc m) -> p kc m", kc=8),
            wd[:, mb * 128:(mb + 1) * 128].rearrange("(kc p) m -> p kc m",
                                                     p=128), wls[mb % 2])
    for mb in range(8):
        p.wait("s", wls[mb % 2], wlt[mb])
        e_act(p, wstage[mb % 2][:], wstage[mb % 2][:], AF.Abs)
        sm = p.marker("s")
        p.wait("t", sm[0], sm[1])
        for half in range(2):
            e_mm(p, np_["ps_s"][:, 0:512], np_["onesc"][:, :],
                 wstage[mb % 2][:, half * 512:(half + 1) * 512],
                 start=(mb == 0 and half == 0), stop=(mb == 7 and half == 1))
        tm = p.marker("t")
        if mb + 2 < 8:
            p.wait("y", tm[0], tm[1])
            _, wlt[mb + 2] = e_dma(p, "y", wstage[mb % 2][:].rearrange(
                "p (kc m) -> p kc m", kc=8),
                wd[:, (mb + 2) * 128:(mb + 3) * 128].rearrange(
                    "(kc p) m -> p kc m", p=128), wls[mb % 2])
    p.wait("v", tm[0], tm[1])
    e_copy(p, wsums[:, 0:512], np_["ps_s"][:, 0:512])
    p.op("v", lambda eng: eng.tensor_reduce(out=wtot[:], in_=wsums[:, 0:512],
                                            axis=AX.X, op=ALU.add))
    e_ts(p, wtot[:], wtot[:], 1.0 / (1024.0 * 1024.0))
    p.wait("v", gm[0], gm[1])
    e_ts(p, svec[0:1, 3:4], wtot[:], 1.0)
    e_ts(p, svec[0:1, 0:1], wtot[:], 1.0, 1e-5, op0=ALU.mult, op1=ALU.add)
    p.op("v", lambda eng: eng.reciprocal(out=svec[0:1, 0:1],
                                         in_=svec[0:1, 0:1]))
    p.op("v", lambda eng: eng.reciprocal(out=svec[0:1, 1:2], in_=mx1[:]))
    e_ts(p, svec[0:1, 1:2], svec[0:1, 1:2], QB)
    e_tt(p, svec[0:1, 2:3], mx1[:], wtot[:], ALU.mult)
    e_ts(p, svec[0:1, 2:3], svec[0:1, 2:3], 1.0 / QB)
    vm2 = p.marker("v")
    p.wait("y", vm2[0], vm2[1])
    scs = p.sem("scst")
    e_dma(p, "y", scal_d[:, :], svec[0:1, 0:4], scs)
    p.wait("y", scs, scs.n)
    _, sct = e_dma(p, "y", svec[:, 0:4],
                   scal_d[0:1, :].partition_broadcast(128).squeeze(1), scs)
    p.wait("v", scs, sct)
    p.wait("s", scs, sct)

    xqsem = [p.sem("xqa2"), p.sem("xqb2")]
    xqt = {}
    for tc in range(T // 512):
        for d in range(8):
            _, qt_ = e_dma(p, "y", x2ch[tc % 2][:, d * 512:(d + 1) * 512],
                           xn2d[d * 128:(d + 1) * 128, tc * 512:(tc + 1) * 512],
                           xqsem[tc % 2])
        xqt[tc] = qt_
        if tc >= 1:
            p.wait("v", xqsem[(tc - 1) % 2], xqt[tc - 1])
            for d in range(8):
                src = x2ch[(tc - 1) % 2][:, d * 512:(d + 1) * 512]
                e_ts(p, src, src, svec[:, 1:2], None, op0=ALU.mult)
                e_ts(p, xq[d][:, (tc - 1) * 512:tc * 512], src,
                     -(QB - 1e-5), QB - 1e-5, op0=ALU.max, op1=ALU.min)
            vm = p.marker("v")
            p.wait("y", vm[0], vm[1])
    tc = T // 512 - 1
    p.wait("v", xqsem[tc % 2], xqt[tc])
    for d in range(8):
        src = x2ch[tc % 2][:, d * 512:(d + 1) * 512]
        e_ts(p, src, src, svec[:, 1:2], None, op0=ALU.mult)
        e_ts(p, xq[d][:, tc * 512:(tc + 1) * 512], src,
             -(QB - 1e-5), QB - 1e-5, op0=ALU.max, op1=ALU.min)
    p.barrier("ytvsg")

    wqs = [p.sem("wqa"), p.sem("wqb")]

    def wq_fn(mb, wsem):
        sl = wstage[mb % 2]
        dsem = wqs[mb % 2]
        _, t1 = e_dma(p, "g", sl[:].rearrange("p (kc m) -> p kc m", kc=8),
                      wd[:, mb * 128:(mb + 1) * 128].rearrange(
                          "(kc p) m -> p kc m", p=128), dsem)
        p.wait("v", dsem, t1)
        e_ts(p, sl[:], sl[:], svec[:, 0:1], None, op0=ALU.mult)
        e_ts(p, sl[:], sl[:], MAGIC, MAGIC, op0=ALU.add, op1=ALU.subtract)
        e_ts(p, lp["wbuf"][mb % 2][:, 0:1024], sl[:], -1.0, 1.0, op0=ALU.max,
             op1=ALU.min)
        vm = p.marker("v")
        p.wait("t", vm[0], vm[1])
        return 0

    p.close_scope(sc0)
    lp = lin_pool(p, 8, np_ps=4)
    xrs = [p.sb("xrs0", [128, T]) for _ in range(2)]
    sch2 = [p.sb("sch0", [128, T]) for _ in range(2)]
    xlsem2 = [p.sem("xla"), p.sem("xlb")]
    ostm2 = [p.sem("osta"), p.sem("ostb")]
    xl_t = {}
    ev_done = {}
    st_done = {}

    def ev_out(mb, tc, t0, tw, psum, tmark):
        if tc == 0:
            if mb == 0:
                for m2 in (0, 1):
                    _, xl_t[m2] = e_dma(p, "y", xrs[m2 % 2][:],
                                        sIn[m2 * 128:(m2 + 1) * 128, :],
                                        xlsem2[m2 % 2])
            elif mb + 1 < 8:
                p.wait("y", ev_done[mb - 1][0], ev_done[mb - 1][1])
                _, xl_t[mb + 1] = e_dma(p, "y", xrs[(mb + 1) % 2][:],
                                        sIn[(mb + 1) * 128:(mb + 2) * 128, :],
                                        xlsem2[(mb + 1) % 2])
        sch = sch2[mb % 2]
        p.wait("s", tmark[0], tmark[1])
        if tc == 0:
            p.wait("s", xlsem2[mb % 2], xl_t[mb])
            if mb >= 2:
                p.wait("s", ostm2[mb % 2], st_done[mb - 2])
        e_act(p, sch[:, t0:t0 + tw], psum, AF.Identity, scale=svec[:, 2:3])
        sm = p.marker("s")
        p.wait("v", sm[0], sm[1])
        e_tt(p, sch[:, t0:t0 + tw], sch[:, t0:t0 + tw],
             xrs[mb % 2][:, t0:t0 + tw], ALU.add)
        vm = p.marker("v")
        ev_done[mb] = vm
        p.wait("y", vm[0], vm[1])
        e_dma(p, "y", x_out[mb * 128:(mb + 1) * 128, t0:t0 + tw],
              sch[:, t0:t0 + tw], ostm2[mb % 2])
        st_done[mb] = ostm2[mb % 2].n
        return vm

    linear_fm(p, lp, None, xq, D, D, ev_out, w_fn=wq_fn)
    if own:
        final_wait_all(p)
        return p.finalize()


def router_gates(p, lp, xnb, wr_dram, br_dram, E, cT_dram, gate_mode,
                 idn_dram=None):
    wr_sb = [p.sb("wrs", [128, E], BF16) for _ in range(8)]
    brow = p.sb("brow", [1, E], BF16)
    ones_t = p.sb("onetr", [1, T], BF16)
    lg = p.sb("lg", [128, 8])
    m8 = p.sb("m8r", [128, 8])
    gsc = p.sb("gsc", [128, 8])
    cmb = p.sb("cmb", [128, E])
    cmbb = p.sb("cmbb", [128, E], BF16)
    csum = p.sb("csum", [128, 1])
    cps = p.pt("cps", [E, 128], BF16)
    idn = p.sb("idnr", [128, 128], BF16)
    idsem = p.sem("idl")
    e_dma(p, "g", idn[:], idn_dram, idsem)
    p.wait("t", idsem, idsem.n)
    ct_stage = p.sb("ctst", [E, 128], BF16)
    e_memset(p, ones_t[:], 1.0)
    e_memset(p, lg[:], -1e30)
    wsem = p.sem("wrl")
    for k in range(8):
        e_dma(p, "g", wr_sb[k][:], wr_dram[k * 128:(k + 1) * 128, :], wsem)
    if br_dram is not None:
        _, bt = e_dma(p, "g", brow[:], br_dram.unsqueeze(0), wsem)
    p.wait("t", wsem, wsem.n)
    vm0 = p.marker("v")
    p.wait("t", vm0[0], vm0[1])
    cstm = p.sem("cstm")
    for tt in range(T // 128):
        t0 = tt * 128
        pp = lp["ps"][tt % 2]
        if lp["prev"][tt % 2] is not None:
            p.wait("t", lp["prev"][tt % 2][0], lp["prev"][tt % 2][1])
        for k in range(8):
            e_mm(p, pp[0:128, 0:E], xnb[k][:, t0:t0 + 128], wr_sb[k][:, 0:E],
                 start=(k == 0), stop=False)
        e_mm(p, pp[0:128, 0:E], ones_t[0:1, t0:t0 + 128], brow[0:1, 0:E],
             start=False, stop=True)
        tm = p.marker("t")
        p.wait("v", tm[0], tm[1])
        p.op("v", lambda eng, pp=pp: eng.tensor_reduce(
            out=csum[:], in_=pp[0:128, 0:E], axis=AX.X, op=ALU.max,
            negate=True))
        vm = p.marker("v")
        p.wait("s", vm[0], vm[1])
        e_act(p, lg[:, 0:E], pp[0:128, 0:E], AF.Exp, bias=csum[:],
              accum=gsc[:, 0:1])
        sm = p.marker("s")
        p.wait("v", sm[0], sm[1])
        p.op("v", lambda eng: eng.reciprocal(out=csum[:], in_=gsc[:, 0:1]))
        e_ts(p, lg[:, 0:E], lg[:, 0:E], csum[:], None, op0=ALU.mult)
        p.op("v", lambda eng: eng.max(out=m8[:], in_=lg[:]))
        if gate_mode == "ratio":
            e_tt(p, gsc[:, 2:3], m8[:, 0:1], m8[:, 1:2], ALU.add)
            p.op("v", lambda eng: eng.reciprocal(out=gsc[:, 3:4],
                                                 in_=gsc[:, 2:3]))
            e_tt(p, gsc[:, 0:1], m8[:, 0:1], gsc[:, 3:4], ALU.mult)
            e_tt(p, gsc[:, 1:2], m8[:, 1:2], gsc[:, 3:4], ALU.mult)
        else:
            vm = p.marker("v")
            p.wait("s", vm[0], vm[1])
            e_act(p, gsc[:, 4:6], m8[:, 0:2], AF.Exp)
            sm = p.marker("s")
            p.wait("v", sm[0], sm[1])
            e_tt(p, gsc[:, 2:3], gsc[:, 4:5], gsc[:, 5:6], ALU.add)
            p.op("v", lambda eng: eng.reciprocal(out=gsc[:, 3:4],
                                                 in_=gsc[:, 2:3]))
            e_tt(p, gsc[:, 0:1], gsc[:, 4:5], gsc[:, 3:4], ALU.mult)
            e_tt(p, gsc[:, 1:2], gsc[:, 5:6], gsc[:, 3:4], ALU.mult)
        e_ts(p, cmb[:, 0:E], lg[:, 0:E], m8[:, 0:1], None, op0=ALU.is_equal)
        e_ts(p, lg[:, 0:E], lg[:, 0:E], m8[:, 1:2], None, op0=ALU.is_equal)
        e_ts(p, cmb[:, 0:E], cmb[:, 0:E], gsc[:, 0:1], None, op0=ALU.mult)
        e_ts(p, lg[:, 0:E], lg[:, 0:E], gsc[:, 1:2], None, op0=ALU.mult)
        e_tt(p, cmb[:, 0:E], cmb[:, 0:E], lg[:, 0:E], ALU.add)
        e_copy(p, cmbb[:, 0:E], cmb[:, 0:E])
        e_memset(p, lg[:], -1e30)
        vm2 = p.marker("v")
        p.wait("t", vm2[0], vm2[1])
        p.op("t", lambda eng, pp=cps, src=cmbb: eng.transpose(
            pp[0:E, 0:128], src[0:128, 0:E], idn[:]))
        ttm = p.marker("t")
        lp["prev"][tt % 2] = ttm
        p.wait("v", ttm[0], ttm[1])
        p.wait("v", cstm, cstm.n)
        e_copy(p, ct_stage[0:E, :], cps[0:E, 0:128])
        vm3 = p.marker("v")
        p.wait("y", vm3[0], vm3[1])
        e_dma(p, "y", cT_dram[0:E, t0:t0 + 128], ct_stage[0:E, :], cstm)
    p.barrier("ytvsg")


def build_L2f(ctx=None):
    own = ctx is None
    if own:
        nc = bass.Bass("TRN2", target_bir_lowering=False)
        p = P(nc)
        ctx = (nc, p, None)
    else:
        nc, p, _ = ctx
    handles = {}
    for _ in _L2f_stream(ctx, handles):
        pass
    p.barrier("ytvsg")
    _L2f_final(ctx, handles)
    if own:
        final_wait_all(p)
        return p.finalize()


def _L2f_stream(ctx, handles):
    E, F_ = 4, 4096
    nc, p, io = ctx
    xn = _dram(nc, io, "xn", [D, T], F32, "ExternalInput")
    sIn = _dram(nc, io, "sin", [D, T], F32, "ExternalInput")
    rw = _dram(nc, io, "router_w", [D, E], F32, "ExternalInput")
    rb = _dram(nc, io, "router_b", [E], F32, "ExternalInput")
    w1 = _dram(nc, io, "w1", [E, D, F_], F32, "ExternalInput")
    w3 = _dram(nc, io, "w3", [E, D, F_], F32, "ExternalInput")
    w2 = _dram(nc, io, "w2", [E, F_, D], F32, "ExternalInput")
    x_out = _dram(nc, io, "x3", [D, T], F32, "ExternalOutput")
    iden = _dram(nc, io, "iden", [128, 128], F32, "ExternalInput")
    cT = _dram(nc, io, "cTscr", [E, T], BF16, "Internal")
    acc_d = _dram(nc, io, "accscr", [D, T], F32, "Internal")
    xnb_d = _dram(nc, io, "xnbscr", [D, T], BF16, "Internal")
    handles["sin"] = sIn
    handles["x3"] = x_out
    handles["acc"] = acc_d

    scr = []
    p.scope_stack.append(scr)
    xnb = [p.sb("xnb2", [128, T], BF16, scope=scr) for _ in range(8)]
    rpool = {"ps": [p.pt("rps", [128, 512], scope=scr) for _ in range(2)],
             "prev": [None, None]}
    xls = p.sem("xl2")
    for dd in range(8):
        e_dma(p, "g", xnb[dd][:], xn[dd * 128:(dd + 1) * 128, :], xls)
    p.wait("t", xls, xls.n)
    p.wait("v", xls, xls.n)
    p.wait("y", xls, xls.n)
    s_st = p.sem("fin")
    for dd in range(8):
        _, st_t = e_dma(p, "y", xnb_d[dd * 128:(dd + 1) * 128, :],
                        xnb[dd][:], s_st)
    p.wait("y", s_st, st_t)
    router_gates(p, rpool, xnb, rw, rb, E, cT, "ratio", idn_dram=iden)
    p.scope_stack.pop()
    p.close_scope(scr)
    yield "router"

    NFB = 4
    NKG = F_ // (128 * NFB)
    ntc = T // 512
    lfsb = []
    wu1 = p.sb("wu1f", [128, 8 * 512], BF16, scope=lfsb)
    wu3 = [p.sb("wu3f", [128, 8 * 512], BF16, scope=lfsb) for _ in range(2)]
    wd2 = p.sb("wd2f", [128, NFB * 1024], BF16, scope=lfsb)
    xnc = [p.sb("xncf", [128, 8 * 512], BF16, scope=lfsb) for _ in range(2)]
    crp = [p.sb("crpf", [128, 512], BF16, scope=lfsb)]
    hch = [p.sb("hchf", [128, NFB * 512], BF16, scope=lfsb) for _ in range(2)]
    asb = [p.sb("asbf", [128, 512], BF16, scope=lfsb) for _ in range(2)]
    pqb = [p.sb("pqbf", [128, 512], BF16, scope=lfsb) for _ in range(2)]
    obp = [p.sb("obpf", [128, 2 * 512], scope=lfsb) for _ in range(2)]
    lfps = []
    upp = p.pt("uppf", [128, 512], scope=lfps)
    upq = p.pt("upqf", [128, 512], scope=lfps)
    dnp = [p.pt("dnpf", [128, 512], scope=lfps) for _ in range(2)]
    s_w13 = p.sem("w13a")
    s_w3 = [p.sem("xl2"), p.sem("wrl")]
    s_w2 = p.sem("w13b")
    s_xn = [p.sem("sxa"), p.sem("sxb")]
    s_cr = [p.sem("cra"), p.sem("crb")]
    s_ac = [p.sem("acca"), p.sem("accb")]

    units = [(e, kg, tc) for e in range(E) for kg in range(NKG)
             for tc in range(ntc)]
    NU = len(units)

    def load_w1(e, kg, gate):
        if gate is not None:
            p.wait("g", gate[0], gate[1])
        f0 = kg * NFB * 128
        _, t1 = e_dma(p, "g", wu1[:].rearrange("p (kc m) -> p kc m", kc=8),
                      w1[e, :, f0:f0 + NFB * 128].rearrange(
                          "(kc p) m -> p kc m", p=128), s_w13)
        return t1

    def load_w3(e, kg, kb, gate):
        if gate is not None:
            p.wait("g", gate[0], gate[1])
        f0 = kg * NFB * 128
        _, t2 = e_dma(p, "g", wu3[kb][:].rearrange("p (kc m) -> p kc m", kc=8),
                      w3[e, :, f0:f0 + NFB * 128].rearrange(
                          "(kc p) m -> p kc m", p=128), s_w3[kb])
        return t2

    def load_w2(e, kg, gate):
        if gate is not None:
            p.wait("g", gate[0], gate[1])
        f0 = kg * NFB * 128
        _, t2 = e_dma(p, "g", wd2[:].rearrange("p (kc m) -> p kc m", kc=NFB),
                      w2[e, f0:f0 + NFB * 128, :].rearrange(
                          "(kc p) m -> p kc m", p=128), s_w2)
        return t2

    def load_xn(ui):
        tc = units[ui][2]
        b = ui % 2
        if xn_use[b] is not None:
            p.wait("y", xn_use[b][0], xn_use[b][1])
        _, t_ = e_dma(p, "y", xnc[b][:].rearrange("p (kc m) -> p kc m", kc=8),
                      xnb_d[:, tc * 512:(tc + 1) * 512].rearrange(
                          "(kc p) m -> p kc m", p=128), s_xn[b])
        xn_tgt[b] = t_

    def load_cr(ui):
        e, _, tc = units[ui]
        b = ui % 2
        if crp_use[0] is not None:
            p.wait("g", crp_use[0][0], crp_use[0][1])
        _, t_ = e_dma(p, "g", crp[0][:],
                      cT[e:e + 1, tc * 512:(tc + 1) * 512]
                      .partition_broadcast(128).squeeze(1), s_cr[b])
        cr_tgt[b] = t_

    xn_use = [None, None]
    crp_use = [None, None]
    xn_tgt = [None, None]
    cr_tgt = [None, None]
    w1_tgt = load_w1(0, 0, None)
    w3_tgt = [load_w3(0, 0, 0, None), None]
    w13_use = [None, None]
    w2_tgt = load_w2(0, 0, None)
    load_xn(0)
    load_cr(0)
    up_free = [None, None]
    dn_free = [None, None]
    hch_wr = [None, None]
    hch_rd = [None, None]
    ob_last = [None, None]
    regT = {}
    vread = [None, None]
    w13_new = [True]
    w2_new = [True]
    yield "steady"

    npair = [0]
    for ui, (e, kg, tc) in enumerate(units):
        sl = ui % 2
        kgi = ui // ntc
        kb = kgi % 2
        for half in range(2):
            if half == 0:
                if ui + 1 < NU:
                    load_xn(ui + 1)
                if tc == 0 and ui + ntc < NU:
                    e2, kg2, _ = units[ui + ntc]
                    w3_tgt[1 - kb] = load_w3(e2, kg2, 1 - kb,
                                             w13_use[1 - kb])
                if w13_new[0]:
                    p.wait("t", s_w13, w1_tgt)
                    p.wait("t", s_w3[kb], w3_tgt[kb])
                    w13_new[0] = False
                p.wait("t", s_xn[sl], xn_tgt[sl])
                if hch_rd[sl] is not None:
                    p.wait("v", hch_rd[sl][0], hch_rd[sl][1])
            first_v = (half == 0)
            for fb in (2 * half, 2 * half + 1):
                if up_free[0] is not None:
                    p.wait("t", up_free[0][0], up_free[0][1])
                for k in range(8):
                    e_mm(p, upp[0:128, 0:512],
                         wu1[:, k * 512 + fb * 128:k * 512 + (fb + 1) * 128],
                         xnc[sl][:, k * 512:(k + 1) * 512], start=(k == 0),
                         stop=(k == 7))
                tma = p.marker("t")
                if up_free[1] is not None:
                    p.wait("t", up_free[1][0], up_free[1][1])
                for k in range(8):
                    e_mm(p, upq[0:128, 0:512],
                         wu3[kb][:, k * 512 + fb * 128:k * 512 + (fb + 1) * 128],
                         xnc[sl][:, k * 512:(k + 1) * 512], start=(k == 0),
                         stop=(k == 7))
                tmb = p.marker("t")
                p.wait("s", tma[0], tma[1])
                if vread[fb % 2] is not None:
                    p.wait("s", vread[fb % 2][0], vread[fb % 2][1])
                e_act(p, asb[fb % 2][:], upp[0:128, 0:512], AF.Silu)
                sma = p.marker("s")
                p.wait("s", tmb[0], tmb[1])
                e_act(p, pqb[fb % 2][:], upq[0:128, 0:512], AF.Identity)
                smb = p.marker("s")
                up_free[0], up_free[1] = sma, smb
                p.wait("v", sma[0], sma[1])
                p.wait("v", smb[0], smb[1])
                if first_v:
                    p.wait("v", s_cr[sl], cr_tgt[sl])
                    first_v = False
                hd = hch[sl][:, fb * 512:(fb + 1) * 512]
                e_tt(p, hd, asb[fb % 2][:], pqb[fb % 2][:], ALU.mult)
                e_tt(p, hd, hd, crp[0][:], ALU.mult)
                vread[fb % 2] = p.marker("v")
            if half == 1:
                hch_wr[sl] = p.marker("v")
                crp_use[0] = hch_wr[sl]
                xn_use[sl] = p.marker("t")
                if ui + 1 < NU:
                    load_cr(ui + 1)
                if tc == ntc - 1:
                    w13_use[kb] = xn_use[sl]
                    w13_new[0] = True
                    if ui + 1 < NU:
                        e2, kg2, _ = units[ui + 1]
                        w1_tgt = load_w1(e2, kg2, p.marker("t"))
            yield ("up", ui, half)
        for half in range(2):
            if half == 0:
                p.wait("t", hch_wr[sl][0], hch_wr[sl][1])
                if w2_new[0]:
                    p.wait("t", s_w2, w2_tgt)
                    w2_new[0] = False
            for mbp in (2 * half, 2 * half + 1):
                ob = npair[0] % 2
                npair[0] += 1
                for sub in range(2):
                    mb = mbp * 2 + sub
                    dn = dnp[mb % 2]
                    if dn_free[mb % 2] is not None:
                        p.wait("t", dn_free[mb % 2][0], dn_free[mb % 2][1])
                    for k in range(NFB):
                        e_mm(p, dn[0:128, 0:512],
                             wd2[:, k * 1024 + mb * 128:
                                 k * 1024 + (mb + 1) * 128],
                             hch[sl][:, k * 512:(k + 1) * 512],
                             start=(k == 0), stop=(k == NFB - 1))
                    tmd = p.marker("t")
                    p.wait("s", tmd[0], tmd[1])
                    if ob_last[ob] is not None and sub == 0:
                        p.wait("s", s_ac[ob], ob_last[ob])
                    e_act(p, obp[ob][:, sub * 512:(sub + 1) * 512],
                          dn[0:128, 0:512], AF.Identity)
                    dn_free[mb % 2] = p.marker("s")
                sm = dn_free[1]
                p.wait("g", sm[0], sm[1])
                reg = (mbp, tc)
                if reg in regT:
                    rs, rt = regT[reg]
                    p.wait("g", rs, rt)
                    p.op("g", lambda eng, mbp=mbp, tc=tc, ob=ob:
                         eng.dma_start(
                             out=acc_d[mbp * 256:(mbp + 1) * 256,
                                       tc * 512:(tc + 1) * 512].rearrange(
                                 "(kc p) m -> p kc m", p=128),
                             in_=obp[ob][:].rearrange("p (kc m) -> p kc m",
                                                      kc=2),
                             accum_op=ALU.add), s_ac[ob], 16, kind="d")
                else:
                    p.op("g", lambda eng, mbp=mbp, tc=tc, ob=ob:
                         eng.dma_start(
                             out=acc_d[mbp * 256:(mbp + 1) * 256,
                                       tc * 512:(tc + 1) * 512].rearrange(
                                 "(kc p) m -> p kc m", p=128),
                             in_=obp[ob][:].rearrange("p (kc m) -> p kc m",
                                                      kc=2)), s_ac[ob], 16,
                         kind="d")
                p.dma_sems.add(s_ac[ob])
                ob_last[ob] = s_ac[ob].n
                regT[reg] = (s_ac[ob], s_ac[ob].n)
            if half == 1:
                hch_rd[sl] = p.marker("t")
                if tc == ntc - 1 and ui + 1 < NU:
                    e2, kg2, _ = units[ui + 1]
                    w2_tgt = load_w2(e2, kg2, p.marker("t"))
                    w2_new[0] = True
            yield ("dn", ui, half)
    if ob_last[0] is not None:
        p.wait("g", s_ac[0], ob_last[0])
    if ob_last[1] is not None:
        p.wait("g", s_ac[1], ob_last[1])
    p.close_scope(lfps)
    p.close_scope(lfsb)


def _L2f_final(ctx, handles):
    nc, p, io = ctx
    sIn, acc_d, x_out = handles["sin"], handles["acc"], handles["x3"]
    fsem = p.sem("fin")
    fst2 = [p.sem("osta"), p.sem("ostb")]
    sch = [p.sb("fch2", [128, 1024]) for _ in range(2)]
    hch = [p.sb("fhh2", [128, 1024]) for _ in range(2)]
    fl = {}
    fs_t = {}
    nhalf = T // 1024
    for dd in range(8):
        for half in range(nhalf):
            u = (dd * nhalf + half) % 2
            c0 = half * 1024
            key = dd * nhalf + half
            if key >= 2:
                pk = key - 2
                p.wait("y", fl[pk][0], fl[pk][1])
                p.wait("y", fst2[u], fs_t[pk])
            _, t1 = e_dma(p, "y", sch[u][:],
                          sIn[dd * 128:(dd + 1) * 128, c0:c0 + 1024], fsem)
            _, t2 = e_dma(p, "y", hch[u][:],
                          acc_d[dd * 128:(dd + 1) * 128, c0:c0 + 1024], fsem)
            p.wait("v", fsem, t2)
            e_tt(p, sch[u][:], sch[u][:], hch[u][:], ALU.add)
            vm = p.marker("v")
            p.wait("y", vm[0], vm[1])
            _, ft = e_dma(p, "y", x_out[dd * 128:(dd + 1) * 128, c0:c0 + 1024],
                          sch[u][:], fst2[u])
            fl[key] = vm
            fs_t[key] = ft


def build_L1f(ctx=None):
    E, F_ = 8, 1024
    dc = 256
    own = ctx is None
    if own:
        nc = bass.Bass("TRN2", target_bir_lowering=False)
        p = P(nc)
        io = None
    else:
        nc, p, io = ctx
    xn = _dram(nc, io, "xn", [D, T], F32, "ExternalInput")
    sIn = _dram(nc, io, "sin", [D, T], F32, "ExternalInput")
    g1d = _dram(nc, io, "g1", [D], F32, "ExternalInput")
    g2d = _dram(nc, io, "g2", [D], F32, "ExternalInput")
    WqT = _dram(nc, io, "WqT", [D, D], F32, "ExternalInput")
    WkcT = _dram(nc, io, "WkcT", [D, dc], F32, "ExternalInput")
    WvcT = _dram(nc, io, "WvcT", [D, dc], F32, "ExternalInput")
    WoT = _dram(nc, io, "WoT", [D, D], F32, "ExternalInput")
    WrT = _dram(nc, io, "WrT", [D, E], F32, "ExternalInput")
    ebias = _dram(nc, io, "ebias", [E], F32, "ExternalInput")
    sW1T = _dram(nc, io, "sW1T", [D, F_], F32, "ExternalInput")
    sW2T = _dram(nc, io, "sW2T", [F_, D], F32, "ExternalInput")
    sW3T = _dram(nc, io, "sW3T", [D, F_], F32, "ExternalInput")
    eW1T = _dram(nc, io, "eW1T", [E, D, F_], F32, "ExternalInput")
    eW2T = _dram(nc, io, "eW2T", [E, F_, D], F32, "ExternalInput")
    eW3T = _dram(nc, io, "eW3T", [E, D, F_], F32, "ExternalInput")
    iden = _dram(nc, io, "iden", [128, 128], F32, "ExternalInput")
    x_out = _dram(nc, io, "x2", [D, T], F32, "ExternalOutput")
    X2d = _dram(nc, io, "X2scr", [D, T], F32, "Internal")
    cT = _dram(nc, io, "cTscr", [E, T], BF16, "Internal")
    acc_d = _dram(nc, io, "accscr", [D, T], F32, "Internal")

    g1v, _, _ = load_bias_cols(p, g1d, 8, "g1v")
    g2v, gs, gt = load_bias_cols(p, g2d, 8, "g2v")
    p.wait("v", gs, gt)

    nx = [p.sb("nx1", [128, T], BF16) for _ in range(8)]
    ao_d = _dram(nc, io, "aoscr", [D, T], BF16, "Internal")
    lp = lin_pool(p, 8)
    tmp512 = p.sb("tmp512", [128, 512])
    sc1 = []
    np_ = norm_pool(p, scope=sc1)
    xch = [p.sb("xch1", [128, 512 * 8], scope=sc1) for _ in range(2)]

    def rms_apply(d, tc, t0, tw, xa, mu_bc, rstd_bc):
        e_stt(p, tmp512[:, 0:tw], xa, g1v[:, d:d + 1], rstd_bc,
              ALU.mult, ALU.mult)
        e_copy(p, nx[d][:, t0:t0 + tw], tmp512[:, 0:tw])

    norm_feat(p, np_, ("dram", xn), rms_apply, 1e-6, mean_center=False,
              xch=xch)
    p.close_scope(sc1)

    sc2 = []
    kc = [p.sb("kc1", [128, T], BF16, scope=sc2) for _ in range(2)]
    vt = p.sb("vt1", [128, (T // 128) * 258], BF16, scope=sc2)
    qh = [p.sb("qh1", [128, T], BF16, scope=sc2) for _ in range(2)]
    ao = [p.sb("ao1", [128, T], BF16, scope=sc2) for _ in range(2)]
    wv_sb = [p.sb("wv1", [128, dc], BF16, scope=sc2) for _ in range(8)]
    ones_t = p.sb("onet1", [1, T], BF16, scope=sc2)
    onesq = p.sb("onsq1", [1, 128], scope=sc2)
    o_scr = p.sb("oscr1", [128, 512 * 2], scope=sc2)
    osum = p.sb("osum1", [1, 512], scope=sc2)
    rec = p.sb("rec1", [1, 512], scope=sc2)
    state = dict(
        sc_ps=[p.pt("scp1", [128, 512], scope=sc2) for _ in range(2)],
        pbuf=[p.sb("pb1", [128, 512], BF16, scope=sc2) for _ in range(2)],
        ov_ps=[[p.pt("ovp1", [128, 512], scope=sc2)],
               [p.pt("ovp1b", [128, 512], scope=sc2)],
               [p.pt("ovp1c", [1, 512], scope=sc2)]],
    )
    e_memset(p, ones_t[:], 1.0)
    e_memset(p, onesq[:], 1.0)
    ev_kc = evac_to_tiles(p, kc, AF.Identity)
    linear_fm(p, lp, WkcT, nx, D, dc, ev_kc)
    wsem = p.sem("wvs")
    for k in range(8):
        e_dma(p, "g", wv_sb[k][:], WvcT[k * 128:(k + 1) * 128, :], wsem)
    p.wait("t", wsem, wsem.n)
    for tt in range(T // 128):
        e_memset(p, vt[:, tt * 258 + 256:tt * 258 + 258], 1.0)
    vm0 = p.marker("v")
    p.wait("s", vm0[0], vm0[1])

    def ev_v(tt, psum, tmark):
        p.wait("s", tmark[0], tmark[1])
        e_act(p, vt[:, tt * 258:tt * 258 + 256], psum[:, 0:256], AF.Identity)
        return p.marker("s")
    linear_tm(p, lp, wv_sb, nx, D, dc, ev_v)
    p.barrier("ytvsg")

    X2st = [p.sem("osta"), p.sem("ostb")]
    xnld = [p.sem("xla"), p.sem("xlb")]
    for h in range(4):
        ev_q = evac_to_tiles(p, qh, AF.Identity)
        linear_fm(p, lp, WqT[:, 256 * h:256 * h + 256], nx, D, 256, ev_q)
        p.barrier("ytvsg")

        def o_cb(tg, g0, gw, h=h):
            ov = state["ov_ps"]
            e_act(p, o_scr[:, 0:512], ov[0][0][:, :], AF.Identity)
            e_act(p, o_scr[:, 512:1024], ov[1][0][:, :], AF.Identity)
            e_act(p, osum[:, 0:512], ov[2][0][0:1, :], AF.Identity)
            sm = p.marker("s")
            p.wait("v", sm[0], sm[1])
            p.op("v", lambda eng: eng.reciprocal(out=rec[:, 0:512],
                                                 in_=osum[:, 0:512]))
            vm = p.marker("v")
            p.wait("t", vm[0], vm[1])
            for half in range(2):
                e_mm(p, ov[half][0][:, :], onesq[0:1, 0:128], rec[:, 0:512])
            tb = p.marker("t")
            p.wait("v", tb[0], tb[1])
            for half in range(2):
                e_tt(p, ao[half][:, g0:g0 + 512],
                     o_scr[:, half * 512:(half + 1) * 512],
                     ov[half][0][:, :], ALU.mult)
            vo = p.marker("v")
            p.wait("t", vo[0], vo[1])
            p.wait("s", vo[0], vo[1])

        attn_head(
            p,
            sc_chunks=[(lambda s0, k=0: kc[0][:, s0:s0 + 128],
                        lambda g0, gw: qh[0][:, g0:g0 + gw]),
                       (lambda s0: kc[1][:, s0:s0 + 128],
                        lambda g0, gw: qh[1][:, g0:g0 + gw])],
            get_vchunks=lambda sb_: [
                (vt[:, sb_ * 258:sb_ * 258 + 128], 0, 128),
                (vt[:, sb_ * 258 + 128:sb_ * 258 + 256], 1, 128),
                (vt[:, sb_ * 258 + 256:sb_ * 258 + 257], 2, 1)],
            o_cb=o_cb, state=state, tgsz=512, exp_scale=0.0625)
        hm = p.marker("v")
        p.wait("y", hm[0], hm[1])
        aos = p.sem("aosx")
        for half in range(2):
            e_dma(p, "y", ao_d[(2 * h + half) * 128:(2 * h + half + 1) * 128, :],
                  ao[half][:], aos)
        p.wait("v", aos, aos.n)
        p.wait("s", aos, aos.n)
        p.wait("t", aos, aos.n)
    p.barrier("ytvsg")
    p.close_scope(sc2)

    sc2b = []
    obt = [p.sb("obt1", [128, T], BF16, scope=sc2b) for _ in range(8)]
    obs = p.sem("obl")
    for k in range(8):
        e_dma(p, "y", obt[k][:], ao_d[k * 128:(k + 1) * 128, :], obs)
    p.wait("t", obs, obs.n)
    xrs = [p.sb("xr1", [128, 512], scope=sc2b) for _ in range(2)]
    sch2 = [p.sb("sc1h", [128, T], scope=sc2b) for _ in range(2)]
    xl_t = {}
    ev_done = {}
    st_done = {}

    NTC = T // 512

    def ev_X2(mb, tc, t0, tw, psum, tmark):
        u = (mb * NTC + tc) % 2
        uk = mb * NTC + tc
        if uk == 0:
            for k2 in (0, 1):
                _, xl_t[k2] = e_dma(p, "y", xrs[k2 % 2][:],
                                    xn[0:128, k2 * 512:(k2 + 1) * 512],
                                    xnld[k2 % 2])
        if uk + 2 <= 8 * NTC - 1:
            nk_ = uk + 2
            mb2, tc2 = nk_ // NTC, nk_ % NTC
            p.wait("y", ev_done[uk][0], ev_done[uk][1]) if uk in ev_done else None
        sch = sch2[mb % 2]
        p.wait("s", tmark[0], tmark[1])
        p.wait("s", xnld[u], xl_t[uk])
        if tc == 0 and mb >= 2:
            p.wait("s", X2st[mb % 2], st_done[mb - 2])
        e_act(p, sch[:, t0:t0 + tw], psum, AF.Identity)
        sm = p.marker("s")
        p.wait("v", sm[0], sm[1])
        e_tt(p, sch[:, t0:t0 + tw], sch[:, t0:t0 + tw],
             xrs[u][:, 0:tw], ALU.add)
        vm = p.marker("v")
        ev_done[uk] = vm
        if uk + 2 <= 8 * NTC - 1:
            nk_ = uk + 2
            mb2, tc2 = nk_ // NTC, nk_ % NTC
            p.wait("y", vm[0], vm[1])
            _, xl_t[nk_] = e_dma(p, "y", xrs[u][:],
                                 xn[mb2 * 128:(mb2 + 1) * 128,
                                    tc2 * 512:tc2 * 512 + 512], xnld[u])
        p.wait("y", vm[0], vm[1])
        e_dma(p, "y", X2d[mb * 128:(mb + 1) * 128, t0:t0 + tw],
              sch[:, t0:t0 + tw], X2st[mb % 2])
        st_done[mb] = X2st[mb % 2].n
        return vm

    linear_fm(p, lp, WoT, obt, D, D, ev_X2)
    p.barrier("ytvsg")
    p.close_scope(sc2b)

    sc3 = []
    np2_ = norm_pool(p, scope=sc3)
    xch2 = [p.sb("xch1b", [128, 512 * 8], scope=sc3) for _ in range(2)]

    def rms2_apply(d, tc, t0, tw, xa, mu_bc, rstd_bc):
        e_stt(p, tmp512[:, 0:tw], xa, g2v[:, d:d + 1], rstd_bc,
              ALU.mult, ALU.mult)
        e_copy(p, nx[d][:, t0:t0 + tw], tmp512[:, 0:tw])

    norm_feat(p, np2_, ("dram", X2d), rms2_apply, 1e-6, mean_center=False,
              xch=xch2)
    p.close_scope(sc3)

    lp2 = lin_pool(p, 8, np_ps=4)
    router_gates(p, lp2, nx, WrT, ebias, E, cT, "softmax", idn_dram=iden)

    hb = [p.sb("hb1", [128, T], BF16) for _ in range(8)]
    asb = [p.sb("asb1", [128, 512]) for _ in range(2)]
    crep = [p.sb("crep1", [128, T], BF16) for _ in range(2)]
    ob = [p.sb("ob1", [128, 512]) for _ in range(2)]
    wB = [p.sb("wB1", [128, 8 * 128], BF16) for _ in range(2)]
    crs = [p.sem("cra"), p.sem("crb")]
    w13s = [p.sem("w13a"), p.sem("w13b")]
    accs = [p.sem("acca"), p.sem("accb")]
    acc_t = [None, None]
    first_acc = [True]
    wA = lp2["wbuf"]
    crt = {}

    def moe_pass(e):
        W1 = sW1T if e < 0 else eW1T[e]
        W3 = sW3T if e < 0 else eW3T[e]
        W2 = sW2T if e < 0 else eW2T[e]
        if e >= 0:
            _, crt[e % 2] = e_dma(p, "g", crep[e % 2][:],
                                  cT[e:e + 1, :].partition_broadcast(128)
                                  .squeeze(1), crs[e % 2])
            p.wait("v", crs[e % 2], crt[e % 2])
        wt = {}
        for fb in range(2):
            _, t1 = e_dma(p, "g", wA[fb % 2][:].rearrange(
                "p (kc m) -> p kc m", kc=8),
                W1[:, fb * 128:(fb + 1) * 128].rearrange(
                    "(kc p) m -> p kc m", p=128), w13s[fb % 2])
            _, t2 = e_dma(p, "g", wB[fb % 2][:].rearrange(
                "p (kc m) -> p kc m", kc=8),
                W3[:, fb * 128:(fb + 1) * 128].rearrange(
                    "(kc p) m -> p kc m", p=128), w13s[fb % 2])
            wt[fb] = t2
        for fb in range(8):
            p.wait("t", w13s[fb % 2], wt[fb])
            for tc in range(T // 512):
                t0 = tc * 512
                sl = 2 * (tc % 2)
                pp, pq = lp2["ps"][sl], lp2["ps"][sl + 1]
                if lp2["prev"][sl] is not None:
                    p.wait("t", lp2["prev"][sl][0], lp2["prev"][sl][1])
                for k in range(8):
                    e_mm(p, pp[0:128, 0:512],
                         wA[fb % 2][0:128, k * 128:(k + 1) * 128],
                         nx[k][:, t0:t0 + 512], start=(k == 0), stop=(k == 7))
                tma = p.marker("t")
                if lp2["prev"][sl + 1] is not None:
                    p.wait("t", lp2["prev"][sl + 1][0],
                           lp2["prev"][sl + 1][1])
                for k in range(8):
                    e_mm(p, pq[0:128, 0:512],
                         wB[fb % 2][0:128, k * 128:(k + 1) * 128],
                         nx[k][:, t0:t0 + 512], start=(k == 0), stop=(k == 7))
                tmb = p.marker("t")
                p.wait("s", tma[0], tma[1])
                prevv = lp2["prev"][sl + 1]
                if prevv is not None:
                    p.wait("s", prevv[0], prevv[1])
                e_act(p, asb[tc % 2][:, 0:512], pp[0:128, 0:512], AF.Silu)
                sm_ = p.marker("s")
                lp2["prev"][sl] = sm_
                p.wait("v", tmb[0], tmb[1])
                p.wait("v", sm_[0], sm_[1])
                e_tt(p, hb[fb][:, t0:t0 + 512], asb[tc % 2][:, 0:512],
                     pq[0:128, 0:512], ALU.mult)
                if e >= 0:
                    e_tt(p, hb[fb][:, t0:t0 + 512], hb[fb][:, t0:t0 + 512],
                         crep[e % 2][:, t0:t0 + 512], ALU.mult)
                lp2["prev"][sl + 1] = p.marker("v")
            if fb + 2 < 8:
                gm = p.marker("t")
                p.wait("g", gm[0], gm[1])
                _, t1 = e_dma(p, "g", wA[fb % 2][:].rearrange(
                    "p (kc m) -> p kc m", kc=8),
                    W1[:, (fb + 2) * 128:(fb + 3) * 128].rearrange(
                        "(kc p) m -> p kc m", p=128), w13s[fb % 2])
                _, t2 = e_dma(p, "g", wB[fb % 2][:].rearrange(
                    "p (kc m) -> p kc m", kc=8),
                    W3[:, (fb + 2) * 128:(fb + 3) * 128].rearrange(
                        "(kc p) m -> p kc m", p=128), w13s[fb % 2])
                wt[fb + 2] = t2
        hvm = p.marker("v")
        p.wait("g", hvm[0], hvm[1])
        w2t = {}
        for mb in range(2):
            _, w2t[mb] = e_dma(p, "g", wB[mb % 2][:].rearrange(
                "p (kc m) -> p kc m", kc=8),
                W2[:, mb * 128:(mb + 1) * 128].rearrange(
                    "(kc p) m -> p kc m", p=128), w13s[mb % 2])
        for mb in range(8):
            p.wait("t", w13s[mb % 2], w2t[mb])
            for tc in range(T // 512):
                t0 = tc * 512
                uu = mb * (T // 512) + tc
                u = uu % 2
                u4 = uu % 4
                pp = lp2["ps"][u4]
                if lp2["prev"][u4] is not None:
                    p.wait("t", lp2["prev"][u4][0], lp2["prev"][u4][1])
                for k in range(8):
                    e_mm(p, pp[0:128, 0:512],
                         wB[mb % 2][0:128, k * 128:(k + 1) * 128],
                         hb[k][:, t0:t0 + 512], start=(k == 0), stop=(k == 7))
                tm = p.marker("t")
                p.wait("s", tm[0], tm[1])
                if acc_t[u] is not None:
                    p.wait("s", accs[u], acc_t[u])
                e_act(p, ob[u][:, 0:512], pp[0:128, 0:512], AF.Identity)
                sm = p.marker("s")
                lp2["prev"][u4] = sm
                p.wait("g", sm[0], sm[1])
                if first_acc[0]:
                    p.op("g", lambda eng, mb=mb, t0=t0, u=u: eng.dma_start(
                        out=acc_d[mb * 128:(mb + 1) * 128, t0:t0 + 512],
                        in_=ob[u][:, 0:512]), accs[u], 16, kind="d")
                else:
                    p.op("g", lambda eng, mb=mb, t0=t0, u=u: eng.dma_start(
                        out=acc_d[mb * 128:(mb + 1) * 128, t0:t0 + 512],
                        in_=ob[u][:, 0:512], accum_op=ALU.add), accs[u], 16,
                        kind="d")
                p.dma_sems.add(accs[u])
                acc_t[u] = accs[u].n
            if mb + 2 < 8:
                gm2 = p.marker("t")
                p.wait("g", gm2[0], gm2[1])
                _, w2t[mb + 2] = e_dma(p, "g", wB[mb % 2][:].rearrange(
                    "p (kc m) -> p kc m", kc=8),
                    W2[:, (mb + 2) * 128:(mb + 3) * 128].rearrange(
                        "(kc p) m -> p kc m", p=128), w13s[mb % 2])
        first_acc[0] = False
        gm3 = p.marker("t")
        p.wait("g", gm3[0], gm3[1])

    moe_pass(-1)
    for e in range(E):
        moe_pass(e)
    p.barrier("ytvsg")

    fsem = p.sem("fin")
    fst2 = [p.sem("osta"), p.sem("ostb")]
    sch = [p.sb("fc1", [128, 1024]) for _ in range(2)]
    hch = [p.sb("fh1", [128, 1024]) for _ in range(2)]
    xch3 = [p.sb("fx1", [128, 1024]) for _ in range(2)]
    fl = {}
    fs_t = {}
    nhalf = T // 1024
    for dd in range(8):
        for half in range(nhalf):
            u = (dd * nhalf + half) % 2
            c0 = half * 1024
            key = dd * nhalf + half
            if key >= 2:
                pk = key - 2
                p.wait("y", fl[pk][0], fl[pk][1])
                p.wait("y", fst2[u], fs_t[pk])
            _, t1 = e_dma(p, "y", sch[u][:],
                          sIn[dd * 128:(dd + 1) * 128, c0:c0 + 1024], fsem)
            _, t1b = e_dma(p, "y", xch3[u][:],
                           X2d[dd * 128:(dd + 1) * 128, c0:c0 + 1024], fsem)
            _, t2 = e_dma(p, "y", hch[u][:],
                          acc_d[dd * 128:(dd + 1) * 128, c0:c0 + 1024], fsem)
            p.wait("v", fsem, t2)
            e_tt(p, sch[u][:], sch[u][:], xch3[u][:], ALU.add)
            e_tt(p, sch[u][:], sch[u][:], hch[u][:], ALU.add)
            vm = p.marker("v")
            p.wait("y", vm[0], vm[1])
            _, ft = e_dma(p, "y", x_out[dd * 128:(dd + 1) * 128, c0:c0 + 1024],
                          sch[u][:], fst2[u])
            fl[key] = vm
            fs_t[key] = ft
    if own:
        final_wait_all(p)
        return p.finalize()


def build_L1a(ctx=None):
    DI = 512
    CH = 128
    own = ctx is None
    if own:
        nc = bass.Bass("TRN2", target_bir_lowering=False)
        p = P(nc)
        io = None
    else:
        nc, p, io = ctx
    x1 = _dram(nc, io, "x", [D, T], F32, "ExternalInput")
    Wq = _dram(nc, io, "Wq", [D, DI], F32, "ExternalInput")
    Wk = _dram(nc, io, "Wk", [D, DI], F32, "ExternalInput")
    Wv = _dram(nc, io, "Wv", [D, DI], F32, "ExternalInput")
    Wa = _dram(nc, io, "Wa", [D, DI], F32, "ExternalInput")
    ba = _dram(nc, io, "ba", [DI], F32, "ExternalInput")
    WoT = _dram(nc, io, "Wo", [DI, D], F32, "ExternalInput")
    gam = _dram(nc, io, "gamma", [D], F32, "ExternalInput")
    bet = _dram(nc, io, "beta", [D], F32, "ExternalInput")
    blk2 = _dram(nc, io, "blk2", [128, 2], F32, "ExternalInput")
    blk2b = _dram(nc, io, "blk2b", [2, 128], F32, "ExternalInput")
    triu = _dram(nc, io, "triu", [128, 128], F32, "ExternalInput")
    iden = _dram(nc, io, "iden", [128, 128], F32, "ExternalInput")
    xn_out = _dram(nc, io, "xn", [D, T], F32, "ExternalOutput")
    s_out = _dram(nc, io, "s", [D, T], F32, "ExternalOutput")

    gv, _, _ = load_bias_cols(p, gam, 8, "gv")
    bv, bs, bt = load_bias_cols(p, bet, 8, "bv")
    bav, bas, bat = load_bias_cols(p, ba, 4, "bav")
    p.wait("v", bs, bt)
    p.wait("s", bas, bat)

    lp = lin_pool(p, 8)
    sc1 = []
    np_ = norm_pool(p, scope=sc1)
    xch = [p.sb("xchA", [128, 512 * 8], scope=sc1) for _ in range(2)]
    xnf_ch = [p.sb("xnfA", [128, 512 * 8], scope=sc1) for _ in range(2)]
    stx = [p.sem("sxa"), p.sem("sxb")]
    st_marks = [None, None]

    def ln_apply(d, tc, t0, tw, xa, mu_bc, rstd_bc):
        ch = xnf_ch[tc % 2]
        if d == 0 and st_marks[tc % 2] is not None:
            p.wait("v", stx[tc % 2], st_marks[tc % 2])
        dst = ch[:, d * 512:d * 512 + tw]
        e_tt(p, dst, xa, mu_bc, ALU.subtract)
        e_tt(p, dst, dst, rstd_bc, ALU.mult)
        if d == 7:
            vm = p.marker("v")
            p.wait("y", vm[0], vm[1])
            for dd in range(8):
                e_dma(p, "y", xn_out[dd * 128:(dd + 1) * 128, t0:t0 + tw],
                      ch[:, dd * 512:dd * 512 + tw], stx[tc % 2])
            st_marks[tc % 2] = stx[tc % 2].n

    norm_feat(p, np_, ("dram", x1), ln_apply, 1e-5, xch=xch)
    p.close_scope(sc1)

    xnb2 = [p.sb("xnb2a", [128, T], BF16) for _ in range(8)]
    sc1b = []
    np1b = norm_pool(p, scope=sc1b)
    xchb = [p.sb("xchB", [128, 512 * 8], scope=sc1b) for _ in range(2)]
    tmpb = p.sb("tmpB", [128, 512], scope=sc1b)

    def ln2_apply(d, tc, t0, tw, xa, mu_bc, rstd_bc):
        e_tt(p, tmpb[:, 0:tw], xa, mu_bc, ALU.subtract)
        e_tt(p, tmpb[:, 0:tw], tmpb[:, 0:tw], rstd_bc, ALU.mult)
        e_ts(p, tmpb[:, 0:tw], tmpb[:, 0:tw], gv[:, d:d + 1], bv[:, d:d + 1],
             op0=ALU.mult, op1=ALU.add)
        e_copy(p, xnb2[d][:, t0:t0 + tw], tmpb[:, 0:tw])

    norm_feat(p, np1b, ("dram", xn_out), ln2_apply, 1e-5, xch=xchb)
    p.close_scope(sc1b)

    qh = [p.sb("qhA", [128, T], BF16) for _ in range(2)]
    kh = [p.sb("khA", [128, T], BF16) for _ in range(2)]
    la = [p.sb("laA", [128, T]) for _ in range(2)]
    vtA = p.sb("vtA", [128, (T // 128) * 256], BF16)
    ao = [p.sb("aoA", [128, T], BF16) for _ in range(4)]
    Sst = [p.sb("SstA", [64, 64]) for _ in range(4)]
    Sstb = [p.sb("SstbA", [64, 64], BF16) for _ in range(4)]
    cumt = [p.sb("cumA", [128, CH]) for _ in range(2)]
    ep = [p.sb("epA", [128, CH]) for _ in range(2)]
    em_ = [p.sb("emA", [128, CH]) for _ in range(2)]
    e2_ = [p.sb("e2A", [128, CH]) for _ in range(2)]
    qt_ = [p.sb("qtA", [64, CH], BF16) for _ in range(4)]
    kt_ = [p.sb("ktA", [64, CH], BF16) for _ in range(4)]
    k2_ = [p.sb("k2A", [64, CH], BF16) for _ in range(4)]
    pbA = p.sb("pbA", [128, CH], BF16)
    k2T = p.sb("k2TA", [128, 64], BF16)
    onesA = p.sb("onesA", [128, CH])
    blkA = p.sb("blkA", [128, 2])
    blkB = p.sb("blkB", [2, 128])
    triA = p.sb("triA", [128, 128], BF16)
    idnA = p.sb("idnA", [128, 128], BF16)
    sqA = p.sb("sqA", [128, 512])
    ssA = p.sb("ssA", [2, 512])
    wvA = [p.sb("wvA", [128, 256], BF16) for _ in range(8)]
    e_memset(p, onesA[:], 1.0)
    csem = p.sem("clc")
    csg = p.sem("cstg")
    e_dma(p, "y", blkA[:], blk2, csem)
    e_dma(p, "y", blkB[:], blk2b, csem)
    e_dma(p, "g", triA[:], triu, csg)
    e_dma(p, "g", idnA[:], iden, csg)
    p.wait("t", csem, csem.n)
    p.wait("v", csem, csem.n)
    p.wait("t", csg, csg.n)
    p.wait("v", csg, csg.n)
    scps = p.pt("scpsA", [128, CH])
    ops_ = p.pt("opsA", [64, CH])
    k2ps = p.pt("k2psA", [128, 64], BF16)
    dsps = p.pt("dspsA", [64, 64])
    bcps = p.pt("bcpsA", [128, 512])

    for half in range(2):
        c0col = half * 256
        ev_q = evac_to_tiles(p, qh, AF.Identity)
        linear_fm(p, lp, Wq[:, c0col:c0col + 256], xnb2, D, 256, ev_q)
        def ev_a(mb, tc, t0, tw, psum, tmark):
            p.wait("s", tmark[0], tmark[1])
            e_act(p, sqA[:, 0:tw], psum, AF.Sigmoid,
                  bias=bav[:, 2 * half + mb:2 * half + mb + 1])
            e_act(p, la[mb][:, t0:t0 + tw], sqA[:, 0:tw], AF.Ln)
            return p.marker("s")
        linear_fm(p, lp, Wa[:, c0col:c0col + 256], xnb2, D, 256, ev_a)
        prev_kv = [None]

        def ev_k(mb, tc, t0, tw, psum, tmark):
            p.wait("s", tmark[0], tmark[1])
            if prev_kv[0] is not None:
                p.wait("s", prev_kv[0][0], prev_kv[0][1])
            e_act(p, sqA[:, 0:tw], psum, AF.Square)
            sm = p.marker("s")
            p.wait("t", sm[0], sm[1])
            e_mm(p, bcps[0:2, 0:tw], blkA[0:128, 0:2], sqA[:, 0:tw])
            tm = p.marker("t")
            p.wait("v", tm[0], tm[1])
            e_copy(p, ssA[:, 0:tw], bcps[0:2, 0:tw])
            vm = p.marker("v")
            p.wait("s", vm[0], vm[1])
            e_act(p, ssA[:, 0:tw], ssA[:, 0:tw], AF.Ln)
            e_act(p, ssA[:, 0:tw], ssA[:, 0:tw], AF.Exp, scale=0.5)
            sm2 = p.marker("s")
            p.wait("v", sm2[0], sm2[1])
            e_ts(p, ssA[:, 0:tw], ssA[:, 0:tw], 1e-12, None, op0=ALU.max)
            p.op("v", lambda eng, tw=tw: eng.reciprocal(out=ssA[:, 0:tw],
                                                        in_=ssA[:, 0:tw]))
            vm2 = p.marker("v")
            p.wait("t", vm2[0], vm2[1])
            e_mm(p, bcps[:, 0:tw], blkB[0:2, :], ssA[0:2, 0:tw])
            tb = p.marker("t")
            p.wait("s", tb[0], tb[1])
            e_act(p, sqA[:, 0:tw], psum, AF.Identity)
            sm3 = p.marker("s")
            p.wait("v", sm3[0], sm3[1])
            e_tt(p, kh[mb][:, t0:t0 + tw], sqA[:, 0:tw], bcps[:, 0:tw],
                 ALU.mult)
            prev_kv[0] = p.marker("v")
            return prev_kv[0]
        linear_fm(p, lp, Wk[:, c0col:c0col + 256], xnb2, D, 256, ev_k)
        wsem = p.sem("wvs")
        for k in range(8):
            e_dma(p, "g", wvA[k][:],
                  Wv[k * 128:(k + 1) * 128, c0col:c0col + 256], wsem)
        p.wait("t", wsem, wsem.n)

        def ev_v(tt, psum, tmark):
            p.wait("s", tmark[0], tmark[1])
            e_act(p, vtA[:, tt * 256:tt * 256 + 256], psum[:, 0:256],
                  AF.Identity)
            return p.marker("s")
        linear_tm(p, lp, wvA, xnb2, D, 256, ev_v)
        for hl in range(4):
            e_memset(p, Sst[hl][:], 0.0)
            e_memset(p, Sstb[hl][:], 0.0)
        p.barrier("ytvsg")

        smark = {}
        for c in range(T // CH):
            t0 = c * CH
            u = c % 2
            for tl in range(2):
                p.op("v", lambda eng, tl=tl, t0=t0, u=u: eng.tensor_tensor_scan(
                    out=cumt[tl][:, 0:CH] if u == 0 else cumt[tl][:, 0:CH],
                    data0=onesA[:, 0:CH], data1=la[tl][:, t0:t0 + CH],
                    initial=0.0, op0=ALU.mult, op1=ALU.add))
                vm = p.marker("v")
                p.wait("s", vm[0], vm[1])
                e_act(p, ep[tl][:], cumt[tl][:], AF.Exp)
                sm = p.marker("s")
                p.wait("v", sm[0], sm[1])
                e_ts(p, em_[tl][:], cumt[tl][:], -60.0, None, op0=ALU.max)
                e_ts(p, e2_[tl][:], cumt[tl][:], cumt[tl][:, CH - 1:CH],
                     None, op0=ALU.subtract)
                vm2 = p.marker("v")
                p.wait("s", vm2[0], vm2[1])
                e_act(p, em_[tl][:], em_[tl][:], AF.Exp, scale=-1.0)
                e_act(p, e2_[tl][:], e2_[tl][:], AF.Exp, scale=-1.0)
                sm2 = p.marker("s")
                p.wait("v", sm2[0], sm2[1])
                for sh in range(2):
                    off = sh * 64
                    hl = 2 * tl + sh
                    e_tt(p, qt_[hl][:], qh[tl][off:off + 64, t0:t0 + CH],
                         ep[tl][off:off + 64, :], ALU.mult)
                    e_tt(p, kt_[hl][:], kh[tl][off:off + 64, t0:t0 + CH],
                         em_[tl][off:off + 64, :], ALU.mult)
                    e_tt(p, k2_[hl][:], kh[tl][off:off + 64, t0:t0 + CH],
                         e2_[tl][off:off + 64, :], ALU.mult)
            vql = p.marker("v")
            p.wait("t", vql[0], vql[1])
            for hl in range(4):
                tl, off = hl // 2, (hl % 2) * 64
                vsl = vtA[:, (t0 // 128) * 256 + hl * 64:
                          (t0 // 128) * 256 + hl * 64 + 64]
                e_mm(p, scps[:, 0:CH], kt_[hl][:, :], qt_[hl][:, :])
                tm = p.marker("t")
                p.wait("v", tm[0], tm[1])
                e_tt(p, pbA[:], scps[:, 0:CH], triA[:], ALU.mult)
                vmk = p.marker("v")
                p.wait("t", vmk[0], vmk[1])
                e_mm(p, ops_[0:64, 0:CH], vsl, pbA[:], start=True, stop=False)
                e_mm(p, ops_[0:64, 0:CH], Sstb[hl][:, :], qt_[hl][:, :],
                     start=False, stop=True)
                p.op("t", lambda eng, hl=hl: eng.transpose(
                    k2ps[:, 0:64], k2_[hl][:, :], idnA[0:64, 0:64]))
                tm2 = p.marker("t")
                p.wait("s", tm2[0], tm2[1])
                e_act(p, ao[2 * half + hl // 2][off:off + 64, t0:t0 + CH]
                      if False else sqA[0:64, 0:CH], ops_[0:64, 0:CH],
                      AF.Identity)
                e_act(p, k2T[:, 0:64], k2ps[:, 0:64], AF.Identity)
                sm3 = p.marker("s")
                p.wait("v", sm3[0], sm3[1])
                e_copy(p, ao[2 * half + hl // 2][off:off + 64, t0:t0 + CH],
                       sqA[0:64, 0:CH])
                p.wait("t", sm3[0], sm3[1])
                e_mm(p, dsps[:, :], k2T[:, 0:64], vsl)
                tm3 = p.marker("t")
                p.wait("v", tm3[0], tm3[1])
                e_ts(p, Sst[hl][:, :], Sst[hl][:, :],
                     ep[tl][off:off + 64, CH - 1:CH], None, op0=ALU.mult)
                e_tt(p, Sst[hl][:, :], Sst[hl][:, :], dsps[:, :], ALU.add)
                e_copy(p, Sstb[hl][:, :], Sst[hl][:, :])
                vS = p.marker("v")
                p.wait("t", vS[0], vS[1])
        p.barrier("ytvsg")

    xrs = [la[0][:, 0:512], la[0][:, 512:1024]]
    xrs2 = [la[1][:, 0:512], la[1][:, 512:1024]]
    sch2 = [qh[0][:, 0:1024].bitcast(F32), qh[1][:, 0:1024].bitcast(F32)]
    xnld = [p.sem("xla"), p.sem("xlb")]
    stS = [p.sem("osta"), p.sem("ostb")]
    xl_t = {}
    ev_done = {}
    st_done = {}
    NTC = T // 512

    def ev_s(mb, tc, t0, tw, psum, tmark):
        u = (mb * NTC + tc) % 2
        uk = mb * NTC + tc
        if uk == 0:
            for k2 in (0, 1):
                _, xl_t[k2] = e_dma(p, "y", xrs[k2 % 2],
                                    x1[0:128, k2 * 512:(k2 + 1) * 512],
                                    xnld[k2 % 2])
                e_dma(p, "y", xrs2[k2 % 2],
                      xn_out[0:128, k2 * 512:(k2 + 1) * 512], xnld[k2 % 2])
                xl_t[k2] = xnld[k2 % 2].n
        sch = sch2[u]
        p.wait("s", tmark[0], tmark[1])
        p.wait("s", xnld[u], xl_t[uk])
        if uk >= 2 and (uk - 2) in st_done:
            p.wait("s", stS[u], st_done[uk - 2])
        e_act(p, sch[:, 0:tw], psum, AF.Identity)
        sm = p.marker("s")
        p.wait("v", sm[0], sm[1])
        e_tt(p, sch[:, 0:tw], sch[:, 0:tw], xrs[u][:, 0:tw], ALU.add)
        e_tt(p, sch[:, 0:tw], sch[:, 0:tw], xrs2[u][:, 0:tw], ALU.add)
        vm = p.marker("v")
        ev_done[uk] = vm
        if uk + 2 <= 8 * NTC - 1:
            nk_ = uk + 2
            mb2, tc2 = nk_ // NTC, nk_ % NTC
            p.wait("y", vm[0], vm[1])
            _, _t = e_dma(p, "y", xrs[u][:],
                          x1[mb2 * 128:(mb2 + 1) * 128,
                             tc2 * 512:tc2 * 512 + 512], xnld[u])
            e_dma(p, "y", xrs2[u][:],
                  xn_out[mb2 * 128:(mb2 + 1) * 128,
                         tc2 * 512:tc2 * 512 + 512], xnld[u])
            xl_t[nk_] = xnld[u].n
        p.wait("y", vm[0], vm[1])
        e_dma(p, "y", s_out[mb * 128:(mb + 1) * 128, t0:t0 + tw],
              sch[:, 0:tw], stS[u])
        st_done[uk] = stS[u].n
        return vm

    linear_fm(p, lp, WoT, ao, DI, D, ev_s)
    if own:
        final_wait_all(p)
        return p.finalize()


def build_L2a(ctx=None, copar_start=None, copar_mid=None, copar=None,
              copar_end=None):
    DI2 = 2048
    NST = 16
    FFH = 2731
    own = ctx is None
    if own:
        nc = bass.Bass("TRN2", target_bir_lowering=False)
        p = P(nc)
        io = None
    else:
        nc, p, io = ctx
    x2 = _dram(nc, io, "x", [D, T], F32, "ExternalInput")
    nmw = _dram(nc, io, "norm_m_w", [D], F32, "ExternalInput")
    ipT = _dram(nc, io, "in_projT", [D, 2 * DI2], F32, "ExternalInput")
    cw = _dram(nc, io, "conv_w", [4, DI2], F32, "ExternalInput")
    cb = _dram(nc, io, "conv_b", [DI2], F32, "ExternalInput")
    xpBT = _dram(nc, io, "xpBT", [DI2, NST], F32, "ExternalInput")
    xpCT = _dram(nc, io, "xpCT", [DI2, NST], F32, "ExternalInput")
    xpdT = _dram(nc, io, "xpdT", [DI2, 1], F32, "ExternalInput")
    dtw = _dram(nc, io, "dt_w", [DI2], F32, "ExternalInput")
    dtb = _dram(nc, io, "dt_b", [DI2], F32, "ExternalInput")
    Alog = _dram(nc, io, "A_log", [DI2, NST], F32, "ExternalInput")
    Dskip = _dram(nc, io, "D_skip", [DI2], F32, "ExternalInput")
    opT = _dram(nc, io, "out_projT", [DI2, D], F32, "ExternalInput")
    naw = _dram(nc, io, "norm_attn_w", [D], F32, "ExternalInput")
    qwT = _dram(nc, io, "qwT", [D, D], F32, "ExternalInput")
    kwT = _dram(nc, io, "kwT", [D, 512], F32, "ExternalInput")
    vwT = _dram(nc, io, "vwT", [D, 512], F32, "ExternalInput")
    owT = _dram(nc, io, "owT", [D, D], F32, "ExternalInput")
    nfw = _dram(nc, io, "norm_ffn_w", [D], F32, "ExternalInput")
    gwT = _dram(nc, io, "gwT", [D, FFH], F32, "ExternalInput")
    uwT = _dram(nc, io, "uwT", [D, FFH], F32, "ExternalInput")
    dwT = _dram(nc, io, "dwT", [FFH, D], F32, "ExternalInput")
    trid = _dram(nc, io, "trid", [4, 128, 512], F32, "ExternalInput")
    x_out = _dram(nc, io, "s", [D, T], F32, "ExternalOutput")
    xn_d = _dram(nc, io, "xn", [D, T], F32, "ExternalOutput")
    zs_d = _dram(nc, io, "zscr", [DI2, T], BF16, "Internal")
    xs_d = _dram(nc, io, "xsscr", [DI2, T], BF16, "Internal")
    Bt_d = _dram(nc, io, "Btscr", [NST, T], F32, "Internal")
    Ct_d = _dram(nc, io, "Ctscr", [NST, T], F32, "Internal")
    dtr_d = _dram(nc, io, "dtrscr", [1, T], F32, "Internal")
    h1_d = _dram(nc, io, "h1scr", [D, T], F32, "Internal")
    h2_d = _dram(nc, io, "h2scr", [D, T], F32, "Internal")
    ao_d = _dram(nc, io, "ao2scr", [D, T], BF16, "Internal")
    ffa_d = _dram(nc, io, "ffascr", [D, T], F32, "Internal")

    gv0, gs0, gt0 = load_bias_cols(p, nmw, 8, "gnm")
    p.wait("v", gs0, gt0)
    lp = lin_pool(p, 16)
    sc_xr = []
    xr = [p.sb("xr2", [128, T], BF16, scope=sc_xr) for _ in range(8)]

    sc1 = []
    np1 = norm_pool(p, scope=sc1)
    xch = [p.sb("xch2", [128, 512 * 8], scope=sc1) for _ in range(2)]
    xnf_ch = [p.sb("xnf2", [128, 512 * 8], scope=sc1) for _ in range(2)]
    stx = [p.sem("sxa"), p.sem("sxb")]
    st_marks = [None, None]

    def ln_apply(d, tc, t0, tw, xa, mu_bc, rstd_bc):
        ch = xnf_ch[tc % 2]
        if d == 0 and st_marks[tc % 2] is not None:
            p.wait("v", stx[tc % 2], st_marks[tc % 2])
        dst = ch[:, d * 512:d * 512 + tw]
        e_tt(p, dst, xa, mu_bc, ALU.subtract)
        e_tt(p, dst, dst, rstd_bc, ALU.mult)
        if d == 7:
            vm = p.marker("v")
            p.wait("y", vm[0], vm[1])
            for dd in range(8):
                e_dma(p, "y", xn_d[dd * 128:(dd + 1) * 128, t0:t0 + tw],
                      ch[:, dd * 512:dd * 512 + tw], stx[tc % 2])
            st_marks[tc % 2] = stx[tc % 2].n

    norm_feat(p, np1, ("dram", x2), ln_apply, 1e-5, xch=xch)
    p.close_scope(sc1)
    sc1b = []
    np1b = norm_pool(p, scope=sc1b)
    xchb = [p.sb("xchb2", [128, 512 * 8], scope=sc1b) for _ in range(2)]
    tmpc = p.sb("tmpc2", [128, 512], scope=sc1b)

    def rms_apply(d, tc, t0, tw, xa, mu_bc, rstd_bc):
        e_stt(p, tmpc[:, 0:tw], xa, gv0[:, d:d + 1], rstd_bc, ALU.mult,
              ALU.mult)
        e_copy(p, xr[d][:, t0:t0 + tw], tmpc[:, 0:tw])

    norm_feat(p, np1b, ("dram", xn_d), rms_apply, 1e-6, mean_center=False,
              xch=xchb)
    p.close_scope(sc1b)
    if copar_start is not None:
        p.barrier("ytvsg")
        copar_start()

    sc2 = []
    cvw = p.sb("cvw2", [128, 4 * 16], scope=sc2)
    cvb = p.sb("cvb2", [128, 16], scope=sc2)
    xpB = p.sb("xpB2", [128, 16 * NST], BF16, scope=sc2)
    xpC = p.sb("xpC2", [128, 16 * NST], BF16, scope=sc2)
    xpd = p.sb("xpd2", [128, 16], BF16, scope=sc2)
    convf = p.sb("convf", [128, T], scope=sc2)
    convo = p.sb("convo", [128, T], scope=sc2)
    cacc = p.sb("cacc2", [128, T], scope=sc2)
    zch = p.sb("zch2", [128, T], BF16, scope=sc2)
    Bps = p.pt("Bps2", [NST, 512], scope=sc2)
    Cps = p.pt("Cps2", [NST, 512], scope=sc2)
    dps = p.pt("dps2", [1, 512], scope=sc2)
    Bsb = p.sb("Bsb2", [NST, T], scope=sc2)
    Csb = p.sb("Csb2", [NST, T], scope=sc2)
    dsb = p.sb("dsb2", [1, T], scope=sc2)
    cl = p.sem("clc")
    for k4 in range(4):
        for dt_ in range(16):
            e_dma(p, "y", cvw[:, k4 * 16 + dt_:k4 * 16 + dt_ + 1],
                  cw[k4, dt_ * 128:(dt_ + 1) * 128].unsqueeze(1), cl)
    for dt_ in range(16):
        e_dma(p, "y", cvb[:, dt_:dt_ + 1],
              cb[dt_ * 128:(dt_ + 1) * 128].unsqueeze(1), cl)
    clg = p.sem("cstg")
    for dt_ in range(16):
        e_dma(p, "g", xpB[:, dt_ * NST:(dt_ + 1) * NST],
              xpBT[dt_ * 128:(dt_ + 1) * 128, :], clg)
        e_dma(p, "g", xpC[:, dt_ * NST:(dt_ + 1) * NST],
              xpCT[dt_ * 128:(dt_ + 1) * 128, :], clg)
        e_dma(p, "g", xpd[:, dt_:dt_ + 1],
              xpdT[dt_ * 128:(dt_ + 1) * 128, :], clg)
    p.wait("v", cl, cl.n)
    p.wait("s", cl, cl.n)
    p.wait("t", clg, clg.n)
    Bacc = [None, None, None]
    stz = p.sem("stz")
    stxs = p.sem("stxs")

    def ev_ip(mb, tc, t0, tw, psum, tmark):
        if mb < 16:
            p.wait("s", tmark[0], tmark[1])
            e_act(p, convf[:, t0:t0 + tw], psum, AF.Identity)
            return p.marker("s")
        p.wait("s", tmark[0], tmark[1])
        e_act(p, zch[:, t0:t0 + tw], psum, AF.Silu)
        sm = p.marker("s")
        if tc == T // 512 - 1:
            p.wait("y", sm[0], sm[1])
            e_dma(p, "y", zs_d[(mb - 16) * 128:(mb - 15) * 128, :], zch[:],
                  stz)
            ym = stz.n
            p.wait("s", stz, ym)
        return sm

    def post_conv(mb):
        dcol = mb
        p.wait("v", stxs, stxs.n)
        e_ts(p, cacc[:, 0:T], convf[:, 0:T],
             cvw[:, 3 * 16 + dcol:3 * 16 + dcol + 1], None, op0=ALU.mult)
        for kk in range(3):
            sh = 3 - kk
            e_stt(p, cacc[:, sh:T], convf[:, 0:T - sh],
                  cvw[:, kk * 16 + dcol:kk * 16 + dcol + 1], cacc[:, sh:T],
                  ALU.mult, ALU.add)
        vm = p.marker("v")
        p.wait("s", vm[0], vm[1])
        e_act(p, convo[:, 0:T], cacc[:, 0:T], AF.Silu,
              bias=cvb[:, dcol:dcol + 1])
        sm = p.marker("s")
        p.wait("v", sm[0], sm[1])
        cb16 = cacc[:, 0:T].bitcast(BF16)
        e_copy(p, cb16[:, 0:T], convo[:, 0:T])
        vm2 = p.marker("v")
        p.wait("y", vm2[0], vm2[1])
        e_dma(p, "y", xs_d[mb * 128:(mb + 1) * 128, :], cb16[:, 0:T], stxs)
        p.wait("t", vm2[0], vm2[1])
        for tc in range(T // 512):
            t0 = tc * 512
            e_mm(p, Bps[:, :], xpB[:, mb * NST:(mb + 1) * NST],
                 cb16[:, t0:t0 + 512])
            e_mm(p, Cps[:, :], xpC[:, mb * NST:(mb + 1) * NST],
                 cb16[:, t0:t0 + 512])
            e_mm(p, dps[:, :], xpd[:, mb:mb + 1], cb16[:, t0:t0 + 512])
            tm = p.marker("t")
            p.wait("v", tm[0], tm[1])
            if mb == 0:
                e_copy(p, Bsb[:, t0:t0 + 512], Bps[:, :])
                e_copy(p, Csb[:, t0:t0 + 512], Cps[:, :])
                e_copy(p, dsb[0:1, t0:t0 + 512], dps[0:1, :])
            else:
                e_tt(p, Bsb[:, t0:t0 + 512], Bsb[:, t0:t0 + 512], Bps[:, :],
                     ALU.add)
                e_tt(p, Csb[:, t0:t0 + 512], Csb[:, t0:t0 + 512], Cps[:, :],
                     ALU.add)
                e_tt(p, dsb[0:1, t0:t0 + 512], dsb[0:1, t0:t0 + 512],
                     dps[0:1, :], ALU.add)
            vmq = p.marker("v")
            p.wait("t", vmq[0], vmq[1])
        cm = p.marker("t")
        p.wait("s", cm[0], cm[1])
        p.wait("v", cm[0], cm[1])

    for mb in range(16):
        ev_done_ip = []

        def ev_ip_x(mbx, tc, t0, tw, psum, tmark, mb=mb):
            p.wait("s", tmark[0], tmark[1])
            e_act(p, convf[:, t0:t0 + tw], psum, AF.Identity)
            return p.marker("s")
        ems = linear_fm(p, lp, ipT[:, mb * 128:(mb + 1) * 128], xr, D, 128,
                        ev_ip_x)
        for m_ in ems:
            if m_ is not None:
                p.wait("v", m_[0], m_[1])
        post_conv(mb)
    for mb in range(16):
        def ev_ip_z(mbx, tc, t0, tw, psum, tmark, mb=mb):
            p.wait("s", tmark[0], tmark[1])
            e_act(p, zch[:, t0:t0 + tw], psum, AF.Silu)
            sm = p.marker("s")
            if t0 + tw >= T:
                p.wait("y", sm[0], sm[1])
                e_dma(p, "y", zs_d[mb * 128:(mb + 1) * 128, :], zch[:], stz)
                p.wait("s", stz, stz.n)
            return sm
        linear_fm(p, lp, ipT[:, DI2 + mb * 128:DI2 + (mb + 1) * 128], xr, D,
                  128, ev_ip_z)
    vb = p.marker("v")
    p.wait("y", vb[0], vb[1])
    sB = p.sem("sB2")
    e_dma(p, "y", Bt_d[:, :], Bsb[:], sB)
    e_dma(p, "y", Ct_d[:, :], Csb[:], sB)
    e_dma(p, "y", dtr_d[:, :], dsb[:], sB)
    p.barrier("ytvsg")
    p.close_scope(sc2)
    p.close_scope(sc_xr)
    if copar_mid is not None:
        copar_mid()

    sc3 = []
    TCH = 1024
    NTCH = T // TCH
    Avec = p.sb("Avec2", [128, 16 * NST], scope=sc3)
    dskv = p.sb("dskv2", [128, 16], scope=sc3)
    dtrep = p.sb("dtrep", [128, TCH], scope=sc3)
    delta = p.sb("del2", [128, TCH], scope=sc3)
    ych = p.sb("ych2", [128, TCH], scope=sc3)
    xs_ch = [p.sb("xsch2", [128, TCH], BF16, scope=sc3) for _ in range(16)]
    zss = p.sb("zsch2", [128, TCH], BF16, scope=sc3)
    Brep = [p.sb("Brep2", [128, TCH], BF16, scope=sc3) for _ in range(NST)]
    Crep = [p.sb("Crep2", [128, TCH], BF16, scope=sc3) for _ in range(NST)]
    dA2 = [p.sb("dA2", [128, TCH], scope=sc3) for _ in range(2)]
    u2 = [p.sb("u2", [128, TCH], BF16, scope=sc3)]
    dxb = p.sb("dxb2", [128, TCH], BF16, scope=sc3)
    hsc = p.sb("hsc2", [128, TCH // 2], scope=sc3)
    hscb = p.sb("hscb2", [128, TCH], BF16, scope=sc3)
    tmp2 = p.sb("tmp22", [128, TCH // 2], scope=sc3)
    carry_all = p.sb("car2", [128, 16 * NST], scope=sc3)
    carry = [carry_all[:, i * NST:(i + 1) * NST] for i in range(16)]
    xnch = [delta[:, 0:512], delta[:, 512:1024]]
    schh = [ych[:, 0:512], ych[:, 512:1024]]
    sdl = p.sem("sdl")
    for dt_ in range(16):
        e_dma(p, "y", Avec[:, dt_ * NST:(dt_ + 1) * NST],
              Alog[dt_ * 128:(dt_ + 1) * 128, :], sdl)
        e_dma(p, "y", dskv[:, dt_:dt_ + 1],
              Dskip[dt_ * 128:(dt_ + 1) * 128].unsqueeze(1), sdl)
    dtwv, _, _ = load_bias_cols(p, dtw, 16, "dtwv", scope=sc3)
    dtbv, ds2, dt2 = load_bias_cols(p, dtb, 16, "dtbv", scope=sc3)
    p.wait("s", sdl, sdl.n)
    p.wait("v", sdl, sdl.n)
    p.wait("v", ds2, dt2)
    e_act(p, Avec[:], Avec[:], AF.Exp)
    smA = p.marker("s")
    p.wait("v", smA[0], smA[1])
    e_ts(p, Avec[:], Avec[:], -1.0, None, op0=ALU.mult)
    vmA = p.marker("v")
    p.wait("s", vmA[0], vmA[1])
    for dt_ in range(16):
        e_memset(p, carry[dt_][:], 0.0)
    p.barrier("ytvsg")
    dA_rd = [None, None]
    brs = [p.sem("brsa"), p.sem("brsb")]
    xss = [p.sem("xssa"), p.sem("xssb")]
    zsm = p.sem("zsm")
    drs = [p.sem("drsa"), p.sem("drsb")]
    yzm = [None]
    pchunk = {}
    for tch in range(NTCH):
        c0 = tch * TCH
        if pchunk:
            p.wait("g", pchunk["vsc"][0], pchunk["vsc"][1])
            p.wait("y", pchunk["tend"][0], pchunk["tend"][1])
            p.wait("y", pchunk["vsc"][0], pchunk["vsc"][1])
        for n in range(NST):
            e_dma(p, "g", Brep[n][:],
                  Bt_d[n:n + 1, c0:c0 + TCH].partition_broadcast(128)
                  .squeeze(1), brs[tch % 2])
        for n in range(NST):
            _, brt = e_dma(p, "g", Crep[n][:],
                           Ct_d[n:n + 1, c0:c0 + TCH].partition_broadcast(128)
                           .squeeze(1), brs[tch % 2])
        for dt_ in range(16):
            _, xst = e_dma(p, "y", xs_ch[dt_][:],
                           xs_d[dt_ * 128:(dt_ + 1) * 128, c0:c0 + TCH],
                           xss[tch % 2])
        _, drt = e_dma(p, "y", dtrep[:, 0:TCH],
                       dtr_d[0:1, c0:c0 + TCH].partition_broadcast(128)
                       .squeeze(1), drs[tch % 2])
        p.wait("v", xss[tch % 2], xst)
        p.wait("s", xss[tch % 2], xst)
        p.wait("v", brs[tch % 2], brt)
        p.wait("v", drs[tch % 2], drt)
        p.wait("s", drs[tch % 2], drt)
        if pchunk:
            for uu, (sem_, tgt_) in pchunk["h1"].items():
                p.wait("v", sem_, tgt_)
                p.wait("s", sem_, tgt_)
            p.wait("s", pchunk["vend"][0], pchunk["vend"][1])
            p.wait("v", pchunk["send"][0], pchunk["send"][1])
        zst = {}
        for dt_ in range(16):
            for hh in range(2):
                h0 = hh * (TCH // 2)
                e_ts(p, tmp2[:, 0:TCH // 2], dtrep[:, h0:h0 + TCH // 2],
                     dtwv[:, dt_:dt_ + 1], dtbv[:, dt_:dt_ + 1],
                     op0=ALU.mult, op1=ALU.add)
                vmd = p.marker("v")
                p.wait("s", vmd[0], vmd[1])
                e_softplus(p, delta, tmp2, hsc, n0=h0, nw=TCH // 2,
                           tmp0=True, in0=True)
                smh = p.marker("s")
                p.wait("v", smh[0], smh[1])
            spm = p.marker("s")
            p.wait("v", spm[0], spm[1])
            e_tt(p, dxb[:], delta[:], xs_ch[dt_][:], ALU.mult)
            vdx = p.marker("v")
            if yzm[0] is not None:
                p.wait("y", yzm[0][0], yzm[0][1])
            _, zst[dt_] = e_dma(p, "y", zss[:],
                                zs_d[dt_ * 128:(dt_ + 1) * 128, c0:c0 + TCH],
                                zsm)
            for n in range(NST):
                bA = n % 2
                p.wait("s", vdx[0], vdx[1])
                if dA_rd[bA] is not None:
                    p.wait("s", dA_rd[bA][0], dA_rd[bA][1])
                e_act(p, dA2[bA][:], delta[:], AF.Exp,
                      scale=Avec[:, dt_ * NST + n:dt_ * NST + n + 1])
                sm = p.marker("s")
                e_tt(p, u2[0][:], dxb[:], Brep[n][:], ALU.mult)
                p.wait("v", sm[0], sm[1])
                p.op("v", lambda eng, dt_=dt_, n=n, bA=bA:
                     eng.tensor_tensor_scan(
                         out=hscb[:], data0=dA2[bA][:], data1=u2[0][:],
                         initial=carry[dt_][:, n:n + 1], op0=ALU.mult,
                         op1=ALU.add))
                e_copy(p, carry[dt_][:, n:n + 1], hscb[:, TCH - 1:TCH])
                if n == 0:
                    e_tt(p, ych[:], hscb[:], Crep[n][:], ALU.mult)
                else:
                    e_tt(p, hscb[:], hscb[:], Crep[n][:], ALU.mult)
                    e_tt(p, ych[:], ych[:], hscb[:], ALU.add)
                dA_rd[bA] = p.marker("v")
                if copar is not None:
                    copar()
            e_stt(p, ych[:], xs_ch[dt_][:], dskv[:, dt_:dt_ + 1], ych[:],
                  ALU.mult, ALU.add)
            p.wait("v", zsm, zst[dt_])
            e_tt(p, xs_ch[dt_][:], ych[:], zss[:], ALU.mult)
            yzm[0] = p.marker("v")
        vsc = p.marker("v")
        ssc = p.marker("s")
        p.wait("t", vsc[0], vsc[1])
        h1st = [p.sem("h1sa"), p.sem("h1sb")]
        xnld = [p.sem("xla"), p.sem("xlb")]
        xt_l = {}
        evd = {}

        def ev_op(mb, tc, t0, tw, psum, tmark, c0=c0, h1st=h1st, xnld=xnld,
                  xt_l=xt_l, evd=evd):
            u = (mb * (TCH // 512) + tc) % 2
            uk = mb * (TCH // 512) + tc
            if uk == 0:
                p.wait("y", vsc[0], vsc[1])
                p.wait("y", ssc[0], ssc[1])
                for k2 in (0, 1):
                    _, xt_l[k2] = e_dma(p, "y", xnch[k2][:],
                                        xn_d[0:128, c0 + k2 * 512:
                                             c0 + (k2 + 1) * 512], xnld[k2])
            p.wait("s", tmark[0], tmark[1])
            p.wait("s", xnld[u], xt_l[uk])
            if uk >= 2 and (uk - 2) in evd:
                p.wait("s", h1st[u], evd[uk - 2])
            e_act(p, schh[u][:, 0:tw], psum, AF.Identity)
            sm = p.marker("s")
            p.wait("v", sm[0], sm[1])
            e_tt(p, schh[u][:, 0:tw], schh[u][:, 0:tw], xnch[u][:, 0:tw],
                 ALU.add)
            vm = p.marker("v")
            if uk + 2 <= 2 * (TCH // 512) * 8 // 2 - 1 and uk + 2 < 8 * (TCH // 512):
                nk_ = uk + 2
                mb2, tc2 = nk_ // (TCH // 512), nk_ % (TCH // 512)
                p.wait("y", vm[0], vm[1])
                _, xt_l[nk_] = e_dma(p, "y", xnch[u][:],
                                     xn_d[mb2 * 128:(mb2 + 1) * 128,
                                          c0 + tc2 * 512:c0 + tc2 * 512 + 512],
                                     xnld[u])
            p.wait("y", vm[0], vm[1])
            _, st_ = e_dma(p, "y", h1_d[mb * 128:(mb + 1) * 128,
                                        c0 + t0:c0 + t0 + tw],
                           schh[u][:, 0:tw], h1st[u])
            evd[uk] = st_
            return vm

        linear_fm(p, lp, opT, xs_ch, DI2, D, ev_op, n_free=TCH)
        nuk = 8 * (TCH // 512)
        pchunk = {
            "vsc": vsc, "tend": p.marker("t"), "vend": p.marker("v"),
            "send": p.marker("s"),
            "h1": {u_: (h1st[u_], evd[nuk - 2 + u_]) for u_ in (0, 1)},
        }
    p.barrier("ytvsg")
    p.close_scope(sc3)
    if copar_end is not None:
        copar_end()
    p.barrier("ytvsg")

    gv1, gs1, gt1 = load_bias_cols(p, naw, 8, "gna")
    p.wait("v", gs1, gt1)
    xa = [p.sb("xa2", [128, T], BF16) for _ in range(8)]
    sc4 = []
    np4 = norm_pool(p, scope=sc4)
    xch4 = [p.sb("xch42", [128, 512 * 8], scope=sc4) for _ in range(2)]
    tmp4 = p.sb("tmp42", [128, 512], scope=sc4)

    def rms_a(d, tc, t0, tw, xav, mu_bc, rstd_bc):
        e_stt(p, tmp4[:, 0:tw], xav, gv1[:, d:d + 1], rstd_bc, ALU.mult,
              ALU.mult)
        e_copy(p, xa[d][:, t0:t0 + tw], tmp4[:, 0:tw])

    norm_feat(p, np4, ("dram", h1_d), rms_a, 1e-6, mean_center=False,
              xch=xch4)
    p.close_scope(sc4)

    sc5 = []
    kt5 = [p.sb("kt5", [128, T], BF16, scope=sc5) for _ in range(4)]
    vt5 = p.sb("vt5", [128, (T // 128) * 516], BF16, scope=sc5)
    qh5 = [p.sb("qh5", [128, T], BF16, scope=sc5) for _ in range(2)]
    ah5 = [p.sb("ah5", [128, T], BF16, scope=sc5) for _ in range(2)]
    wv5 = [p.sb("wv5", [128, 512], BF16, scope=sc5) for _ in range(8)]
    tri5 = [p.sb("tri5", [128, 512], BF16, scope=sc5) for _ in range(4)]
    o_scr5 = p.sb("oscr5", [128, 1024], scope=sc5)
    osum5 = p.sb("osum5", [1, 512], scope=sc5)
    rec5 = p.sb("rec5", [1, 512], scope=sc5)
    onesq5 = p.sb("onsq5", [1, 128], scope=sc5)
    ones_t5 = p.sb("onet5", [1, T], BF16, scope=sc5)
    state5 = dict(
        sc_ps=[p.pt("scp5", [128, 512], scope=sc5) for _ in range(2)],
        pbuf=[p.sb("pb5", [128, 512], BF16, scope=sc5) for _ in range(2)],
        ov_ps=[[p.pt("ovp5a", [128, 512], scope=sc5)],
               [p.pt("ovp5b", [128, 512], scope=sc5)],
               [p.pt("ovp5c", [1, 512], scope=sc5)]],
    )
    e_memset(p, onesq5[:], 1.0)
    e_memset(p, ones_t5[:], 1.0)
    t5s = p.sem("cstg")
    for j in range(4):
        e_dma(p, "g", tri5[j][:], trid[j], t5s)
    ev_k5 = evac_to_tiles(p, kt5, AF.Identity)
    linear_fm(p, lp, kwT, xa, D, 512, ev_k5)
    wsem5 = p.sem("wvs")
    for k in range(8):
        e_dma(p, "g", wv5[k][:], vwT[k * 128:(k + 1) * 128, :], wsem5)
    p.wait("t", wsem5, wsem5.n)
    p.wait("t", t5s, t5s.n)
    p.wait("v", t5s, t5s.n)
    for tt in range(T // 128):
        for kv in range(2):
            e_memset(p, vt5[:, tt * 516 + kv * 258 + 256:
                           tt * 516 + kv * 258 + 258], 1.0)
    vm5 = p.marker("v")
    p.wait("s", vm5[0], vm5[1])

    def ev_v5(tt, psum, tmark):
        p.wait("s", tmark[0], tmark[1])
        for kv in range(2):
            e_act(p, vt5[:, tt * 516 + kv * 258:tt * 516 + kv * 258 + 256],
                  psum[:, kv * 256:kv * 256 + 256], AF.Identity)
        return p.marker("s")
    linear_tm(p, lp, wv5, xa, D, 512, ev_v5)
    p.barrier("ytvsg")

    ao5st = p.sem("aosx")
    for h in range(4):
        kv = h // 2
        ev_q5 = evac_to_tiles(p, qh5, AF.Identity)
        linear_fm(p, lp, qwT[:, 256 * h:256 * h + 256], xa, D, 256, ev_q5)
        p.barrier("ytvsg")

        def causal_sbs(tg, h=h):
            return list(range((tg + 1) * 4))

        def mask_cb(sb_, tg, pb, em, h=h):
            base = tg * 4
            if sb_ < base:
                return None
            p.wait("v", em[0], em[1])
            e_tt(p, pb[:, 0:512], pb[:, 0:512], tri5[sb_ - base][:], ALU.mult)
            return p.marker("v")

        def o_cb5(tg, g0, gw, h=h):
            ov = state5["ov_ps"]
            e_act(p, o_scr5[:, 0:512], ov[0][0][:, :], AF.Identity)
            e_act(p, o_scr5[:, 512:1024], ov[1][0][:, :], AF.Identity)
            e_act(p, osum5[:, 0:512], ov[2][0][0:1, :], AF.Identity)
            sm = p.marker("s")
            p.wait("v", sm[0], sm[1])
            p.op("v", lambda eng: eng.reciprocal(out=rec5[:, 0:512],
                                                 in_=osum5[:, 0:512]))
            vm = p.marker("v")
            p.wait("t", vm[0], vm[1])
            for half in range(2):
                e_mm(p, ov[half][0][:, :], onesq5[0:1, 0:128], rec5[:, 0:512])
            tb = p.marker("t")
            p.wait("v", tb[0], tb[1])
            for half in range(2):
                e_tt(p, ah5[half][:, g0:g0 + 512],
                     o_scr5[:, half * 512:(half + 1) * 512],
                     ov[half][0][:, :], ALU.mult)
            vo = p.marker("v")
            p.wait("t", vo[0], vo[1])
            p.wait("s", vo[0], vo[1])

        attn_head(
            p,
            sc_chunks=[(lambda s0, kv=kv: kt5[2 * kv][:, s0:s0 + 128],
                        lambda g0, gw: qh5[0][:, g0:g0 + gw]),
                       (lambda s0, kv=kv: kt5[2 * kv + 1][:, s0:s0 + 128],
                        lambda g0, gw: qh5[1][:, g0:g0 + gw])],
            get_vchunks=lambda sb_, kv=kv: [
                (vt5[:, sb_ * 516 + kv * 258:sb_ * 516 + kv * 258 + 128], 0, 128),
                (vt5[:, sb_ * 516 + kv * 258 + 128:sb_ * 516 + kv * 258 + 256], 1, 128),
                (vt5[:, sb_ * 516 + kv * 258 + 256:sb_ * 516 + kv * 258 + 257], 2, 1)],
            o_cb=o_cb5, state=state5, tgsz=512, causal_sbs=causal_sbs,
            mask_cb=mask_cb, exp_scale=0.0625)
        hm = p.marker("v")
        p.wait("y", hm[0], hm[1])
        for half in range(2):
            e_dma(p, "y", ao_d[(2 * h + half) * 128:(2 * h + half + 1) * 128, :],
                  ah5[half][:], ao5st)
        p.wait("v", ao5st, ao5st.n)
        p.wait("s", ao5st, ao5st.n)
        p.wait("t", ao5st, ao5st.n)
    p.barrier("ytvsg")
    p.close_scope(sc5)

    sc6 = []
    obt6 = [p.sb("obt6", [128, T], BF16, scope=sc6) for _ in range(8)]
    xrs6 = [p.sb("xr6", [128, 512], scope=sc6) for _ in range(2)]
    sch6 = [p.sb("sc6", [128, 512], scope=sc6) for _ in range(2)]
    ob6 = p.sem("obl")
    for k in range(8):
        e_dma(p, "y", obt6[k][:], ao_d[k * 128:(k + 1) * 128, :], ob6)
    p.wait("t", ob6, ob6.n)
    h2st = [p.sem("osta"), p.sem("ostb")]
    xnl6 = [p.sem("xla"), p.sem("xlb")]
    xt6 = {}
    evd6 = {}
    NTC = T // 512

    def ev_h2(mb, tc, t0, tw, psum, tmark):
        u = (mb * NTC + tc) % 2
        uk = mb * NTC + tc
        if uk == 0:
            for k2 in (0, 1):
                _, xt6[k2] = e_dma(p, "y", xrs6[k2 % 2][:],
                                   h1_d[0:128, k2 * 512:(k2 + 1) * 512],
                                   xnl6[k2 % 2])
        p.wait("s", tmark[0], tmark[1])
        p.wait("s", xnl6[u], xt6[uk])
        if uk >= 2 and (uk - 2) in evd6:
            p.wait("s", h2st[u], evd6[uk - 2])
        e_act(p, sch6[u][:, 0:tw], psum, AF.Identity)
        sm = p.marker("s")
        p.wait("v", sm[0], sm[1])
        e_tt(p, sch6[u][:, 0:tw], sch6[u][:, 0:tw], xrs6[u][:, 0:tw], ALU.add)
        vm = p.marker("v")
        if uk + 2 <= 8 * NTC - 1:
            nk_ = uk + 2
            mb2, tc2 = nk_ // NTC, nk_ % NTC
            p.wait("y", vm[0], vm[1])
            _, xt6[nk_] = e_dma(p, "y", xrs6[u][:],
                                h1_d[mb2 * 128:(mb2 + 1) * 128,
                                     tc2 * 512:tc2 * 512 + 512], xnl6[u])
        p.wait("y", vm[0], vm[1])
        _, st_ = e_dma(p, "y", h2_d[mb * 128:(mb + 1) * 128, t0:t0 + tw],
                       sch6[u][:, 0:tw], h2st[u])
        evd6[uk] = st_
        return vm

    linear_fm(p, lp, owT, obt6, D, D, ev_h2)
    p.barrier("ytvsg")
    p.close_scope(sc6)

    gv2, gs2, gt2 = load_bias_cols(p, nfw, 8, "gnf")
    p.wait("v", gs2, gt2)
    xf = xa
    sc7 = []
    np7 = norm_pool(p, scope=sc7)
    xch7 = [p.sb("xch72", [128, 512 * 8], scope=sc7) for _ in range(2)]
    tmp7 = p.sb("tmp72", [128, 512], scope=sc7)

    def rms_f(d, tc, t0, tw, xav, mu_bc, rstd_bc):
        e_stt(p, tmp7[:, 0:tw], xav, gv2[:, d:d + 1], rstd_bc, ALU.mult,
              ALU.mult)
        e_copy(p, xf[d][:, t0:t0 + tw], tmp7[:, 0:tw])

    norm_feat(p, np7, ("dram", h2_d), rms_f, 1e-6, mean_center=False,
              xch=xch7)
    p.close_scope(sc7)

    sc8 = []
    lpF = lin_pool(p, 16, scope=sc8, np_ps=4)
    hb8 = [p.sb("hb8", [128, T], BF16, scope=sc8) for _ in range(8)]
    asb8 = [p.sb("asb8", [128, 512], scope=sc8) for _ in range(2)]
    wB8 = [p.sb("wB8", [128, 8 * 128], BF16, scope=sc8) for _ in range(2)]
    ob8 = [p.sb("ob8", [128, 512], scope=sc8) for _ in range(2)]
    w13s8 = [p.sem("w13a"), p.sem("w13b")]
    accs8 = [p.sem("acca"), p.sem("accb")]
    acc_t8 = [None, None]
    first8 = [True]
    wA8 = lpF["wbuf"]
    KGS = [(0, 8), (8, 8), (16, 6)]
    for (kg0, kgn) in KGS:
        wt8 = {}

        def load8(fbl, gate):
            fb = kg0 + fbl
            f0 = fb * 128
            fw = min(128, FFH - f0)
            if gate is not None:
                p.wait("g", gate[0], gate[1])
            _, t1 = e_dma(p, "g", wA8[fbl % 2][:, 0:8 * fw].rearrange(
                "p (kc m) -> p kc m", kc=8),
                gwT[:, f0:f0 + fw].rearrange("(kc p) m -> p kc m", p=128),
                w13s8[fbl % 2])
            _, t2 = e_dma(p, "g", wB8[fbl % 2][:, 0:8 * fw].rearrange(
                "p (kc m) -> p kc m", kc=8),
                uwT[:, f0:f0 + fw].rearrange("(kc p) m -> p kc m", p=128),
                w13s8[fbl % 2])
            wt8[fbl] = t2

        load8(0, None)
        if kgn > 1:
            load8(1, None)
        for fbl in range(kgn):
            fb = kg0 + fbl
            f0 = fb * 128
            fw = min(128, FFH - f0)
            p.wait("t", w13s8[fbl % 2], wt8[fbl])
            for tc in range(T // 512):
                t0 = tc * 512
                sl = 2 * (tc % 2)
                pp, pq = lpF["ps"][sl], lpF["ps"][sl + 1]
                if lpF["prev"][sl] is not None:
                    p.wait("t", lpF["prev"][sl][0], lpF["prev"][sl][1])
                for k in range(8):
                    e_mm(p, pp[0:fw, 0:512],
                         wA8[fbl % 2][0:128, k * fw:(k + 1) * fw],
                         xf[k][:, t0:t0 + 512], start=(k == 0), stop=(k == 7))
                tma = p.marker("t")
                if lpF["prev"][sl + 1] is not None:
                    p.wait("t", lpF["prev"][sl + 1][0],
                           lpF["prev"][sl + 1][1])
                for k in range(8):
                    e_mm(p, pq[0:fw, 0:512],
                         wB8[fbl % 2][0:128, k * fw:(k + 1) * fw],
                         xf[k][:, t0:t0 + 512], start=(k == 0), stop=(k == 7))
                tmb = p.marker("t")
                p.wait("s", tma[0], tma[1])
                prevv = lpF["prev"][sl + 1]
                if prevv is not None:
                    p.wait("s", prevv[0], prevv[1])
                e_act(p, asb8[tc % 2][0:fw, 0:512], pp[0:fw, 0:512], AF.Silu)
                sm_ = p.marker("s")
                lpF["prev"][sl] = sm_
                p.wait("v", tmb[0], tmb[1])
                p.wait("v", sm_[0], sm_[1])
                e_tt(p, hb8[fbl][0:fw, t0:t0 + 512], asb8[tc % 2][0:fw, 0:512],
                     pq[0:fw, 0:512], ALU.mult)
                lpF["prev"][sl + 1] = p.marker("v")
            if fbl + 2 < kgn:
                gm = p.marker("t")
                p.wait("g", gm[0], gm[1])
                load8(fbl + 2, gm)
        hvm = p.marker("v")
        p.wait("g", hvm[0], hvm[1])
        w2t8 = {}
        krows = min(kgn * 128, FFH - kg0 * 128)
        nk8 = (krows + 127) // 128

        def loadw2(mb, gate):
            if gate is not None:
                p.wait("g", gate[0], gate[1])
            sn = None
            for k in range(nk8):
                k0 = kg0 * 128 + k * 128
                kw = min(128, FFH - k0)
                _, sn = e_dma(p, "g", wB8[mb % 2][0:kw, k * 128:(k + 1) * 128],
                              dwT[k0:k0 + kw, mb * 128:(mb + 1) * 128],
                              w13s8[mb % 2])
            w2t8[mb] = sn

        loadw2(0, None)
        loadw2(1, None)
        for mb in range(8):
            p.wait("t", w13s8[mb % 2], w2t8[mb])
            for tc in range(T // 512):
                t0 = tc * 512
                uu = mb * (T // 512) + tc
                u = uu % 2
                u4 = uu % 4
                pp = lpF["ps"][u4]
                if lpF["prev"][u4] is not None:
                    p.wait("t", lpF["prev"][u4][0], lpF["prev"][u4][1])
                for k in range(nk8):
                    kw = min(128, FFH - (kg0 * 128 + k * 128))
                    e_mm(p, pp[0:128, 0:512],
                         wB8[mb % 2][0:kw, k * 128:(k + 1) * 128],
                         hb8[k][0:kw, t0:t0 + 512], start=(k == 0),
                         stop=(k == nk8 - 1))
                tm = p.marker("t")
                p.wait("s", tm[0], tm[1])
                if acc_t8[u] is not None:
                    p.wait("s", accs8[u], acc_t8[u])
                e_act(p, ob8[u][:, 0:512], pp[0:128, 0:512], AF.Identity)
                sm = p.marker("s")
                lpF["prev"][u4] = sm
                p.wait("g", sm[0], sm[1])
                if first8[0]:
                    p.op("g", lambda eng, mb=mb, t0=t0, u=u: eng.dma_start(
                        out=ffa_d[mb * 128:(mb + 1) * 128, t0:t0 + 512],
                        in_=ob8[u][:, 0:512]), accs8[u], 16, kind="d")
                else:
                    p.op("g", lambda eng, mb=mb, t0=t0, u=u: eng.dma_start(
                        out=ffa_d[mb * 128:(mb + 1) * 128, t0:t0 + 512],
                        in_=ob8[u][:, 0:512], accum_op=ALU.add), accs8[u], 16,
                        kind="d")
                p.dma_sems.add(accs8[u])
                acc_t8[u] = accs8[u].n
            if mb + 2 < 8:
                gm2 = p.marker("t")
                loadw2(mb + 2, gm2)
        first8[0] = False
    p.barrier("ytvsg")
    p.close_scope(sc8)

    fsem = p.sem("fin")
    fst2 = [p.sem("osta"), p.sem("ostb")]
    sch = [p.sb("fcL2", [128, 1024]) for _ in range(2)]
    hch = [p.sb("fhL2", [128, 1024]) for _ in range(2)]
    xch9 = [p.sb("fxL2", [128, 1024]) for _ in range(2)]
    fl = {}
    fs_t = {}
    nhalf = T // 1024
    for dd in range(8):
        for half in range(nhalf):
            u = (dd * nhalf + half) % 2
            c0 = half * 1024
            key = dd * nhalf + half
            if key >= 2:
                pk = key - 2
                p.wait("y", fl[pk][0], fl[pk][1])
                p.wait("y", fst2[u], fs_t[pk])
            e_dma(p, "y", sch[u][:], x2[dd * 128:(dd + 1) * 128,
                                        c0:c0 + 1024], fsem)
            e_dma(p, "y", xch9[u][:], h2_d[dd * 128:(dd + 1) * 128,
                                           c0:c0 + 1024], fsem)
            _, t2 = e_dma(p, "y", hch[u][:], ffa_d[dd * 128:(dd + 1) * 128,
                                                   c0:c0 + 1024], fsem)
            p.wait("v", fsem, t2)
            e_tt(p, sch[u][:], sch[u][:], xch9[u][:], ALU.add)
            e_tt(p, sch[u][:], sch[u][:], hch[u][:], ALU.add)
            vm = p.marker("v")
            p.wait("y", vm[0], vm[1])
            _, ft = e_dma(p, "y", x_out[dd * 128:(dd + 1) * 128, c0:c0 + 1024],
                          sch[u][:], fst2[u])
            fl[key] = vm
            fs_t[key] = ft
    if own:
        final_wait_all(p)
        return p.finalize()


def build_L0a(ctx=None):
    DI0 = 2048
    NH0 = 256
    NST0 = 16
    own = ctx is None
    if own:
        nc = bass.Bass("TRN2", target_bir_lowering=False)
        p = P(nc)
        io = None
    else:
        nc, p, io = ctx
    x0 = _dram(nc, io, "x", [D, T], F32, "ExternalInput")
    lnw = _dram(nc, io, "ln_w", [D], F32, "ExternalInput")
    lnb = _dram(nc, io, "ln_b", [D], F32, "ExternalInput")
    WinZ = _dram(nc, io, "WinZ", [D, DI0], F32, "ExternalInput")
    WinX = _dram(nc, io, "WinX", [D, DI0], F32, "ExternalInput")
    WinB = _dram(nc, io, "WinB", [D, NST0], F32, "ExternalInput")
    WinC = _dram(nc, io, "WinC", [D, NST0], F32, "ExternalInput")
    WinD = _dram(nc, io, "WinD", [D, NH0], F32, "ExternalInput")
    cwx = _dram(nc, io, "cwx", [4, DI0], F32, "ExternalInput")
    cwB = _dram(nc, io, "cwB", [4, NST0], F32, "ExternalInput")
    cwC = _dram(nc, io, "cwC", [4, NST0], F32, "ExternalInput")
    cbx = _dram(nc, io, "cbx", [DI0], F32, "ExternalInput")
    cbB = _dram(nc, io, "cbB", [NST0], F32, "ExternalInput")
    cbC = _dram(nc, io, "cbC", [NST0], F32, "ExternalInput")
    Alog = _dram(nc, io, "A_log", [NH0], F32, "ExternalInput")
    Dprep = _dram(nc, io, "Dp_rep", [DI0], F32, "ExternalInput")
    dtbias = _dram(nc, io, "dt_bias", [NH0], F32, "ExternalInput")
    rmsw = _dram(nc, io, "rms_w", [DI0], F32, "ExternalInput")
    WoutT = _dram(nc, io, "W_out", [DI0, D], F32, "ExternalInput")
    xn_out = _dram(nc, io, "xn", [D, T], F32, "ExternalOutput")
    s_out = _dram(nc, io, "s", [D, T], F32, "ExternalOutput")
    zs_d = _dram(nc, io, "zscr0", [DI0, T], BF16, "Internal")
    xs_d = _dram(nc, io, "xsscr0", [DI0, T], BF16, "Internal")
    Bt_d = _dram(nc, io, "Btscr0", [NST0, T], F32, "Internal")
    Ct_d = _dram(nc, io, "Ctscr0", [NST0, T], F32, "Internal")
    dA_d = _dram(nc, io, "dAscr0", [NH0, T], F32, "Internal")
    dt_d = _dram(nc, io, "dtscr0", [NH0, T], F32, "Internal")

    gv, _, _ = load_bias_cols(p, lnw, 8, "g0v")
    bv, bs0, bt0 = load_bias_cols(p, lnb, 8, "b0v")
    p.wait("v", bs0, bt0)
    lp = lin_pool(p, 16, np_ps=4)

    sc_x = []
    xn2 = [p.sb("xn20", [128, T], BF16, scope=sc_x) for _ in range(8)]
    sc1 = []
    np1 = norm_pool(p, scope=sc1)
    xch = [p.sb("xch0", [128, 512 * 8], scope=sc1) for _ in range(2)]
    xnf_ch = [p.sb("xnf0", [128, 512 * 8], scope=sc1) for _ in range(2)]
    stx = [p.sem("sxa"), p.sem("sxb")]
    st_marks = [None, None]

    def ln_apply(d, tc, t0, tw, xa, mu_bc, rstd_bc):
        ch = xnf_ch[tc % 2]
        if d == 0 and st_marks[tc % 2] is not None:
            p.wait("v", stx[tc % 2], st_marks[tc % 2])
        dst = ch[:, d * 512:d * 512 + tw]
        e_tt(p, dst, xa, mu_bc, ALU.subtract)
        e_tt(p, dst, dst, rstd_bc, ALU.mult)
        if d == 7:
            vm = p.marker("v")
            p.wait("y", vm[0], vm[1])
            for dd in range(8):
                e_dma(p, "y", xn_out[dd * 128:(dd + 1) * 128, t0:t0 + tw],
                      ch[:, dd * 512:dd * 512 + tw], stx[tc % 2])
            st_marks[tc % 2] = stx[tc % 2].n

    norm_feat(p, np1, ("dram", x0), ln_apply, 1e-5, xch=xch)
    p.close_scope(sc1)
    sc1b = []
    np1b = norm_pool(p, scope=sc1b)
    xchb = [p.sb("xchb0", [128, 512 * 8], scope=sc1b) for _ in range(2)]
    tmpb = p.sb("tmpb0", [128, 512], scope=sc1b)

    def ln2_apply(d, tc, t0, tw, xa, mu_bc, rstd_bc):
        e_tt(p, tmpb[:, 0:tw], xa, mu_bc, ALU.subtract)
        e_tt(p, tmpb[:, 0:tw], tmpb[:, 0:tw], rstd_bc, ALU.mult)
        e_ts(p, tmpb[:, 0:tw], tmpb[:, 0:tw], gv[:, d:d + 1], bv[:, d:d + 1],
             op0=ALU.mult, op1=ALU.add)
        e_copy(p, xn2[d][:, t0:t0 + tw], tmpb[:, 0:tw])

    norm_feat(p, np1b, ("dram", xn_out), ln2_apply, 1e-5, xch=xchb)
    p.close_scope(sc1b)

    sc2 = []
    cvwx = p.sb("cvwx0", [128, 4 * 16], scope=sc2)
    cvbx = p.sb("cvbx0", [128, 16], scope=sc2)
    convf = p.sb("convf0", [128, T], scope=sc2)
    convo = p.sb("convo0", [128, T], scope=sc2)
    cacc = p.sb("cacc0", [128, T], scope=sc2)
    zch = p.sb("zch0", [128, T], BF16, scope=sc2)
    BCw = p.sb("BCw0", [16, 10], scope=sc2)
    Bpre = p.sb("Bpre0", [16, T], scope=sc2)
    Bacc = p.sb("Bacc0", [16, T], scope=sc2)
    dcb = [p.sb("dcb0", [128, 512], scope=sc2) for _ in range(3)]
    Avc = p.sb("Avc0", [128, 2], scope=sc2)
    dbv = p.sb("dbv0", [128, 2], scope=sc2)
    cl = p.sem("clc")
    for k4 in range(4):
        for dt_ in range(16):
            e_dma(p, "y", cvwx[:, k4 * 16 + dt_:k4 * 16 + dt_ + 1],
                  cwx[k4, dt_ * 128:(dt_ + 1) * 128].unsqueeze(1), cl)
    for dt_ in range(16):
        e_dma(p, "y", cvbx[:, dt_:dt_ + 1],
              cbx[dt_ * 128:(dt_ + 1) * 128].unsqueeze(1), cl)
    for k4 in range(4):
        e_dma(p, "y", BCw[:, k4:k4 + 1], cwB[k4].unsqueeze(1), cl)
        e_dma(p, "y", BCw[:, 4 + k4:5 + k4], cwC[k4].unsqueeze(1), cl)
    e_dma(p, "y", BCw[:, 8:9], cbB.unsqueeze(1), cl)
    for j in range(2):
        e_dma(p, "y", Avc[:, j:j + 1],
              Alog[j * 128:(j + 1) * 128].unsqueeze(1), cl)
        e_dma(p, "y", dbv[:, j:j + 1],
              dtbias[j * 128:(j + 1) * 128].unsqueeze(1), cl)
    p.wait("v", cl, cl.n)
    p.wait("s", cl, cl.n)
    e_act(p, Avc[:], Avc[:], AF.Exp)
    smA = p.marker("s")
    p.wait("v", smA[0], smA[1])
    e_ts(p, Avc[:], Avc[:], -1.0, None, op0=ALU.mult)
    vmA = p.marker("v")
    p.wait("s", vmA[0], vmA[1])

    stz = p.sem("stz")
    stxs = p.sem("stxs")

    for mb in range(16):
        def ev_z(mbx, tc, t0, tw, psum, tmark, mb=mb):
            p.wait("s", tmark[0], tmark[1])
            e_act(p, zch[:, t0:t0 + tw], psum, AF.Silu)
            sm = p.marker("s")
            if t0 + tw >= T:
                p.wait("y", sm[0], sm[1])
                e_dma(p, "y", zs_d[mb * 128:(mb + 1) * 128, :], zch[:], stz)
                p.wait("s", stz, stz.n)
            return sm
        linear_fm(p, lp, WinZ[:, mb * 128:(mb + 1) * 128], xn2, D, 128, ev_z)

    for mb in range(16):
        def ev_x(mbx, tc, t0, tw, psum, tmark):
            p.wait("s", tmark[0], tmark[1])
            e_act(p, convf[:, t0:t0 + tw], psum, AF.Identity)
            return p.marker("s")
        ems = linear_fm(p, lp, WinX[:, mb * 128:(mb + 1) * 128], xn2, D, 128,
                        ev_x)
        for m_ in ems:
            if m_ is not None:
                p.wait("v", m_[0], m_[1])
        dcol = mb
        p.wait("v", stxs, stxs.n)
        e_ts(p, cacc[:, 0:T], convf[:, 0:T],
             cvwx[:, 3 * 16 + dcol:3 * 16 + dcol + 1], None, op0=ALU.mult)
        for kk in range(3):
            sh = 3 - kk
            e_stt(p, cacc[:, sh:T], convf[:, 0:T - sh],
                  cvwx[:, kk * 16 + dcol:kk * 16 + dcol + 1], cacc[:, sh:T],
                  ALU.mult, ALU.add)
        vm = p.marker("v")
        p.wait("s", vm[0], vm[1])
        e_act(p, convo[:, 0:T], cacc[:, 0:T], AF.Silu,
              bias=cvbx[:, dcol:dcol + 1])
        sm = p.marker("s")
        p.wait("v", sm[0], sm[1])
        cb16 = cacc[:, 0:T].bitcast(BF16)
        e_copy(p, cb16[:, 0:T], convo[:, 0:T])
        vm2 = p.marker("v")
        p.wait("y", vm2[0], vm2[1])
        e_dma(p, "y", xs_d[mb * 128:(mb + 1) * 128, :], cb16[:, 0:T], stxs)

    for (Wsrc, cOff, cbcol, dst) in ((WinB, 0, 8, Bt_d), (WinC, 4, 9, Ct_d)):
        if cbcol == 9:
            e_dma(p, "y", BCw[:, 9:10], cbC.unsqueeze(1), cl)
            p.wait("s", cl, cl.n)
            p.wait("v", cl, cl.n)

        def ev_bc(mbx, tc, t0, tw, psum, tmark):
            p.wait("s", tmark[0], tmark[1])
            e_act(p, Bpre[:, t0:t0 + tw], psum[0:16, 0:tw], AF.Identity)
            return p.marker("s")
        ems = linear_fm(p, lp, Wsrc, xn2, D, NST0, ev_bc)
        for m_ in ems:
            if m_ is not None:
                p.wait("v", m_[0], m_[1])
        e_ts(p, Bacc[:, 0:T], Bpre[:, 0:T], BCw[0:16, cOff + 3:cOff + 4],
             None, op0=ALU.mult)
        for kk in range(3):
            sh = 3 - kk
            e_stt(p, Bacc[:, sh:T], Bpre[:, 0:T - sh],
                  BCw[0:16, cOff + kk:cOff + kk + 1], Bacc[:, sh:T],
                  ALU.mult, ALU.add)
        vm = p.marker("v")
        p.wait("s", vm[0], vm[1])
        e_act(p, Bpre[:, 0:T], Bacc[:, 0:T], AF.Silu,
              bias=BCw[0:16, cbcol:cbcol + 1])
        sm = p.marker("s")
        p.wait("y", sm[0], sm[1])
        sbc = p.sem("sB2")
        e_dma(p, "y", dst[:, :], Bpre[0:16, 0:T], sbc)
        p.wait("s", sbc, sbc.n)
        p.wait("v", sbc, sbc.n)

    sdt = p.sem("sdt0")
    for mb in range(2):
        def ev_dt(mbx, tc, t0, tw, psum, tmark, mb=mb):
            p.wait("s", tmark[0], tmark[1])
            p.wait("s", sdt, sdt.n)
            e_act(p, dcb[0][:, 0:tw], psum, AF.Identity,
                  bias=dbv[:, mb:mb + 1])
            sm0 = p.marker("s")
            p.wait("v", sm0[0], sm0[1])
            e_softplus(p, dcb[1], dcb[0], dcb[2], nw=tw)
            vm = p.marker("v")
            p.wait("s", vm[0], vm[1])
            p.wait("y", vm[0], vm[1])
            e_dma(p, "y", dt_d[mb * 128:(mb + 1) * 128, t0:t0 + tw],
                  dcb[1][:, 0:tw], sdt)
            e_act(p, dcb[2][:, 0:tw], dcb[1][:, 0:tw], AF.Exp,
                  scale=Avc[:, mb:mb + 1])
            sm = p.marker("s")
            p.wait("y", sm[0], sm[1])
            e_dma(p, "y", dA_d[mb * 128:(mb + 1) * 128, t0:t0 + tw],
                  dcb[2][:, 0:tw], sdt)
            return sm
        linear_fm(p, lp, WinD[:, mb * 128:(mb + 1) * 128], xn2, D, 128,
                  ev_dt)
    p.wait("s", sdt, sdt.n)
    p.wait("v", sdt, sdt.n)
    p.barrier("ytvsg")
    p.close_scope(sc2)
    p.close_scope(sc_x)

    sc3 = []
    TCH = 1024
    NTCH = T // TCH
    dArep = [p.sb("dArep0", [128, TCH], scope=sc3) for _ in range(2)]
    dtrep = p.sb("dtrep0", [128, TCH], scope=sc3)
    gch = [p.sb("gch0", [128, TCH], scope=sc3) for _ in range(16)]
    xs_ch = [p.sb("xsch0", [128, TCH], BF16, scope=sc3) for _ in range(16)]
    zss = p.sb("zss0", [128, TCH], BF16, scope=sc3)
    Brep = [p.sb("Brep0", [128, TCH], BF16, scope=sc3) for _ in range(NST0)]
    Crep = [p.sb("Crep0", [128, TCH], BF16, scope=sc3) for _ in range(NST0)]
    u2 = [p.sb("u20", [128, TCH], BF16, scope=sc3)]
    dxb0 = p.sb("dxb0", [128, TCH], BF16, scope=sc3)
    hscb = p.sb("hscb0", [128, TCH], BF16, scope=sc3)
    hsc = p.sb("hsc0", [128, TCH], scope=sc3)
    tmp2 = p.sb("tmp20", [128, TCH], scope=sc3)
    carry = [p.sb("car0", [128, NST0], scope=sc3) for _ in range(16)]
    Dpv = p.sb("Dpv0", [128, 16], scope=sc3)
    rwv = p.sb("rwv0", [128, 16], scope=sc3)
    xnch = [tmp2[:, 0:512], tmp2[:, 512:1024]]
    x0ch = [hsc[:, 0:512], hsc[:, 512:1024]]
    schh = [dArep[0][:, 0:512], dArep[0][:, 512:1024]]
    np3 = norm_pool(p, scope=sc3, slim=True)
    sdl = p.sem("sdl")
    for dt_ in range(16):
        e_dma(p, "y", Dpv[:, dt_:dt_ + 1],
              Dprep[dt_ * 128:(dt_ + 1) * 128].unsqueeze(1), sdl)
        e_dma(p, "y", rwv[:, dt_:dt_ + 1],
              rmsw[dt_ * 128:(dt_ + 1) * 128].unsqueeze(1), sdl)
        e_memset(p, carry[dt_][:], 0.0)
    p.wait("v", sdl, sdl.n)
    p.wait("s", sdl, sdl.n)
    p.barrier("ytvsg")
    brs = [p.sem("brsa"), p.sem("brsb")]
    xss = [p.sem("xssa"), p.sem("xssb")]
    zsm = p.sem("zsm")
    drs = [p.sem("drsa"), p.sem("drsb")]
    drs2 = p.sem("sdl")
    for tch in range(NTCH):
        c0 = tch * TCH

        def load_dA(dt2, gate, c0=c0):
            b = dt2 % 2
            hl2 = dt2 * 16
            if gate is not None:
                p.wait("y", gate[0], gate[1])
            _, d1 = e_dma(p, "y", dArep[b][:],
                          dA_d[hl2:hl2 + 16, c0:c0 + TCH].unsqueeze(1)
                          .to_broadcast([16, 8, TCH]), drs[b])
            return d1

        def load_dt(dt2, gate, c0=c0):
            hl2 = dt2 * 16
            if gate is not None:
                p.wait("y", gate[0], gate[1])
            _, d2 = e_dma(p, "y", dtrep[:],
                          dt_d[hl2:hl2 + 16, c0:c0 + TCH].unsqueeze(1)
                          .to_broadcast([16, 8, TCH]), drs2)
            return d2
        for n in range(NST0):
            e_dma(p, "g", Brep[n][:],
                  Bt_d[n:n + 1, c0:c0 + TCH].partition_broadcast(128)
                  .squeeze(1), brs[tch % 2])
        for n in range(NST0):
            _, brt = e_dma(p, "g", Crep[n][:],
                           Ct_d[n:n + 1, c0:c0 + TCH].partition_broadcast(128)
                           .squeeze(1), brs[tch % 2])
        for dt_ in range(16):
            _, xst = e_dma(p, "y", xs_ch[dt_][:],
                           xs_d[dt_ * 128:(dt_ + 1) * 128, c0:c0 + TCH],
                           xss[tch % 2])
        p.wait("v", xss[tch % 2], xst)
        p.wait("s", xss[tch % 2], xst)
        p.wait("v", brs[tch % 2], brt)
        zst = {}
        yzm = [None]
        dr_t = {0: load_dA(0, None)}
        dt_t = {0: load_dt(0, None)}
        dr_use = [None, None]
        for dt_ in range(16):
            b = dt_ % 2
            if dt_ + 1 < 16:
                dr_t[dt_ + 1] = load_dA(dt_ + 1, dr_use[(dt_ + 1) % 2])
            p.wait("v", drs2, dt_t[dt_])
            e_tt(p, dxb0[:], dtrep[:], xs_ch[dt_][:], ALU.mult)
            vdx = p.marker("v")
            if dt_ + 1 < 16:
                dt_t[dt_ + 1] = load_dt(dt_ + 1, vdx)
            p.wait("v", drs[b], dr_t[dt_])
            if yzm[0] is not None:
                p.wait("y", yzm[0][0], yzm[0][1])
            _, zst[dt_] = e_dma(p, "y", zss[:],
                                zs_d[dt_ * 128:(dt_ + 1) * 128, c0:c0 + TCH],
                                zsm)
            for n in range(NST0):
                e_tt(p, u2[0][:], dxb0[:], Brep[n][:], ALU.mult)
                p.op("v", lambda eng, dt_=dt_, n=n, b=b:
                     eng.tensor_tensor_scan(
                         out=hscb[:], data0=dArep[b][:], data1=u2[0][:],
                         initial=carry[dt_][:, n:n + 1], op0=ALU.mult,
                         op1=ALU.add))
                e_copy(p, carry[dt_][:, n:n + 1], hscb[:, TCH - 1:TCH])
                if n == 0:
                    e_tt(p, gch[dt_][:], hscb[:], Crep[n][:], ALU.mult)
                else:
                    e_tt(p, hscb[:], hscb[:], Crep[n][:], ALU.mult)
                    e_tt(p, gch[dt_][:], gch[dt_][:], hscb[:], ALU.add)
            e_stt(p, gch[dt_][:], xs_ch[dt_][:], Dpv[:, dt_:dt_ + 1],
                  gch[dt_][:], ALU.mult, ALU.add)
            p.wait("v", zsm, zst[dt_])
            e_tt(p, gch[dt_][:], gch[dt_][:], zss[:], ALU.mult)
            yzm[0] = p.marker("v")
            dr_use[b] = yzm[0]
        vvm = p.marker("v")
        p.wait("s", vvm[0], vvm[1])
        p.wait("t", vvm[0], vvm[1])

        def rms_apply(d, tc2, t0, tw, xa, mu_bc, rstd_bc):
            e_stt(p, tmp2[:, 0:tw], xa, rwv[:, d:d + 1], rstd_bc, ALU.mult,
                  ALU.mult)
            e_copy(p, xs_ch[d][:, t0:t0 + tw], tmp2[:, 0:tw])

        norm_feat(p, np3, ("sb", gch), rms_apply, 1e-5, nd=16,
                  mean_center=False, n_free=TCH, n_feat=DI0)
        h1st = [p.sem("h1sa"), p.sem("h1sb")]
        xnld = [p.sem("xla"), p.sem("xlb")]
        xt_l = {}
        evd = {}

        def ev_out(mb, tc, t0, tw, psum, tmark, c0=c0, h1st=h1st, xnld=xnld,
                   xt_l=xt_l, evd=evd):
            u = (mb * (TCH // 512) + tc) % 2
            uk = mb * (TCH // 512) + tc
            if uk == 0:
                for k2 in (0, 1):
                    e_dma(p, "y", xnch[k2][:],
                          xn_out[0:128, c0 + k2 * 512:c0 + (k2 + 1) * 512],
                          xnld[k2])
                    _, xt_l[k2] = e_dma(p, "y", x0ch[k2][:],
                                        x0[0:128, c0 + k2 * 512:
                                           c0 + (k2 + 1) * 512], xnld[k2])
            p.wait("s", tmark[0], tmark[1])
            p.wait("s", xnld[u], xt_l[uk])
            if uk >= 2 and (uk - 2) in evd:
                p.wait("s", h1st[u], evd[uk - 2])
            e_act(p, schh[u][:, 0:tw], psum, AF.Identity)
            sm = p.marker("s")
            p.wait("v", sm[0], sm[1])
            e_tt(p, schh[u][:, 0:tw], schh[u][:, 0:tw], xnch[u][:, 0:tw],
                 ALU.add)
            e_tt(p, schh[u][:, 0:tw], schh[u][:, 0:tw], x0ch[u][:, 0:tw],
                 ALU.add)
            vm = p.marker("v")
            if uk + 2 < 8 * (TCH // 512):
                nk_ = uk + 2
                mb2, tc2 = nk_ // (TCH // 512), nk_ % (TCH // 512)
                p.wait("y", vm[0], vm[1])
                e_dma(p, "y", xnch[u][:],
                      xn_out[mb2 * 128:(mb2 + 1) * 128,
                             c0 + tc2 * 512:c0 + tc2 * 512 + 512], xnld[u])
                _, xt_l[nk_] = e_dma(p, "y", x0ch[u][:],
                                     x0[mb2 * 128:(mb2 + 1) * 128,
                                        c0 + tc2 * 512:c0 + tc2 * 512 + 512],
                                     xnld[u])
            p.wait("y", vm[0], vm[1])
            _, st_ = e_dma(p, "y", s_out[mb * 128:(mb + 1) * 128,
                                         c0 + t0:c0 + t0 + tw],
                           schh[u][:, 0:tw], h1st[u])
            evd[uk] = st_
            return vm

        linear_fm(p, lp, WoutT, xs_ch, DI0, D, ev_out, n_free=TCH)
        p.barrier("ytvsg")
    p.close_scope(sc3)
    if own:
        final_wait_all(p)
        return p.finalize()


_NC_CACHE = {}


def _get_nc(name, builder):
    if name not in _NC_CACHE:
        _NC_CACHE[name] = builder()
    return _NC_CACHE[name]


def _runT(name, builder, per_core, shared, outs):
    nc = _get_nc(name, builder)
    in_maps = []
    for c in range(B):
        m = dict(shared)
        for k, v in per_core.items():
            m[k] = v[c]
        in_maps.append(m)
    res = run_bass_kernel_spmd(nc, in_maps, core_ids=list(range(B)))
    return {o: [np.asarray(res.results[c][o]) for c in range(B)] for o in outs}


def kernel_unfused(**inputs):
    C = np.ascontiguousarray
    f32 = np.float32
    g = lambda k: np.asarray(inputs[k])
    x = g("L0_attn_x")
    xs = [C(x[b].T).astype(f32, copy=False) for b in range(B)]
    iden = np.eye(128, dtype=f32)
    DI0, NST0, NH0 = 2048, 16, 256

    Win = g("L0_attn_W_in")
    cw0 = g("L0_attn_conv_w")
    cb0 = g("L0_attn_conv_b")
    sh0 = {
        "ln_w": g("L0_attn_ln_w"), "ln_b": g("L0_attn_ln_b"),
        "WinZ": C(Win[:, :DI0]), "WinX": C(Win[:, DI0:2 * DI0]),
        "WinB": C(Win[:, 2 * DI0:2 * DI0 + NST0]),
        "WinC": C(Win[:, 2 * DI0 + NST0:2 * DI0 + 2 * NST0]),
        "WinD": C(Win[:, -NH0:]),
        "cwx": C(cw0[:, 0, :DI0]), "cwB": C(cw0[:, 0, DI0:DI0 + NST0]),
        "cwC": C(cw0[:, 0, DI0 + NST0:]),
        "cbx": C(cb0[:DI0]), "cbB": C(cb0[DI0:DI0 + NST0]),
        "cbC": C(cb0[DI0 + NST0:]),
        "A_log": g("L0_attn_A_log"),
        "Dp_rep": C(np.repeat(g("L0_attn_Dp"), 8)),
        "dt_bias": g("L0_attn_dt_bias"), "rms_w": g("L0_attn_rms_w"),
        "W_out": g("L0_attn_W_out"),
    }
    r = _runT("L0a", build_L0a, {"x": xs}, sh0, ["xn", "s"])
    xn_c, s_c = r["xn"], r["s"]
    sh = {"weight": g("L0_ffn_weight"), "lng": g("L0_ffn_ln_gamma"),
          "lnb": g("L0_ffn_ln_beta")}
    r = _runT("L0f", build_L0f, {"xn": xn_c, "sin": s_c}, sh, ["x1"])
    xs = r["x1"]

    blk2 = np.zeros((128, 2), f32)
    blk2[0:64, 0] = 1.0
    blk2[64:128, 1] = 1.0
    sh = {"Wq": g("L1_attn_Wq"), "Wk": g("L1_attn_Wk"), "Wv": g("L1_attn_Wv"),
          "Wa": g("L1_attn_Wa"), "ba": g("L1_attn_ba"), "Wo": g("L1_attn_Wo"),
          "gamma": g("L1_attn_gamma"), "beta": g("L1_attn_beta"),
          "blk2": blk2, "blk2b": C(blk2.T),
          "triu": C(np.triu(np.ones((128, 128), f32))), "iden": iden}
    r = _runT("L1a", build_L1a, {"x": xs}, sh, ["xn", "s"])
    xn_c, s_c = r["xn"], r["s"]
    sh = {"g1": g("L1_ffn_g1"), "g2": g("L1_ffn_g2"),
          "WqT": C(g("L1_ffn_Wq").T), "WkcT": C(g("L1_ffn_Wkc").T),
          "WvcT": C(g("L1_ffn_Wvc").T), "WoT": C(g("L1_ffn_Wo").T),
          "WrT": C(g("L1_ffn_Wr").T), "ebias": g("L1_ffn_expert_bias"),
          "sW1T": C(g("L1_ffn_sW1").T), "sW2T": C(g("L1_ffn_sW2").T),
          "sW3T": C(g("L1_ffn_sW3").T),
          "eW1T": C(g("L1_ffn_eW1").transpose(0, 2, 1)),
          "eW2T": C(g("L1_ffn_eW2").transpose(0, 2, 1)),
          "eW3T": C(g("L1_ffn_eW3").transpose(0, 2, 1)), "iden": iden}
    r = _runT("L1f", build_L1f, {"xn": xn_c, "sin": s_c}, sh, ["x2"])
    xs = r["x2"]

    xpw = g("L2_attn_x_proj_w")
    trid = np.zeros((4, 128, 512), f32)
    for j in range(4):
        si = np.arange(128)[:, None]
        ti = np.arange(512)[None, :]
        trid[j] = ((j * 128 + si) <= ti).astype(f32)
    sh = {"norm_m_w": g("L2_attn_norm_m_w"),
          "in_projT": C(g("L2_attn_in_proj_w").T),
          "conv_w": C(g("L2_attn_conv_w")[:, 0, :].T),
          "conv_b": g("L2_attn_conv_b"),
          "xpBT": C(xpw[:16].T), "xpCT": C(xpw[16:32].T),
          "xpdT": C(xpw[32:33].T), "dt_w": C(g("L2_attn_dt_w")[:, 0]),
          "dt_b": g("L2_attn_dt_b"), "A_log": g("L2_attn_A_log"),
          "D_skip": g("L2_attn_D_skip"),
          "out_projT": C(g("L2_attn_out_proj_w").T),
          "norm_attn_w": g("L2_attn_norm_attn_w"),
          "qwT": C(g("L2_attn_q_w").T), "kwT": C(g("L2_attn_k_w").T),
          "vwT": C(g("L2_attn_v_w").T), "owT": C(g("L2_attn_o_w").T),
          "norm_ffn_w": g("L2_attn_norm_ffn_w"),
          "gwT": C(g("L2_attn_gate_w").T), "uwT": C(g("L2_attn_up_w").T),
          "dwT": C(g("L2_attn_down_w").T), "trid": trid}
    r = _runT("L2a", build_L2a, {"x": xs}, sh, ["xn", "s"])
    xn_c, s_c = r["xn"], r["s"]
    sh = {"router_w": g("L2_ffn_router_w"), "router_b": g("L2_ffn_router_b"),
          "w1": g("L2_ffn_w1"), "w3": g("L2_ffn_w3"), "w2": g("L2_ffn_w2"),
          "iden": iden}
    r = _runT("L2f", build_L2f, {"xn": xn_c, "sin": s_c}, sh, ["x3"])
    xs = r["x3"]

    sh = {"w_qkv": g("L3_attn_w_qkv"), "b_qkv": g("L3_attn_b_qkv"),
          "w_proj": g("L3_attn_w_proj"), "b_proj": g("L3_attn_b_proj")}
    r = _runT("L3a", build_L3a, {"x": xs}, sh, ["xn", "s"])
    xn_c, s_c = r["xn"], r["s"]
    sh = {"qT": C(g("L3_ffn_qweights").transpose(0, 2, 1)),
          "scT": C(np.repeat(g("L3_ffn_scales"), 16, axis=2)
                   .transpose(0, 2, 1)),
          "biases": g("L3_ffn_biases"),
          "laT": C(g("L3_ffn_lora_a").transpose(0, 2, 1)),
          "lbT": C(g("L3_ffn_lora_b").transpose(0, 2, 1)),
          "lng": g("L3_ffn_ln_gamma"), "lnb": g("L3_ffn_ln_beta")}
    r = _runT("L3f", build_L3f, {"xn": xn_c, "sin": s_c}, sh, ["x4"])
    out = np.stack([np.asarray(v, f32).T for v in r["x4"]])
    return np.ascontiguousarray(out)


def build_fused(chain_n=8):
    nc = bass.Bass("TRN2", target_bir_lowering=False)
    p = P(nc)
    ins_map = []
    x_in = nc.dram_tensor("x", [D, T], F32, kind="ExternalInput").ap()
    out = nc.dram_tensor("out", [D, T], F32, kind="ExternalOutput").ap()
    chain = [("L0a", build_L0a), ("L0f", build_L0f),
             ("L1a", build_L1a), ("L1f", build_L1f),
             ("L2a", build_L2a), ("L2f", build_L2f),
             ("L3a", build_L3a), ("L3f", build_L3f)][:chain_n]
    cur = x_in
    l2f_state = {}
    for i, (tag, bld) in enumerate(chain):
        is_attn = tag.endswith("a")
        io = {"__tag__": tag + "_"}
        if is_attn:
            io["x"] = cur
        else:
            io["xn"] = xn_ap
            io["sin"] = s_ap
        if i == chain_n - 1:
            io[{"L0a": "s", "L0f": "x1", "L1a": "s", "L1f": "x2",
                "L2a": "s", "L2f": "x3", "L3a": "s", "L3f": "x4"}[tag]] = out
        sc = []
        p.scope_stack.append(sc)
        if tag == "L2a" and "L2f" in [t for t, _ in chain]:
            io2f = {"__tag__": "L2f_"}
            if i + 1 == chain_n - 1:
                io2f["x3"] = out
            handles2f = {}
            genbox = [None]

            def l2f_start(io=io, io2f=io2f):
                io2f["xn"] = io["xn"]
                io2f["sin"] = io["s"]
                genbox[0] = _L2f_stream((nc, p, io2f), handles2f)
                next(genbox[0])

            def l2f_mid():
                next(genbox[0])

            def l2f_step():
                next(genbox[0], None)

            def l2f_end():
                while True:
                    if next(genbox[0], "__done__") == "__done__":
                        break

            bld((nc, p, io), copar_start=l2f_start, copar_mid=l2f_mid,
                copar=l2f_step, copar_end=l2f_end)
            l2f_state["io"] = io2f
            l2f_state["handles"] = handles2f
        elif tag == "L2f" and l2f_state:
            io = l2f_state["io"]
            _L2f_final((nc, p, io), l2f_state["handles"])
        else:
            bld((nc, p, io))
        p.barrier("ytvsg")
        p.scope_stack.pop()
        p.close_scope(sc)
        for nm in io.get("__ins__", []):
            ins_map.append(nm)
        if is_attn:
            xn_ap = io["xn"]
            s_ap = io["s"]
        else:
            cur = io[{"L0f": "x1", "L1f": "x2", "L2f": "x3", "L3f": "x4"}[tag]]
    p.sem_prefix = ""
    final_wait_all(p)
    return p.finalize(), ins_map


_FUSED = {}


def kernel(**inputs):
    C = np.ascontiguousarray
    f32 = np.float32
    if "nc" not in _FUSED:
        _FUSED["nc"], _FUSED["ins"] = build_fused()
    nc, ins = _FUSED["nc"], _FUSED["ins"]
    shared, per_core = _prep_all(inputs)
    in_maps = []
    for c in range(B):
        m = {"x": per_core[c]}
        for dram_name, (tag, local) in [(dn, (dn.split("_", 1)[0], dn.split("_", 1)[1])) for dn in [n for n, _ in ins]] if False else []:
            pass
        for dram_name, local in ins:
            m[dram_name] = shared[dram_name]
        in_maps.append(m)
    res = run_bass_kernel_spmd(nc, in_maps, core_ids=list(range(B)))
    outp = np.stack([np.asarray(res.results[c]["out"], f32).T for c in range(B)])
    return np.ascontiguousarray(outp)


def _prep_all(inputs):
    C = np.ascontiguousarray
    f32 = np.float32
    g = lambda k: np.asarray(inputs[k])
    x = g("L0_attn_x")
    per_core = [C(x[b].T).astype(f32, copy=False) for b in range(B)]
    iden = np.eye(128, dtype=f32)
    DI0, NST0, NH0 = 2048, 16, 256
    Win = g("L0_attn_W_in")
    cw0 = g("L0_attn_conv_w")
    cb0 = g("L0_attn_conv_b")
    blk2 = np.zeros((128, 2), f32)
    blk2[0:64, 0] = 1.0
    blk2[64:128, 1] = 1.0
    xpw = g("L2_attn_x_proj_w")
    trid = np.zeros((4, 128, 512), f32)
    for j in range(4):
        si = np.arange(128)[:, None]
        ti = np.arange(512)[None, :]
        trid[j] = ((j * 128 + si) <= ti).astype(f32)
    sh = {}
    def put(tag, d):
        for k, v in d.items():
            sh[tag + "_" + k] = v
    put("L0a", {
        "ln_w": g("L0_attn_ln_w"), "ln_b": g("L0_attn_ln_b"),
        "WinZ": C(Win[:, :DI0]), "WinX": C(Win[:, DI0:2 * DI0]),
        "WinB": C(Win[:, 2 * DI0:2 * DI0 + NST0]),
        "WinC": C(Win[:, 2 * DI0 + NST0:2 * DI0 + 2 * NST0]),
        "WinD": C(Win[:, -NH0:]),
        "cwx": C(cw0[:, 0, :DI0]), "cwB": C(cw0[:, 0, DI0:DI0 + NST0]),
        "cwC": C(cw0[:, 0, DI0 + NST0:]),
        "cbx": C(cb0[:DI0]), "cbB": C(cb0[DI0:DI0 + NST0]),
        "cbC": C(cb0[DI0 + NST0:]),
        "A_log": g("L0_attn_A_log"), "Dp_rep": C(np.repeat(g("L0_attn_Dp"), 8)),
        "dt_bias": g("L0_attn_dt_bias"), "rms_w": g("L0_attn_rms_w"),
        "W_out": g("L0_attn_W_out")})
    put("L0f", {"weight": g("L0_ffn_weight"), "lng": g("L0_ffn_ln_gamma"),
                "lnb": g("L0_ffn_ln_beta")})
    put("L1a", {"Wq": g("L1_attn_Wq"), "Wk": g("L1_attn_Wk"),
                "Wv": g("L1_attn_Wv"), "Wa": g("L1_attn_Wa"),
                "ba": g("L1_attn_ba"), "Wo": g("L1_attn_Wo"),
                "gamma": g("L1_attn_gamma"), "beta": g("L1_attn_beta"),
                "blk2": blk2, "blk2b": C(blk2.T),
                "triu": C(np.triu(np.ones((128, 128), f32))), "iden": iden})
    put("L1f", {"g1": g("L1_ffn_g1"), "g2": g("L1_ffn_g2"),
                "WqT": C(g("L1_ffn_Wq").T), "WkcT": C(g("L1_ffn_Wkc").T),
                "WvcT": C(g("L1_ffn_Wvc").T), "WoT": C(g("L1_ffn_Wo").T),
                "WrT": C(g("L1_ffn_Wr").T), "ebias": g("L1_ffn_expert_bias"),
                "sW1T": C(g("L1_ffn_sW1").T), "sW2T": C(g("L1_ffn_sW2").T),
                "sW3T": C(g("L1_ffn_sW3").T),
                "eW1T": C(g("L1_ffn_eW1").transpose(0, 2, 1)),
                "eW2T": C(g("L1_ffn_eW2").transpose(0, 2, 1)),
                "eW3T": C(g("L1_ffn_eW3").transpose(0, 2, 1)), "iden": iden})
    put("L2a", {"norm_m_w": g("L2_attn_norm_m_w"),
                "in_projT": C(g("L2_attn_in_proj_w").T),
                "conv_w": C(g("L2_attn_conv_w")[:, 0, :].T),
                "conv_b": g("L2_attn_conv_b"),
                "xpBT": C(xpw[:16].T), "xpCT": C(xpw[16:32].T),
                "xpdT": C(xpw[32:33].T), "dt_w": C(g("L2_attn_dt_w")[:, 0]),
                "dt_b": g("L2_attn_dt_b"), "A_log": g("L2_attn_A_log"),
                "D_skip": g("L2_attn_D_skip"),
                "out_projT": C(g("L2_attn_out_proj_w").T),
                "norm_attn_w": g("L2_attn_norm_attn_w"),
                "qwT": C(g("L2_attn_q_w").T), "kwT": C(g("L2_attn_k_w").T),
                "vwT": C(g("L2_attn_v_w").T), "owT": C(g("L2_attn_o_w").T),
                "norm_ffn_w": g("L2_attn_norm_ffn_w"),
                "gwT": C(g("L2_attn_gate_w").T), "uwT": C(g("L2_attn_up_w").T),
                "dwT": C(g("L2_attn_down_w").T), "trid": trid})
    put("L2f", {"router_w": g("L2_ffn_router_w"),
                "router_b": g("L2_ffn_router_b"), "w1": g("L2_ffn_w1"),
                "w3": g("L2_ffn_w3"), "w2": g("L2_ffn_w2"), "iden": iden})
    put("L3a", {"w_qkv": g("L3_attn_w_qkv"), "b_qkv": g("L3_attn_b_qkv"),
                "w_proj": g("L3_attn_w_proj"), "b_proj": g("L3_attn_b_proj")})
    put("L3f", {"qT": C(g("L3_ffn_qweights").transpose(0, 2, 1)),
                "scT": C(np.repeat(g("L3_ffn_scales"), 16, axis=2)
                         .transpose(0, 2, 1)),
                "biases": g("L3_ffn_biases"),
                "laT": C(g("L3_ffn_lora_a").transpose(0, 2, 1)),
                "lbT": C(g("L3_ffn_lora_b").transpose(0, 2, 1)),
                "lng": g("L3_ffn_ln_gamma"), "lnb": g("L3_ffn_ln_beta")})
    return sh, per_core


# version 13 (final)
# speedup vs baseline: 1.1773x; 1.1773x over previous
import os
import numpy as np
import ml_dtypes
import concourse.bass as bass
import concourse.mybir as mybir
from concourse.bass_utils import run_bass_kernel_spmd
from contextlib import ExitStack

F32 = mybir.dt.float32
BF16 = mybir.dt.bfloat16
I32 = mybir.dt.int32
AF = mybir.ActivationFunctionType
ALU = mybir.AluOpType
AX = mybir.AxisListType

B, T, D = 8, 4096, 1024
MAGIC = 12582912.0
ENG = ("y", "t", "v", "s", "g")


class Sem:
    __slots__ = ("h", "n")

    def __init__(self, handle):
        self.h = handle
        self.n = 0


class P:

    def __init__(self, nc):
        self.nc = nc
        self.es = ExitStack()
        self.q = {e: [] for e in ENG}
        self.sems = {}
        self._perm = []
        self._cvecs = {}
        self._uid = 0
        self.scope_stack = []
        self.sem_prefix = ""
        self._sem_cms = {}
        self.dma_sems = set()
        self._cva = self.sb("cvarena", [128, 32])

    def uid(self):
        self._uid += 1
        return self._uid

    def sb(self, name, shape, dt=F32, scope=None):
        cm = self.nc.sbuf_tensor(f"{name}{self.uid()}", shape, dt)
        t = cm.__enter__()
        if scope is None and self.scope_stack:
            scope = self.scope_stack[-1]
        if scope is not None:
            scope.append(cm)
        else:
            self._perm.append(cm)
        return t

    def pt(self, name, shape, dt=F32, scope=None):
        cm = self.nc.psum_tensor(f"{name}{self.uid()}", shape, dt)
        t = cm.__enter__()
        if scope is None and self.scope_stack:
            scope = self.scope_stack[-1]
        if scope is not None:
            scope.append(cm)
        else:
            self._perm.append(cm)
        return t

    def close_scope(self, scope):
        for cm in reversed(scope):
            cm.__exit__(None, None, None)
        scope.clear()

    def sem(self, name):
        if not (name.startswith("prog_") or name.startswith("bar_")):
            name = self.sem_prefix + name
        if name not in self.sems:
            cm = self.nc.semaphore(name)
            h = cm.__enter__()
            self.sems[name] = Sem(h)
            self._sem_cms[name] = cm
        return self.sems[name]

    def release_sems(self, prefix):
        names = [n for n in self.sems if n.startswith(prefix)]
        for n in names:
            sm = self.sems[n]
            self.op("g", lambda eng, h=sm.h: eng.sem_clear(h))
        self.barrier("ytvsg")
        for n in names:
            self.dma_sems.discard(self.sems[n])
            self._sem_cms[n].__exit__(None, None, None)
            del self.sems[n]
            del self._sem_cms[n]

    def op(self, e, fn, sem=None, k=1, kind="i"):
        if kind == "i" and e in "vsg":
            assert sem is None
            ps = self.sem("prog_" + e)
            if ps.n > 0:
                self.q[e].append({"fn": (lambda eng, s=ps, t=ps.n:
                                         eng.wait_ge(s.h, t)),
                                  "sem": None, "k": 0, "kind": "w"})
            ps.n += 1
            self.q[e].append({"fn": fn, "sem": ps, "k": 1, "kind": "i"})
            return ps.n
        ent = {"fn": fn, "sem": sem, "k": k, "kind": kind}
        self.q[e].append(ent)
        if sem is not None:
            sem.n += k
            return sem.n
        return None

    def wait(self, e, sem, tgt=None):
        tv = sem.n if tgt is None else tgt
        if tv > 0:
            self.q[e].append({"fn": (lambda eng, s=sem, t=tv:
                                     eng.wait_ge(s.h, t)),
                              "sem": None, "k": 0, "kind": "w"})
        return tv

    def marker(self, e):
        s = self.sem("prog_" + e)
        if e in "vsg":
            if s.n == 0:
                s.n += 1
                self.q[e].append({"fn": lambda eng: eng.nop(), "sem": s,
                                  "k": 1, "kind": "i"})
            return s, s.n
        q = self.q[e]
        pre = s.n
        if q and q[-1]["kind"] == "i" and q[-1]["sem"] is None:
            if pre > 0:
                q.insert(len(q) - 1,
                         {"fn": (lambda eng, ss=s, t=pre:
                                 eng.wait_ge(ss.h, t)),
                          "sem": None, "k": 0, "kind": "w"})
            q[-1]["sem"] = s
            q[-1]["k"] = 1
            s.n += 1
            return s, s.n
        if pre > 0:
            q.append({"fn": (lambda eng, ss=s, t=pre: eng.wait_ge(ss.h, t)),
                      "sem": None, "k": 0, "kind": "w"})
        s.n += 1
        tv = s.n
        q.append({"fn": lambda eng: eng.nop(), "sem": s, "k": 1, "kind": "i"})
        return s, tv

    def cvec(self, val):
        key = float(val)
        if key not in self._cvecs:
            j = len(self._cvecs)
            assert j < 32, "cvec arena full"
            t = self._cva[:, j:j + 1]
            self.op("v", lambda eng, t=t, v=key: eng.memset(t, v))
            mark = self.marker("v")
            self._cvecs[key] = (t, mark)
        return self._cvecs[key]

    def barrier(self, engines="ytvsg"):
        e0 = engines[0]
        for ds in sorted(self.dma_sems, key=lambda x: id(x)):
            if ds.n > 0:
                self.wait(e0, ds, ds.n)
        marks = [self.marker(e) for e in engines]
        for e in engines:
            for s, tgt in marks:
                self.wait(e, s, tgt)

    def finalize(self):
        nc = self.nc
        with nc.Block() as block:
            for e in ENG:
                dec = dict(y=block.sync, t=block.tensor, v=block.vector,
                           s=block.scalar, g=block.gpsimd)[e]
                ops = self.q[e]

                def mk(ops=ops):
                    def _(eng):
                        for ent in ops:
                            r = ent["fn"](eng)
                            if ent["sem"] is not None:
                                r.then_inc(ent["sem"].h, ent["k"])
                    return _
                dec(mk())
        for cm in reversed(self._perm):
            cm.__exit__(None, None, None)
        self.es.close()
        return nc


def e_dma(p, e, dst, src, sem=None):
    sem = sem or p.sem("dma_" + e)
    p.dma_sems.add(sem)
    tgt = p.op(e, lambda eng: eng.dma_start(out=dst, in_=src), sem, 16,
               kind="d")
    return sem, tgt


def e_act(p, out, in_, func=AF.Identity, bias=0.0, scale=1.0, accum=None,
          sem=None):
    if isinstance(bias, float) and bias != 0.0:
        t, mark = p.cvec(bias)
        p.wait("s", mark[0], mark[1])
        bias = t[0:out.shape[0], :]
    def f(eng):
        kw = {}
        if accum is not None:
            kw["accum_out"] = accum
        return eng.activation(out=out, in_=in_, func=func, bias=bias,
                              scale=scale, **kw)
    return p.op("s", f, sem)


def e_tt(p, out, in0, in1, alu, e="v", sem=None):
    return p.op(e, lambda eng: eng.tensor_tensor(out=out, in0=in0, in1=in1,
                                                 op=alu), sem)


def e_ts(p, out, in0, s1, s2=None, op0=ALU.mult, op1=None, e="v", sem=None,
         accum=None):
    def f(eng):
        kw = dict(out=out, in0=in0, scalar1=s1, scalar2=s2, op0=op0)
        if op1 is not None:
            kw["op1"] = op1
        if accum is not None:
            kw["accum_out"] = accum
        return eng.tensor_scalar(**kw)
    return p.op(e, f, sem)


def e_stt(p, out, in0, scal, in1, op0, op1, e="v", sem=None):
    return p.op(e, lambda eng: eng.scalar_tensor_tensor(
        out=out, in0=in0, scalar=scal, in1=in1, op0=op0, op1=op1), sem)


def e_mm(p, out, lhsT, rhs, start=True, stop=True, sem=None):
    from concourse.bass_types import AP as _AP
    assert isinstance(out, _AP), f"out {type(out)}"
    assert isinstance(lhsT, _AP), f"lhsT {type(lhsT)}"
    assert isinstance(rhs, _AP), f"rhs {type(rhs)}"
    return p.op("t", lambda eng: eng.matmul(out, lhsT, rhs, start=start,
                                            stop=stop), sem)


def e_copy(p, out, in_, e="v", sem=None):
    return p.op(e, lambda eng: eng.tensor_copy(out=out, in_=in_), sem)


def e_memset(p, ap, val, e="v", sem=None):
    return p.op(e, lambda eng: eng.memset(ap, val), sem)


def e_softplus(p, out, in_, tmp, n0=0, nw=None, tmp0=False, in0=False):
    sl = (slice(None), slice(n0, (n0 + nw) if nw is not None else None))
    tl = (slice(None), slice(0, nw)) if tmp0 else sl
    il = tl if in0 else sl
    e_act(p, tmp[tl], in_[il], AF.Abs)
    e_act(p, tmp[tl], tmp[tl], AF.Exp, scale=-1.0)
    e_act(p, tmp[tl], tmp[tl], AF.Ln, bias=1.0)
    sm = p.marker("s")
    e_ts(p, out[sl], in_[il], 0.0, None, op0=ALU.max)
    p.wait("v", sm[0], sm[1])
    e_tt(p, out[sl], out[sl], tmp[tl], ALU.add)
    vm = p.marker("v")
    p.wait("s", vm[0], vm[1])


def lin_pool(p, max_nk, scope=None, np_ps=2):
    return dict(
        wbuf=[p.sb("wb", [128, max_nk * 128], BF16, scope=scope) for _ in range(2)],
        ps=[p.pt("lp", [128, 512], scope=scope) for _ in range(np_ps)],
        wsem=[p.sem("wlda"), p.sem("wldb")],
        prev=[None] * np_ps,
        wprev=[None, None],
        np_ps=np_ps,
    )


def linear_fm(p, pool, w_dram, x_tiles, K, M, evac, n_free=None, extra_k=None,
              w_fn=None):
    n_free = n_free or T
    """psum[(mb,tc)] = sum_k W[k-blk, m-blk]^T @ x[k][:, t-chunk].

    evac(mb, tc, t0, tw, psum_ap, tmark) is called at record time right
    after the unit's matmuls; must wait on tmark before touching psum and
    return a marker gating psum reuse.
    w_fn: optional override(mb, wsem)->wtarget recording the weight load for
    m-block mb into pool['wbuf'][mb%2] (for on-chip dequant etc).
    """
    nk = (K + 127) // 128
    nm = (M + 127) // 128
    assert nk * 128 <= pool["wbuf"][0].shape[-1]
    ntc = (n_free + 511) // 512
    wsems = pool["wsem"]
    wbuf, ps = pool["wbuf"], pool["ps"]
    emarks = list(pool["prev"])
    wmarks = list(pool["wprev"])
    wtargets = []

    def load_w(mb):
        wsem = wsems[mb % 2]
        if wmarks[mb % 2] is not None:
            p.wait("g", wmarks[mb % 2][0], wmarks[mb % 2][1])
        if w_fn is not None:
            wtargets.append(w_fn(mb, wsem))
            return
        m0 = mb * 128
        mw = min(128, M - m0)
        if K % 128 == 0:
            src = w_dram[:, m0:m0 + mw].rearrange("(kc p) m -> p kc m", p=128)
            dst = wbuf[mb % 2][:, 0:nk * mw].rearrange("p (kc m) -> p kc m",
                                                       kc=nk)
            _, wt = e_dma(p, "g", dst, src, wsem)
        else:
            for k in range(nk):
                k0 = k * 128
                kw = min(128, K - k0)
                _, wt = e_dma(p, "g", wbuf[mb % 2][0:kw, k * mw:k * mw + mw],
                              w_dram[k0:k0 + kw, m0:m0 + mw], wsem)
        wtargets.append(wt)

    load_w(0)
    if nm > 1:
        load_w(1)
    for mb in range(nm):
        m0 = mb * 128
        mw = min(128, M - m0)
        p.wait("t", wsems[mb % 2], wtargets[mb])
        for tc in range(ntc):
            t0 = tc * 512
            tw = min(512, n_free - t0)
            u = mb * ntc + tc
            nps = pool.get("np_ps", 2)
            pp = ps[u % nps]
            if emarks[u % nps] is not None:
                p.wait("t", emarks[u % nps][0], emarks[u % nps][1])
            for k in range(nk):
                kw = min(128, K - k * 128)
                e_mm(p, pp[0:mw, 0:tw], wbuf[mb % 2][0:kw, k * mw:k * mw + mw],
                     x_tiles[k][0:kw, t0:t0 + tw], start=(k == 0),
                     stop=(k == nk - 1 and extra_k is None))
            if extra_k is not None:
                extra_k(mb, tc, pp[0:mw, 0:tw], t0, tw, mw)
            tmark = p.marker("t")
            wmarks[mb % 2] = tmark
            em = evac(mb, tc, t0, tw, pp[0:mw, 0:tw], tmark)
            emarks[u % nps] = em
        if mb + 2 < nm:
            load_w(mb + 2)
    pool["prev"] = emarks
    pool["wprev"] = wmarks
    return emarks


def evac_to_tiles(p, dst_tiles, func=AF.Identity, bias=None, scale=1.0,
                  col0=0):
    def evac(mb, tc, t0, tw, psum, tmark):
        p.wait("s", tmark[0], tmark[1])
        mw = psum.shape[0]
        b = bias(mb) if bias is not None else 0.0
        sc = scale(mb) if callable(scale) else scale
        e_act(p, dst_tiles[mb][0:mw, col0 + t0:col0 + t0 + tw], psum, func,
              bias=b, scale=sc)
        return p.marker("s")
    return evac


def linear_tm(p, pool, w_tiles, x_tiles, K, N, evac, bias_row=None,
              ones_row=None, n_free=None):
    n_free = n_free or T
    """out[(tt),(0:N)] = sum_k x[k][:, tt*128..]^T w[k] (+ ones^T bias_row).
    Uses pool['ps'] (128,512) psums; N <= 512."""
    nk = (K + 127) // 128
    ntt = n_free // 128
    ps = pool["ps"]
    emarks = list(pool["prev"])
    for tt in range(ntt):
        t0 = tt * 128
        pp = ps[tt % 2]
        if emarks[tt % 2] is not None:
            p.wait("t", emarks[tt % 2][0], emarks[tt % 2][1])
        for k in range(nk):
            kw = min(128, K - k * 128)
            e_mm(p, pp[0:128, 0:N], x_tiles[k][0:kw, t0:t0 + 128], w_tiles[k][:, 0:N],
                 start=(k == 0), stop=(k == nk - 1 and bias_row is None))
        if bias_row is not None:
            e_mm(p, pp[0:128, 0:N], ones_row[0:1, t0:t0 + 128], bias_row[0:1, 0:N],
                 start=False, stop=True)
        tmark = p.marker("t")
        emarks[tt % 2] = evac(tt, pp[0:128, 0:N], tmark)
    pool["prev"] = emarks
    return emarks


def norm_pool(p, scope=None, slim=False):
    d = dict(
        sq=p.sb("nsq", [128, 512], scope=scope),
        stat=p.sb("nstat", [1, 512], scope=scope),
        mu=p.sb("nmu", [1, 512], scope=scope),
        onesc=p.sb("nonesc", [128, 1], scope=scope),
        onescb=p.sb("nonescb", [128, 1], BF16, scope=scope),
        onesr=p.sb("nonesr", [1, 128], scope=scope),
        ps_q=p.pt("npq", [1, 512], scope=scope),
        ps_rs=p.pt("npr", [128, 512], scope=scope),
        init=False,
    )
    if slim:
        d["ps_s"] = None
        d["ps_mu"] = None
    else:
        d["ps_s"] = p.pt("nps", [1, 512], scope=scope)
        d["ps_mu"] = p.pt("npm", [128, 512], scope=scope)
    return d


def norm_feat(p, np_, src, apply_cb, eps, nd=8, mean_center=True, n_free=None,
              n_feat=None, xch=None):
    n_free = n_free or T
    """src: ('dram', ap) (streams chunks into xch pair) or ('sb', tiles).
    apply_cb(d, tc, t0, tw, xa_fp32_ap, mu_bc_or_None, rstd_bc)."""
    Nf = n_feat or nd * 128
    ntc = (n_free + 511) // 512
    sq, stat, mu = np_["sq"], np_["stat"], np_["mu"]
    onesc, onesr = np_["onesc"], np_["onesr"]
    ps_s, ps_q, ps_mu, ps_rs = np_["ps_s"], np_["ps_q"], np_["ps_mu"], np_["ps_rs"]
    if not np_["init"]:
        e_memset(p, onesc[:], 1.0)
        e_memset(p, np_["onescb"][:], 1.0)
        e_memset(p, onesr[:], 1.0)
        vready = p.marker("v")
        p.wait("t", vready[0], vready[1])
        np_["init"] = True
    streaming = src[0] == "dram"
    if streaming:
        x_dram = src[1]
        lsems = [p.sem("nldxa"), p.sem("nldxb")]
        ltargets = {}

        def load_chunk(tc):
            t0, tw = tc * 512, min(512, n_free - tc * 512)
            for d in range(nd):
                _, lt = e_dma(p, "y", xch[tc % 2][:, d * 512:d * 512 + tw],
                              x_dram[d * 128:(d + 1) * 128, t0:t0 + tw],
                              lsems[tc % 2])
            ltargets[tc] = lt

        load_chunk(0)
        if ntc > 1:
            load_chunk(1)
    else:
        tiles = src[1]

    for tc in range(ntc):
        t0, tw = tc * 512, min(512, n_free - tc * 512)
        if streaming:
            xa = lambda d: xch[tc % 2][:, d * 512:d * 512 + tw]
            p.wait("s", lsems[tc % 2], ltargets[tc])
            p.wait("t", lsems[tc % 2], ltargets[tc])
        else:
            xa = lambda d: tiles[d][:, t0:t0 + tw]
        for d in range(nd):
            e_act(p, sq[:, 0:tw], xa(d), AF.Square)
            sm = p.marker("s")
            p.wait("t", sm[0], sm[1])
            if mean_center:
                onesx = (np_["onescb"] if xa(d).dtype == BF16 else onesc)
                e_mm(p, ps_s[:, 0:tw], onesx[:, :], xa(d),
                     start=(d == 0), stop=(d == nd - 1))
            e_mm(p, ps_q[:, 0:tw], onesc[:, :], sq[:, 0:tw],
                 start=(d == 0), stop=(d == nd - 1))
            tm = p.marker("t")
            p.wait("s", tm[0], tm[1])
        p.wait("v", tm[0], tm[1])
        if mean_center:
            e_ts(p, mu[:, 0:tw], ps_s[:, 0:tw], 1.0 / Nf)
            e_ts(p, stat[:, 0:tw], ps_q[:, 0:tw], 1.0 / Nf)
            e_tt(p, sq[0:1, 0:tw], mu[:, 0:tw], mu[:, 0:tw], ALU.mult)
            e_tt(p, stat[:, 0:tw], stat[:, 0:tw], sq[0:1, 0:tw], ALU.subtract)
        else:
            e_ts(p, stat[:, 0:tw], ps_q[:, 0:tw], 1.0 / Nf)
        vm = p.marker("v")
        p.wait("s", vm[0], vm[1])
        e_act(p, stat[:, 0:tw], stat[:, 0:tw], AF.Ln, bias=float(eps))
        e_act(p, stat[:, 0:tw], stat[:, 0:tw], AF.Exp, scale=-0.5)
        sm2 = p.marker("s")
        p.wait("t", sm2[0], sm2[1])
        if mean_center:
            e_mm(p, ps_mu[:, 0:tw], onesr[:, :], mu[:, 0:tw])
        e_mm(p, ps_rs[:, 0:tw], onesr[:, :], stat[:, 0:tw])
        tb = p.marker("t")
        p.wait("v", tb[0], tb[1])
        for d in range(nd):
            apply_cb(d, tc, t0, tw, xa(d),
                     ps_mu[:, 0:tw] if mean_center else None, ps_rs[:, 0:tw])
        va = p.marker("v")
        p.wait("t", va[0], va[1])
        p.wait("s", va[0], va[1])
        if streaming and tc + 2 < ntc:
            p.wait("y", va[0], va[1])
            load_chunk(tc + 2)
    p.barrier("ytvsg")


def load_bias_cols(p, dram_1d, n, name="bc", scope=None):
    t = p.sb(name, [128, n], scope=scope)
    sem = p.sem("blc")
    for j in range(n):
        e_dma(p, "y", t[:, j:j + 1],
              dram_1d[j * 128:(j + 1) * 128].unsqueeze(1), sem)
    return t, sem, sem.n


def final_wait_all(p):
    for name, s in list(p.sems.items()):
        if s.n > 0:
            p.wait("y", s, s.n)




def _dram(nc, io, name, shape, dt, kind):
    if io is None:
        if kind == "Internal":
            return nc.dram_tensor(name, shape, dt).ap()
        return nc.dram_tensor(name, shape, dt, kind=kind).ap()
    tag = io["__tag__"]
    if kind == "ExternalInput":
        if name in io:
            return io[name]
        ap = nc.dram_tensor(tag + name, shape, dt, kind="ExternalInput").ap()
        io.setdefault("__ins__", []).append((tag + name, name))
        return ap
    if kind == "ExternalOutput":
        if name in io:
            return io[name]
        ap = nc.dram_tensor(tag + name, shape, dt).ap()
        io[name] = ap
        return ap
    return nc.dram_tensor(tag + name, shape, dt).ap()


def attn_head(p, sc_chunks, get_vchunks, o_cb, state, n_free=None, tgsz=1024,
              s_tiles=None, causal_sbs=None, mask_cb=None, exp_scale=1.0):
    n_free = n_free or T
    s_tiles = s_tiles or (T // 128)
    """One head's attention, scores^T orientation, no max-subtraction
    (inputs are normalised and projection weights are ~N(0, 0.02^2), so
    |scores| stays far below the fp32 exp overflow threshold of 88).

    sc_chunks: list of (k_fn(s0)->lhsT(kc,128), q_fn(g0,gw)->rhs(kc,gw)).
    get_vchunks(sb)->list of (lhsT_ap, psum_idx, rows); ov_ps[psum_idx][sub]
    accumulates (rows, 512).
    o_cb(tg, g0, gw): normalise/write, psums reused after.
    causal_sbs: fn(tg)->list of s-tile indices; mask_cb(sb_, tg, pbuf_ap):
    records masking of the exp'd tile (on vector), returns marker or None.
    """
    nsub = tgsz // 512
    sc_ps, pbuf, ov_ps = state["sc_ps"], state["pbuf"], state["ov_ps"]
    ntg = n_free // tgsz
    for tg in range(ntg):
        g0 = tg * tgsz
        exp_marks = [None, None]
        pv_marks = [None, None]
        sbs = list(range(s_tiles)) if causal_sbs is None else causal_sbs(tg)
        for idx, sb_ in enumerate(sbs):
            s0 = sb_ * 128
            pp = sc_ps[idx % 2]
            if exp_marks[idx % 2] is not None:
                p.wait("t", exp_marks[idx % 2][0], exp_marks[idx % 2][1])
            for sub in range(nsub):
                for ic, (kfn, qfn) in enumerate(sc_chunks):
                    e_mm(p, pp[:, sub * 512:(sub + 1) * 512], kfn(s0),
                         qfn(g0 + sub * 512, 512), start=(ic == 0),
                         stop=(ic == len(sc_chunks) - 1))
            tm = p.marker("t")
            p.wait("s", tm[0], tm[1])
            if pv_marks[idx % 2] is not None:
                p.wait("s", pv_marks[idx % 2][0], pv_marks[idx % 2][1])
            e_act(p, pbuf[idx % 2][:, 0:tgsz], pp[:, 0:tgsz], AF.Exp,
                  scale=exp_scale)
            em = p.marker("s")
            if mask_cb is not None:
                mm_ = mask_cb(sb_, tg, pbuf[idx % 2], em)
                if mm_ is not None:
                    em = mm_
            exp_marks[idx % 2] = em
            p.wait("t", em[0], em[1])
            for sub in range(nsub):
                for (vap, pidx, rows) in get_vchunks(sb_):
                    e_mm(p, ov_ps[pidx][sub][0:rows, :], vap,
                         pbuf[idx % 2][:, sub * 512:(sub + 1) * 512],
                         start=(idx == 0), stop=(idx == len(sbs) - 1))
            pvm = p.marker("t")
            pv_marks[idx % 2] = pvm
        for m_ in pv_marks:
            if m_ is not None:
                p.wait("s", m_[0], m_[1])
                p.wait("v", m_[0], m_[1])
        o_cb(tg, g0, tgsz)


def build_L3a(ctx=None):
    own = ctx is None
    if own:
        nc = bass.Bass("TRN2", target_bir_lowering=False)
        p = P(nc)
        io = None
    else:
        nc, p, io = ctx
    xd = _dram(nc, io, "x", [D, T], F32, "ExternalInput")
    wqkv = _dram(nc, io, "w_qkv", [D, 3 * D], F32, "ExternalInput")
    bqkv = _dram(nc, io, "b_qkv", [3 * D], F32, "ExternalInput")
    wproj = _dram(nc, io, "w_proj", [D, D], F32, "ExternalInput")
    bproj = _dram(nc, io, "b_proj", [D], F32, "ExternalInput")
    xn_out = _dram(nc, io, "xn", [D, T], F32, "ExternalOutput")
    s_out = _dram(nc, io, "s", [D, T], F32, "ExternalOutput")
    o_dram = _dram(nc, io, "o_scr", [D, T], BF16, "Internal")

    xnb = [p.sb("xnb", [128, T], BF16) for _ in range(8)]

    sc1 = []
    np1 = norm_pool(p, scope=sc1)
    xch = [p.sb("xch", [128, 512 * 8], scope=sc1) for _ in range(2)]
    xnf_ch = [p.sb("xnf", [128, 512 * 8], scope=sc1) for _ in range(2)]
    stx = [p.sem("stxa"), p.sem("stxb")]
    st_marks = [None, None]

    def ln_apply(d, tc, t0, tw, xa, mu_bc, rstd_bc):
        ch = xnf_ch[tc % 2]
        if d == 0 and st_marks[tc % 2] is not None:
            p.wait("v", stx[tc % 2], st_marks[tc % 2])
        e_tt(p, ch[:, d * 512:d * 512 + tw], xa, mu_bc, ALU.subtract)
        e_tt(p, ch[:, d * 512:d * 512 + tw], ch[:, d * 512:d * 512 + tw],
             rstd_bc, ALU.mult)
        e_copy(p, xnb[d][:, t0:t0 + tw], ch[:, d * 512:d * 512 + tw])
        if d == 7:
            vm = p.marker("v")
            p.wait("y", vm[0], vm[1])
            for dd in range(8):
                e_dma(p, "y", xn_out[dd * 128:(dd + 1) * 128, t0:t0 + tw],
                      ch[:, dd * 512:dd * 512 + tw], stx[tc % 2])
            st_marks[tc % 2] = stx[tc % 2].n

    norm_feat(p, np1, ("dram", xd), ln_apply, 1e-5, xch=xch)
    p.close_scope(sc1)

    sc2 = []
    bq_sb, _, _ = load_bias_cols(p, bqkv, 24, "bqk", scope=sc2)
    ones_t = p.sb("onest", [1, T], BF16, scope=sc2)
    vbias = p.sb("vbias", [1, 256], BF16, scope=sc2)
    onesq = p.sb("onesq", [1, 64], scope=sc2)
    e_memset(p, ones_t[:], 1.0)
    e_memset(p, onesq[:], 1.0)
    qt = [p.sb("qt", [128, T], BF16, scope=sc2) for _ in range(2)]
    kt = [p.sb("kt", [128, T], BF16, scope=sc2) for _ in range(2)]
    ot = [p.sb("ot", [128, T], BF16, scope=sc2) for _ in range(2)]
    vt = p.sb("vt", [128, 32 * 260], BF16, scope=sc2)
    wv_sb = [p.sb("wv", [128, 256], BF16, scope=sc2) for _ in range(8)]
    o_scr = p.sb("oscr", [65, 1024], scope=sc2)
    rec = p.sb("rec", [1, 1024], scope=sc2)
    lp = lin_pool(p, 8, scope=sc2)
    state = dict(
        sc_ps=[p.pt("scps", [128, 1024], scope=sc2) for _ in range(2)],
        pbuf=[p.sb("pb", [128, 1024], BF16, scope=sc2) for _ in range(2)],
        ov_ps=[[p.pt("ovps", [65, 512], scope=sc2) for _ in range(2)]],
    )
    p.barrier("ytvsg")

    for q in range(4):
        ev_q = evac_to_tiles(p, qt, AF.Identity,
                             bias=lambda mb, q=q: bq_sb[:, 2 * q + mb:2 * q + mb + 1])
        linear_fm(p, lp, wqkv[:, 256 * q:256 * q + 256], xnb, D, 256, ev_q)
        ev_k = evac_to_tiles(p, kt, AF.Identity,
                             bias=lambda mb, q=q: bq_sb[:, 8 + 2 * q + mb:8 + 2 * q + mb + 1])
        linear_fm(p, lp, wqkv[:, D + 256 * q:D + 256 * q + 256], xnb, D, 256,
                  ev_k)
        wsem = p.sem("wvs")
        for k in range(8):
            e_dma(p, "g", wv_sb[k][:],
                  wqkv[k * 128:(k + 1) * 128,
                       2 * D + 256 * q:2 * D + 256 * q + 256], wsem)
        e_dma(p, "g", vbias[:],
              bqkv[2 * D + 256 * q:2 * D + 256 * q + 256].unsqueeze(0), wsem)
        p.wait("t", wsem, wsem.n)
        for tt in range(T // 128):
            for h in range(4):
                e_memset(p, vt[:, tt * 260 + h * 65 + 64:tt * 260 + h * 65 + 65],
                         1.0)
        vm0 = p.marker("v")
        p.wait("s", vm0[0], vm0[1])

        def ev_v(tt, psum, tmark):
            p.wait("s", tmark[0], tmark[1])
            for h in range(4):
                e_act(p, vt[:, tt * 260 + h * 65:tt * 260 + h * 65 + 64],
                      psum[:, h * 64:h * 64 + 64], AF.Identity)
            return p.marker("s")
        linear_tm(p, lp, wv_sb, xnb, D, 256, ev_v, bias_row=vbias,
                  ones_row=ones_t)
        p.barrier("ytvsg")

        for hl in range(4):
            koff = (hl % 2) * 64
            kk, qq = kt[hl // 2], qt[hl // 2]
            orow = (hl % 2) * 64

            def o_cb(tg, g0, gw, hl=hl, orow=orow):
                ov = state["ov_ps"][0]
                for sub in range(2):
                    e_act(p, o_scr[:, sub * 512:(sub + 1) * 512],
                          ov[sub][:, :], AF.Identity)
                sm = p.marker("s")
                p.wait("v", sm[0], sm[1])
                p.op("v", lambda eng: eng.reciprocal(out=rec[:, 0:1024],
                                                     in_=o_scr[64:65, 0:1024]))
                vm = p.marker("v")
                p.wait("t", vm[0], vm[1])
                for sub in range(2):
                    e_mm(p, ov[sub][0:64, :], onesq[:, 0:64],
                         rec[:, sub * 512:(sub + 1) * 512])
                tb = p.marker("t")
                p.wait("v", tb[0], tb[1])
                for sub in range(2):
                    e_tt(p, ot[hl // 2][orow:orow + 64,
                                        g0 + sub * 512:g0 + (sub + 1) * 512],
                         o_scr[0:64, sub * 512:(sub + 1) * 512],
                         ov[sub][0:64, :], ALU.mult)
                vo = p.marker("v")
                p.wait("t", vo[0], vo[1])
                p.wait("s", vo[0], vo[1])

            attn_head(
                p,
                sc_chunks=[(lambda s0, kk=kk, koff=koff:
                            kk[koff:koff + 64, s0:s0 + 128],
                            lambda g0, gw, qq=qq, koff=koff:
                            qq[koff:koff + 64, g0:g0 + gw])],
                get_vchunks=lambda sb_, hl=hl: [
                    (vt[:, sb_ * 260 + hl * 65:sb_ * 260 + hl * 65 + 65], 0, 65)],
                o_cb=o_cb, state=state, exp_scale=0.125)

        vm = p.marker("v")
        p.wait("y", vm[0], vm[1])
        som = p.sem("so")
        for i in range(2):
            e_dma(p, "y", o_dram[256 * q + i * 128:256 * q + (i + 1) * 128, :],
                  ot[i][:], som)
        p.wait("t", som, som.n)
        p.wait("v", som, som.n)
        p.wait("s", som, som.n)

    p.barrier("ytvsg")
    p.close_scope(sc2)

    sc3 = []
    bp_sb, bps, bpt = load_bias_cols(p, bproj, 8, "bpj", scope=sc3)
    p.wait("s", bps, bpt)
    obt = [p.sb("obt", [128, T], BF16, scope=sc3) for _ in range(8)]
    xrs = [p.sb("xrs", [128, T], scope=sc3) for _ in range(2)]
    sch2 = [p.sb("sch", [128, T], scope=sc3) for _ in range(2)]
    lp3 = lin_pool(p, 8, scope=sc3)
    olsem = p.sem("obl")
    for k in range(8):
        e_dma(p, "y", obt[k][:], o_dram[k * 128:(k + 1) * 128, :], olsem)
    p.wait("t", olsem, olsem.n)
    xlsem2 = [p.sem("xla"), p.sem("xlb")]
    ostm2 = [p.sem("osta"), p.sem("ostb")]
    xl_t = {}
    ev_done = {}
    st_done = {}

    def ev_proj(mb, tc, t0, tw, psum, tmark):
        if tc == 0:
            if mb == 0:
                for m2 in (0, 1):
                    _, xl_t[m2] = e_dma(p, "y", xrs[m2 % 2][:],
                                        xd[m2 * 128:(m2 + 1) * 128, :],
                                        xlsem2[m2 % 2])
            elif mb + 1 < 8:
                p.wait("y", ev_done[mb - 1][0], ev_done[mb - 1][1])
                _, xl_t[mb + 1] = e_dma(p, "y", xrs[(mb + 1) % 2][:],
                                        xd[(mb + 1) * 128:(mb + 2) * 128, :],
                                        xlsem2[(mb + 1) % 2])
        sch = sch2[mb % 2]
        p.wait("s", tmark[0], tmark[1])
        if tc == 0:
            p.wait("s", xlsem2[mb % 2], xl_t[mb])
            if mb >= 2:
                p.wait("s", ostm2[mb % 2], st_done[mb - 2])
        e_act(p, sch[:, t0:t0 + tw], psum, AF.Identity,
              bias=bp_sb[:, mb:mb + 1])
        sm = p.marker("s")
        p.wait("v", sm[0], sm[1])
        e_tt(p, sch[:, t0:t0 + tw], sch[:, t0:t0 + tw],
             xrs[mb % 2][:, t0:t0 + tw], ALU.add)
        vm = p.marker("v")
        ev_done[mb] = vm
        p.wait("y", vm[0], vm[1])
        e_dma(p, "y", s_out[mb * 128:(mb + 1) * 128, t0:t0 + tw],
              sch[:, t0:t0 + tw], ostm2[mb % 2])
        st_done[mb] = ostm2[mb % 2].n
        return vm

    linear_fm(p, lp3, wproj, obt, D, D, ev_proj)
    p.close_scope(sc3)
    if own:
        final_wait_all(p)
        return p.finalize()


def build_L3f(ctx=None):
    own = ctx is None
    if own:
        nc = bass.Bass("TRN2", target_bir_lowering=False)
        p = P(nc)
        io = None
    else:
        nc, p, io = ctx
    xn = _dram(nc, io, "xn", [D, T], F32, "ExternalInput")
    sIn = _dram(nc, io, "sin", [D, T], F32, "ExternalInput")
    qT = _dram(nc, io, "qT", [6, D, D], I32, "ExternalInput")
    scT = _dram(nc, io, "scT", [6, D, D], F32, "ExternalInput")
    biases = _dram(nc, io, "biases", [6, D], F32, "ExternalInput")
    laT = _dram(nc, io, "laT", [6, D, 32], F32, "ExternalInput")
    lbT = _dram(nc, io, "lbT", [6, 32, D], F32, "ExternalInput")
    lng = _dram(nc, io, "lng", [3, D], F32, "ExternalInput")
    lnb = _dram(nc, io, "lnb", [3, D], F32, "ExternalInput")
    x_out = _dram(nc, io, "x4", [D, T], F32, "ExternalOutput")
    h_dram = _dram(nc, io, "hscr", [D, T], F32, "Internal")

    hcp = p.sem("hcp")
    for dd in range(8):
        e_dma(p, "y", h_dram[dd * 128:(dd + 1) * 128, :],
              xn[dd * 128:(dd + 1) * 128, :], hcp)

    A_b16 = [p.sb("Ab", [128, T], BF16) for _ in range(8)]
    Y_b16 = [p.sb("Yb", [128, T], BF16) for _ in range(8)]
    lam_b = p.sb("lamb", [32, T], BF16)
    lam_f = p.sb("lamf", [32, 512])
    lbT_sb = p.sb("lbTs", [32, D], BF16)
    qstage = [p.sb("qst", [128, 1024], I32) for _ in range(2)]
    sstage = [p.sb("sst", [128, 1024]) for _ in range(2)]
    gvec = p.sb("gv", [128, 8])
    bvec = p.sb("bv", [128, 8])
    bias_sb = p.sb("bias", [128, 8])
    tchunk = p.sb("tch", [128, 512])
    np_ = norm_pool(p)
    lp = lin_pool(p, 8)
    dqsem = [p.sem("dqa"), p.sem("dqb")]

    def dequant_wfn2(j):
        def w_fn(mb, wsem):
            m0 = mb * 128
            sl = qstage[mb % 2]
            ss = sstage[mb % 2]
            dsem = dqsem[mb % 2]
            _, t1 = e_dma(p, "g", sl[:].rearrange("p (kc m) -> p kc m", kc=8),
                          qT[j, :, m0:m0 + 128].rearrange(
                              "(kc p) m -> p kc m", p=128), dsem)
            _, t2 = e_dma(p, "g", ss[:].rearrange("p (kc m) -> p kc m", kc=8),
                          scT[j, :, m0:m0 + 128].rearrange(
                              "(kc p) m -> p kc m", p=128), dsem)
            p.wait("v", dsem, t2)
            e_ts(p, sl[:].bitcast(F32), sl[:], 1.0 / 7.5, -1.0, op0=ALU.mult,
                 op1=ALU.add)
            e_tt(p, lp["wbuf"][mb % 2][:, 0:1024], sl[:].bitcast(F32), ss[:],
                 ALU.mult)
            vm = p.marker("v")
            wsem.n += 0
            p.wait("t", vm[0], vm[1])
            return 0
        return w_fn

    lnsem = p.sem("lnld")
    accsem2 = [p.sem("acca"), p.sem("accb")]
    ybounce = [p.sb("ybnc", [128, 512]) for _ in range(2)]

    for blk in range(3):
        j0, j1 = 2 * blk, 2 * blk + 1
        p.barrier("ytvsg")
        ald = p.sem("ald")
        for dd in range(8):
            e_dma(p, "g", A_b16[dd][:], h_dram[dd * 128:(dd + 1) * 128, :],
                  ald)
        a_t = ald.n
        for c in range(8):
            e_dma(p, "y", gvec[:, c:c + 1],
                  lng[blk, c * 128:(c + 1) * 128].unsqueeze(1), lnsem)
            _, g2 = e_dma(p, "y", bvec[:, c:c + 1],
                          lnb[blk, c * 128:(c + 1) * 128].unsqueeze(1), lnsem)
        p.wait("t", ald, a_t)
        p.wait("v", lnsem, g2)

        bsem = p.sem("bsx")
        for c in range(8):
            _, bt = e_dma(p, "y", bias_sb[:, c:c + 1],
                          biases[j0, c * 128:(c + 1) * 128].unsqueeze(1), bsem)
        lamsem = p.sem("lamx")
        la_sb = p.sb("lasb%d" % blk, [128, 8 * 32], BF16)
        _, lt = e_dma(p, "g", la_sb[:].rearrange("p (kc m) -> p kc m", kc=8),
                      laT[j0].rearrange("(kc p) m -> p kc m", p=128), lamsem)
        p.wait("t", lamsem, lt)

        def ev_lam(mb, tc, t0, tw, psum, tmark):
            p.wait("s", tmark[0], tmark[1])
            e_act(p, lam_b[0:32, t0:t0 + tw], psum, AF.Identity)
            return p.marker("s")
        lam_marks = []
        for tc in range(T // 512):
            t0 = tc * 512
            pp = lp["ps"][tc % 2]
            if lp["prev"][tc % 2] is not None:
                p.wait("t", lp["prev"][tc % 2][0], lp["prev"][tc % 2][1])
            for k in range(8):
                e_mm(p, pp[0:32, :], la_sb[:, k * 32:(k + 1) * 32],
                     A_b16[k][:, t0:t0 + 512], start=(k == 0), stop=(k == 7))
            tm = p.marker("t")
            lp["prev"][tc % 2] = ev_lam(None, tc, t0, 512, pp[0:32, :], tm)
        lbsem = p.sem("lbx")
        _, lbt = e_dma(p, "g", lbT_sb[:], lbT[j0], lbsem)
        p.wait("t", lbsem, lbt)
        p.wait("t", bsem, bt)

        def extra_lora(mb, tc, psum, t0, tw, mw):
            e_mm(p, psum, lbT_sb[0:32, mb * 128:mb * 128 + mw],
                 lam_b[0:32, t0:t0 + tw], start=False, stop=True)

        ev_y1 = evac_to_tiles(p, Y_b16, AF.Identity,
                              bias=lambda mb: bias_sb[:, mb:mb + 1])
        linear_fm(p, lp, None, A_b16, D, D, ev_y1, extra_k=extra_lora,
                  w_fn=dequant_wfn2(j0))
        p.barrier("ytvsg")

        def ln_apply(d, tc, t0, tw, xa, mu_bc, rstd_bc):
            e_tt(p, tchunk[:, 0:tw], xa, mu_bc, ALU.subtract)
            e_tt(p, tchunk[:, 0:tw], tchunk[:, 0:tw], rstd_bc, ALU.mult)
            e_ts(p, tchunk[:, 0:tw], tchunk[:, 0:tw], gvec[:, d:d + 1],
                 bvec[:, d:d + 1], op0=ALU.mult, op1=ALU.add)
            vm = p.marker("v")
            p.wait("s", vm[0], vm[1])
            e_act(p, A_b16[d][:, t0:t0 + tw], tchunk[:, 0:tw], AF.Gelu)
            sm = p.marker("s")
            p.wait("v", sm[0], sm[1])

        norm_feat(p, np_, ("sb", Y_b16), ln_apply, 1e-5)

        for c in range(8):
            _, bt2 = e_dma(p, "y", bias_sb[:, c:c + 1],
                          biases[j1, c * 128:(c + 1) * 128].unsqueeze(1), bsem)
        _, lt2 = e_dma(p, "g", la_sb[:].rearrange("p (kc m) -> p kc m", kc=8),
                       laT[j1].rearrange("(kc p) m -> p kc m", p=128), lamsem)
        p.wait("t", lamsem, lt2)
        for tc in range(T // 512):
            t0 = tc * 512
            pp = lp["ps"][tc % 2]
            if lp["prev"][tc % 2] is not None:
                p.wait("t", lp["prev"][tc % 2][0], lp["prev"][tc % 2][1])
            for k in range(8):
                e_mm(p, pp[0:32, :], la_sb[:, k * 32:(k + 1) * 32],
                     A_b16[k][:, t0:t0 + 512], start=(k == 0), stop=(k == 7))
            tm = p.marker("t")
            lp["prev"][tc % 2] = ev_lam(None, tc, t0, 512, pp[0:32, :], tm)
        _, lbt2 = e_dma(p, "g", lbT_sb[:], lbT[j1], lbsem)
        p.wait("t", lbsem, lbt2)
        p.wait("t", bsem, bt2)

        st_marks = [None, None]

        def ev_y2(mb, tc, t0, tw, psum, tmark):
            u = (mb * (T // 512) + tc) % 2
            p.wait("s", tmark[0], tmark[1])
            if st_marks[u] is not None:
                p.wait("s", accsem2[u], st_marks[u])
            e_act(p, ybounce[u][:, 0:tw], psum, AF.Identity,
                  bias=bias_sb[:, mb:mb + 1])
            sm = p.marker("s")
            p.wait("g", sm[0], sm[1])
            p.op("g", lambda eng, mb=mb, t0=t0, tw=tw, u=u: eng.dma_start(
                out=h_dram[mb * 128:(mb + 1) * 128, t0:t0 + tw],
                in_=ybounce[u][:, 0:tw], accum_op=ALU.add), accsem2[u], 16,
                kind="d")
            st_marks[u] = accsem2[u].n
            return sm
        p.dma_sems.add(accsem2[0])
        p.dma_sems.add(accsem2[1])
        linear_fm(p, lp, None, A_b16, D, D, ev_y2, extra_k=extra_lora,
                  w_fn=dequant_wfn2(j1))
        p.barrier("ytvsg")

    fsem = p.sem("fin")
    fst2 = [p.sem("osta"), p.sem("ostb")]
    sch = [p.sb("fch", [128, 1024]) for _ in range(2)]
    hch = [p.sb("fhh", [128, 1024]) for _ in range(2)]
    fl = {}
    fs_t = {}
    nhalf = T // 1024
    for dd in range(8):
        for half in range(nhalf):
            u = (dd * nhalf + half) % 2
            c0 = half * 1024
            key = dd * nhalf + half
            if key >= 2:
                pk = key - 2
                p.wait("y", fl[pk][0], fl[pk][1])
                p.wait("y", fst2[u], fs_t[pk])
            _, t1 = e_dma(p, "y", sch[u][:], sIn[dd * 128:(dd + 1) * 128,
                                                 c0:c0 + 1024], fsem)
            _, t2 = e_dma(p, "y", hch[u][:], h_dram[dd * 128:(dd + 1) * 128,
                                                    c0:c0 + 1024], fsem)
            p.wait("v", fsem, t2)
            e_tt(p, sch[u][:], sch[u][:], hch[u][:], ALU.add)
            vm = p.marker("v")
            p.wait("y", vm[0], vm[1])
            _, ft = e_dma(p, "y", x_out[dd * 128:(dd + 1) * 128, c0:c0 + 1024],
                          sch[u][:], fst2[u])
            fl[key] = vm
            fs_t[key] = ft
    if own:
        final_wait_all(p)
        return p.finalize()


def build_L0f(ctx=None):
    QB = 128.0
    own = ctx is None
    if own:
        nc = bass.Bass("TRN2", target_bir_lowering=False)
        p = P(nc)
        io = None
    else:
        nc, p, io = ctx
    xn = _dram(nc, io, "xn", [D, T], F32, "ExternalInput")
    sIn = _dram(nc, io, "sin", [D, T], F32, "ExternalInput")
    wd = _dram(nc, io, "weight", [D, D], F32, "ExternalInput")
    lng = _dram(nc, io, "lng", [D], F32, "ExternalInput")
    lnb = _dram(nc, io, "lnb", [D], F32, "ExternalInput")
    x_out = _dram(nc, io, "x1", [D, T], F32, "ExternalOutput")
    scal_d = _dram(nc, io, "scal", [1, 4], F32, "Internal")

    gvec, _, _ = load_bias_cols(p, lng, 8, "gv")
    bvec, bs, bt = load_bias_cols(p, lnb, 8, "bv")
    p.wait("v", bs, bt)

    xn2d = _dram(nc, io, "xn2scr", [D, T], F32, "Internal")
    xq = [p.sb("xq", [128, T], BF16) for _ in range(8)]
    mx = p.sb("mx", [128, 8])
    mxc = p.sb("mxc", [128, 1])
    mx1 = p.sb("mx1", [1, 1])
    svec = p.sb("svec", [128, 4])
    wsums = p.sb("wsums", [1, 1024])
    wtot = p.sb("wtot", [1, 1])
    wstage = [p.sb("wst", [128, 1024]) for _ in range(2)]
    np_ = norm_pool(p)
    sc0 = []
    xch = [p.sb("xchf", [128, 512 * 8], scope=sc0) for _ in range(2)]
    e_memset(p, mx[:], 0.0)

    x2ch = [p.sb("x2ch", [128, 512 * 8], scope=sc0) for _ in range(2)]
    stx2 = [p.sem("sxa"), p.sem("sxb")]
    st_m2 = [None, None]

    def ln_apply(d, tc, t0, tw, xa, mu_bc, rstd_bc):
        ch = x2ch[tc % 2]
        if d == 0 and st_m2[tc % 2] is not None:
            p.wait("v", stx2[tc % 2], st_m2[tc % 2])
        dst = ch[:, d * 512:d * 512 + tw]
        e_tt(p, dst, xa, mu_bc, ALU.subtract)
        e_tt(p, dst, dst, rstd_bc, ALU.mult)
        e_ts(p, dst, dst, gvec[:, d:d + 1], bvec[:, d:d + 1], op0=ALU.mult,
             op1=ALU.add)
        p.op("v", lambda eng, dst=dst: eng.tensor_reduce(
            out=mxc[:], in_=dst, axis=AX.X, op=ALU.max,
            apply_absolute_value=True))
        e_tt(p, mx[:, d:d + 1], mx[:, d:d + 1], mxc[:], ALU.max)
        if d == 7:
            vm = p.marker("v")
            p.wait("y", vm[0], vm[1])
            for dd in range(8):
                e_dma(p, "y", xn2d[dd * 128:(dd + 1) * 128, t0:t0 + tw],
                      ch[:, dd * 512:dd * 512 + tw], stx2[tc % 2])
            st_m2[tc % 2] = stx2[tc % 2].n

    norm_feat(p, np_, ("dram", xn), ln_apply, 1e-3, xch=xch)

    p.op("v", lambda eng: eng.tensor_reduce(out=mxc[:], in_=mx[:], axis=AX.X,
                                            op=ALU.max))
    vm = p.marker("v")
    p.wait("g", vm[0], vm[1])
    p.op("g", lambda eng: eng.tensor_reduce(out=mx1[:], in_=mxc[:], axis=AX.C,
                                            op=ALU.max))
    gm = p.marker("g")

    wls = [p.sem("wla"), p.sem("wlb")]
    wlt = {}
    for mb in range(2):
        _, wlt[mb] = e_dma(p, "y", wstage[mb % 2][:].rearrange(
            "p (kc m) -> p kc m", kc=8),
            wd[:, mb * 128:(mb + 1) * 128].rearrange("(kc p) m -> p kc m",
                                                     p=128), wls[mb % 2])
    for mb in range(8):
        p.wait("s", wls[mb % 2], wlt[mb])
        e_act(p, wstage[mb % 2][:], wstage[mb % 2][:], AF.Abs)
        sm = p.marker("s")
        p.wait("t", sm[0], sm[1])
        for half in range(2):
            e_mm(p, np_["ps_s"][:, 0:512], np_["onesc"][:, :],
                 wstage[mb % 2][:, half * 512:(half + 1) * 512],
                 start=(mb == 0 and half == 0), stop=(mb == 7 and half == 1))
        tm = p.marker("t")
        if mb + 2 < 8:
            p.wait("y", tm[0], tm[1])
            _, wlt[mb + 2] = e_dma(p, "y", wstage[mb % 2][:].rearrange(
                "p (kc m) -> p kc m", kc=8),
                wd[:, (mb + 2) * 128:(mb + 3) * 128].rearrange(
                    "(kc p) m -> p kc m", p=128), wls[mb % 2])
    p.wait("v", tm[0], tm[1])
    e_copy(p, wsums[:, 0:512], np_["ps_s"][:, 0:512])
    p.op("v", lambda eng: eng.tensor_reduce(out=wtot[:], in_=wsums[:, 0:512],
                                            axis=AX.X, op=ALU.add))
    e_ts(p, wtot[:], wtot[:], 1.0 / (1024.0 * 1024.0))
    p.wait("v", gm[0], gm[1])
    e_ts(p, svec[0:1, 3:4], wtot[:], 1.0)
    e_ts(p, svec[0:1, 0:1], wtot[:], 1.0, 1e-5, op0=ALU.mult, op1=ALU.add)
    p.op("v", lambda eng: eng.reciprocal(out=svec[0:1, 0:1],
                                         in_=svec[0:1, 0:1]))
    p.op("v", lambda eng: eng.reciprocal(out=svec[0:1, 1:2], in_=mx1[:]))
    e_ts(p, svec[0:1, 1:2], svec[0:1, 1:2], QB)
    e_tt(p, svec[0:1, 2:3], mx1[:], wtot[:], ALU.mult)
    e_ts(p, svec[0:1, 2:3], svec[0:1, 2:3], 1.0 / QB)
    vm2 = p.marker("v")
    p.wait("y", vm2[0], vm2[1])
    scs = p.sem("scst")
    e_dma(p, "y", scal_d[:, :], svec[0:1, 0:4], scs)
    p.wait("y", scs, scs.n)
    _, sct = e_dma(p, "y", svec[:, 0:4],
                   scal_d[0:1, :].partition_broadcast(128).squeeze(1), scs)
    p.wait("v", scs, sct)
    p.wait("s", scs, sct)

    xqsem = [p.sem("xqa2"), p.sem("xqb2")]
    xqt = {}
    for tc in range(T // 512):
        for d in range(8):
            _, qt_ = e_dma(p, "y", x2ch[tc % 2][:, d * 512:(d + 1) * 512],
                           xn2d[d * 128:(d + 1) * 128, tc * 512:(tc + 1) * 512],
                           xqsem[tc % 2])
        xqt[tc] = qt_
        if tc >= 1:
            p.wait("v", xqsem[(tc - 1) % 2], xqt[tc - 1])
            for d in range(8):
                src = x2ch[(tc - 1) % 2][:, d * 512:(d + 1) * 512]
                e_ts(p, src, src, svec[:, 1:2], None, op0=ALU.mult)
                e_ts(p, xq[d][:, (tc - 1) * 512:tc * 512], src,
                     -(QB - 1e-5), QB - 1e-5, op0=ALU.max, op1=ALU.min)
            vm = p.marker("v")
            p.wait("y", vm[0], vm[1])
    tc = T // 512 - 1
    p.wait("v", xqsem[tc % 2], xqt[tc])
    for d in range(8):
        src = x2ch[tc % 2][:, d * 512:(d + 1) * 512]
        e_ts(p, src, src, svec[:, 1:2], None, op0=ALU.mult)
        e_ts(p, xq[d][:, tc * 512:(tc + 1) * 512], src,
             -(QB - 1e-5), QB - 1e-5, op0=ALU.max, op1=ALU.min)
    p.barrier("ytvsg")

    wqs = [p.sem("wqa"), p.sem("wqb")]

    def wq_fn(mb, wsem):
        sl = wstage[mb % 2]
        dsem = wqs[mb % 2]
        _, t1 = e_dma(p, "g", sl[:].rearrange("p (kc m) -> p kc m", kc=8),
                      wd[:, mb * 128:(mb + 1) * 128].rearrange(
                          "(kc p) m -> p kc m", p=128), dsem)
        p.wait("v", dsem, t1)
        e_ts(p, sl[:], sl[:], svec[:, 0:1], None, op0=ALU.mult)
        e_ts(p, sl[:], sl[:], MAGIC, MAGIC, op0=ALU.add, op1=ALU.subtract)
        e_ts(p, lp["wbuf"][mb % 2][:, 0:1024], sl[:], -1.0, 1.0, op0=ALU.max,
             op1=ALU.min)
        vm = p.marker("v")
        p.wait("t", vm[0], vm[1])
        return 0

    p.close_scope(sc0)
    lp = lin_pool(p, 8, np_ps=4)
    xrs = [p.sb("xrs0", [128, T]) for _ in range(2)]
    sch2 = [p.sb("sch0", [128, T]) for _ in range(2)]
    xlsem2 = [p.sem("xla"), p.sem("xlb")]
    ostm2 = [p.sem("osta"), p.sem("ostb")]
    xl_t = {}
    ev_done = {}
    st_done = {}

    def ev_out(mb, tc, t0, tw, psum, tmark):
        if tc == 0:
            if mb == 0:
                for m2 in (0, 1):
                    _, xl_t[m2] = e_dma(p, "y", xrs[m2 % 2][:],
                                        sIn[m2 * 128:(m2 + 1) * 128, :],
                                        xlsem2[m2 % 2])
            elif mb + 1 < 8:
                p.wait("y", ev_done[mb - 1][0], ev_done[mb - 1][1])
                _, xl_t[mb + 1] = e_dma(p, "y", xrs[(mb + 1) % 2][:],
                                        sIn[(mb + 1) * 128:(mb + 2) * 128, :],
                                        xlsem2[(mb + 1) % 2])
        sch = sch2[mb % 2]
        p.wait("s", tmark[0], tmark[1])
        if tc == 0:
            p.wait("s", xlsem2[mb % 2], xl_t[mb])
            if mb >= 2:
                p.wait("s", ostm2[mb % 2], st_done[mb - 2])
        e_act(p, sch[:, t0:t0 + tw], psum, AF.Identity, scale=svec[:, 2:3])
        sm = p.marker("s")
        p.wait("v", sm[0], sm[1])
        e_tt(p, sch[:, t0:t0 + tw], sch[:, t0:t0 + tw],
             xrs[mb % 2][:, t0:t0 + tw], ALU.add)
        vm = p.marker("v")
        ev_done[mb] = vm
        p.wait("y", vm[0], vm[1])
        e_dma(p, "y", x_out[mb * 128:(mb + 1) * 128, t0:t0 + tw],
              sch[:, t0:t0 + tw], ostm2[mb % 2])
        st_done[mb] = ostm2[mb % 2].n
        return vm

    linear_fm(p, lp, None, xq, D, D, ev_out, w_fn=wq_fn)
    if own:
        final_wait_all(p)
        return p.finalize()


def router_gates(p, lp, xnb, wr_dram, br_dram, E, cT_dram, gate_mode,
                 idn_dram=None):
    wr_sb = [p.sb("wrs", [128, E], BF16) for _ in range(8)]
    brow = p.sb("brow", [1, E], BF16)
    ones_t = p.sb("onetr", [1, T], BF16)
    lg = p.sb("lg", [128, 8])
    m8 = p.sb("m8r", [128, 8])
    gsc = p.sb("gsc", [128, 8])
    cmb = p.sb("cmb", [128, E])
    cmbb = p.sb("cmbb", [128, E], BF16)
    csum = p.sb("csum", [128, 1])
    cps = p.pt("cps", [E, 128], BF16)
    idn = p.sb("idnr", [128, 128], BF16)
    idsem = p.sem("idl")
    e_dma(p, "g", idn[:], idn_dram, idsem)
    p.wait("t", idsem, idsem.n)
    ct_stage = p.sb("ctst", [E, 128], BF16)
    e_memset(p, ones_t[:], 1.0)
    e_memset(p, lg[:], -1e30)
    wsem = p.sem("wrl")
    for k in range(8):
        e_dma(p, "g", wr_sb[k][:], wr_dram[k * 128:(k + 1) * 128, :], wsem)
    if br_dram is not None:
        _, bt = e_dma(p, "g", brow[:], br_dram.unsqueeze(0), wsem)
    p.wait("t", wsem, wsem.n)
    vm0 = p.marker("v")
    p.wait("t", vm0[0], vm0[1])
    cstm = p.sem("cstm")
    for tt in range(T // 128):
        t0 = tt * 128
        pp = lp["ps"][tt % 2]
        if lp["prev"][tt % 2] is not None:
            p.wait("t", lp["prev"][tt % 2][0], lp["prev"][tt % 2][1])
        for k in range(8):
            e_mm(p, pp[0:128, 0:E], xnb[k][:, t0:t0 + 128], wr_sb[k][:, 0:E],
                 start=(k == 0), stop=False)
        e_mm(p, pp[0:128, 0:E], ones_t[0:1, t0:t0 + 128], brow[0:1, 0:E],
             start=False, stop=True)
        tm = p.marker("t")
        p.wait("v", tm[0], tm[1])
        p.op("v", lambda eng, pp=pp: eng.tensor_reduce(
            out=csum[:], in_=pp[0:128, 0:E], axis=AX.X, op=ALU.max,
            negate=True))
        vm = p.marker("v")
        p.wait("s", vm[0], vm[1])
        e_act(p, lg[:, 0:E], pp[0:128, 0:E], AF.Exp, bias=csum[:],
              accum=gsc[:, 0:1])
        sm = p.marker("s")
        p.wait("v", sm[0], sm[1])
        p.op("v", lambda eng: eng.reciprocal(out=csum[:], in_=gsc[:, 0:1]))
        e_ts(p, lg[:, 0:E], lg[:, 0:E], csum[:], None, op0=ALU.mult)
        p.op("v", lambda eng: eng.max(out=m8[:], in_=lg[:]))
        if gate_mode == "ratio":
            e_tt(p, gsc[:, 2:3], m8[:, 0:1], m8[:, 1:2], ALU.add)
            p.op("v", lambda eng: eng.reciprocal(out=gsc[:, 3:4],
                                                 in_=gsc[:, 2:3]))
            e_tt(p, gsc[:, 0:1], m8[:, 0:1], gsc[:, 3:4], ALU.mult)
            e_tt(p, gsc[:, 1:2], m8[:, 1:2], gsc[:, 3:4], ALU.mult)
        else:
            vm = p.marker("v")
            p.wait("s", vm[0], vm[1])
            e_act(p, gsc[:, 4:6], m8[:, 0:2], AF.Exp)
            sm = p.marker("s")
            p.wait("v", sm[0], sm[1])
            e_tt(p, gsc[:, 2:3], gsc[:, 4:5], gsc[:, 5:6], ALU.add)
            p.op("v", lambda eng: eng.reciprocal(out=gsc[:, 3:4],
                                                 in_=gsc[:, 2:3]))
            e_tt(p, gsc[:, 0:1], gsc[:, 4:5], gsc[:, 3:4], ALU.mult)
            e_tt(p, gsc[:, 1:2], gsc[:, 5:6], gsc[:, 3:4], ALU.mult)
        e_ts(p, cmb[:, 0:E], lg[:, 0:E], m8[:, 0:1], None, op0=ALU.is_equal)
        e_ts(p, lg[:, 0:E], lg[:, 0:E], m8[:, 1:2], None, op0=ALU.is_equal)
        e_ts(p, cmb[:, 0:E], cmb[:, 0:E], gsc[:, 0:1], None, op0=ALU.mult)
        e_ts(p, lg[:, 0:E], lg[:, 0:E], gsc[:, 1:2], None, op0=ALU.mult)
        e_tt(p, cmb[:, 0:E], cmb[:, 0:E], lg[:, 0:E], ALU.add)
        e_copy(p, cmbb[:, 0:E], cmb[:, 0:E])
        e_memset(p, lg[:], -1e30)
        vm2 = p.marker("v")
        p.wait("t", vm2[0], vm2[1])
        p.op("t", lambda eng, pp=cps, src=cmbb: eng.transpose(
            pp[0:E, 0:128], src[0:128, 0:E], idn[:]))
        ttm = p.marker("t")
        lp["prev"][tt % 2] = ttm
        p.wait("v", ttm[0], ttm[1])
        p.wait("v", cstm, cstm.n)
        e_copy(p, ct_stage[0:E, :], cps[0:E, 0:128])
        vm3 = p.marker("v")
        p.wait("y", vm3[0], vm3[1])
        e_dma(p, "y", cT_dram[0:E, t0:t0 + 128], ct_stage[0:E, :], cstm)
    p.barrier("ytvsg")


def build_L2f(ctx=None):
    own = ctx is None
    if own:
        nc = bass.Bass("TRN2", target_bir_lowering=False)
        p = P(nc)
        ctx = (nc, p, None)
    else:
        nc, p, _ = ctx
    handles = {}
    for _ in _L2f_stream(ctx, handles):
        pass
    p.barrier("ytvsg")
    _L2f_final(ctx, handles)
    if own:
        final_wait_all(p)
        return p.finalize()


def _L2f_stream(ctx, handles):
    E, F_ = 4, 4096
    nc, p, io = ctx
    xn = _dram(nc, io, "xn", [D, T], F32, "ExternalInput")
    sIn = _dram(nc, io, "sin", [D, T], F32, "ExternalInput")
    rw = _dram(nc, io, "router_w", [D, E], F32, "ExternalInput")
    rb = _dram(nc, io, "router_b", [E], F32, "ExternalInput")
    w1 = _dram(nc, io, "w1", [E, D, F_], F32, "ExternalInput")
    w3 = _dram(nc, io, "w3", [E, D, F_], F32, "ExternalInput")
    w2 = _dram(nc, io, "w2", [E, F_, D], F32, "ExternalInput")
    x_out = _dram(nc, io, "x3", [D, T], F32, "ExternalOutput")
    iden = _dram(nc, io, "iden", [128, 128], F32, "ExternalInput")
    cT = _dram(nc, io, "cTscr", [E, T], BF16, "Internal")
    acc_d = _dram(nc, io, "accscr", [D, T], F32, "Internal")
    xnb_d = _dram(nc, io, "xnbscr", [D, T], BF16, "Internal")
    handles["sin"] = sIn
    handles["x3"] = x_out
    handles["acc"] = acc_d

    scr = []
    p.scope_stack.append(scr)
    xnb = [p.sb("xnb2", [128, T], BF16, scope=scr) for _ in range(8)]
    rpool = {"ps": [p.pt("rps", [128, 512], scope=scr) for _ in range(2)],
             "prev": [None, None]}
    xls = p.sem("xl2")
    for dd in range(8):
        e_dma(p, "g", xnb[dd][:], xn[dd * 128:(dd + 1) * 128, :], xls)
    p.wait("t", xls, xls.n)
    p.wait("v", xls, xls.n)
    p.wait("y", xls, xls.n)
    s_st = p.sem("fin")
    for dd in range(8):
        _, st_t = e_dma(p, "y", xnb_d[dd * 128:(dd + 1) * 128, :],
                        xnb[dd][:], s_st)
    p.wait("y", s_st, st_t)
    router_gates(p, rpool, xnb, rw, rb, E, cT, "ratio", idn_dram=iden)
    p.scope_stack.pop()
    p.close_scope(scr)
    yield "router"

    NFB = 4
    NKG = F_ // (128 * NFB)
    ntc = T // 512
    lfsb = []
    wu1 = p.sb("wu1f", [128, 8 * 512], BF16, scope=lfsb)
    wu3 = [p.sb("wu3f", [128, 8 * 512], BF16, scope=lfsb) for _ in range(2)]
    wd2 = p.sb("wd2f", [128, NFB * 1024], BF16, scope=lfsb)
    xnc = [p.sb("xncf", [128, 8 * 512], BF16, scope=lfsb) for _ in range(2)]
    crp = [p.sb("crpf", [128, 512], BF16, scope=lfsb)]
    hch = [p.sb("hchf", [128, NFB * 512], BF16, scope=lfsb) for _ in range(2)]
    asb = [p.sb("asbf", [128, 512], BF16, scope=lfsb) for _ in range(2)]
    pqb = [p.sb("pqbf", [128, 512], BF16, scope=lfsb) for _ in range(2)]
    obp = [p.sb("obpf", [128, 2 * 512], scope=lfsb) for _ in range(2)]
    lfps = []
    upp = p.pt("uppf", [128, 512], scope=lfps)
    upq = p.pt("upqf", [128, 512], scope=lfps)
    dnp = [p.pt("dnpf", [128, 512], scope=lfps) for _ in range(2)]
    s_w13 = p.sem("w13a")
    s_w3 = [p.sem("xl2"), p.sem("wrl")]
    s_w2 = p.sem("w13b")
    s_xn = [p.sem("sxa"), p.sem("sxb")]
    s_cr = [p.sem("cra"), p.sem("crb")]
    s_ac = [p.sem("acca"), p.sem("accb")]

    units = [(e, kg, tc) for e in range(E) for kg in range(NKG)
             for tc in range(ntc)]
    NU = len(units)

    def load_w1(e, kg, gate):
        if gate is not None:
            p.wait("g", gate[0], gate[1])
        f0 = kg * NFB * 128
        _, t1 = e_dma(p, "g", wu1[:].rearrange("p (kc m) -> p kc m", kc=8),
                      w1[e, :, f0:f0 + NFB * 128].rearrange(
                          "(kc p) m -> p kc m", p=128), s_w13)
        return t1

    def load_w3(e, kg, kb, gate):
        if gate is not None:
            p.wait("g", gate[0], gate[1])
        f0 = kg * NFB * 128
        _, t2 = e_dma(p, "g", wu3[kb][:].rearrange("p (kc m) -> p kc m", kc=8),
                      w3[e, :, f0:f0 + NFB * 128].rearrange(
                          "(kc p) m -> p kc m", p=128), s_w3[kb])
        return t2

    def load_w2(e, kg, gate):
        if gate is not None:
            p.wait("g", gate[0], gate[1])
        f0 = kg * NFB * 128
        _, t2 = e_dma(p, "g", wd2[:].rearrange("p (kc m) -> p kc m", kc=NFB),
                      w2[e, f0:f0 + NFB * 128, :].rearrange(
                          "(kc p) m -> p kc m", p=128), s_w2)
        return t2

    def load_xn(ui):
        tc = units[ui][2]
        b = ui % 2
        if xn_use[b] is not None:
            p.wait("y", xn_use[b][0], xn_use[b][1])
        _, t_ = e_dma(p, "y", xnc[b][:].rearrange("p (kc m) -> p kc m", kc=8),
                      xnb_d[:, tc * 512:(tc + 1) * 512].rearrange(
                          "(kc p) m -> p kc m", p=128), s_xn[b])
        xn_tgt[b] = t_

    def load_cr(ui):
        e, _, tc = units[ui]
        b = ui % 2
        if crp_use[0] is not None:
            p.wait("g", crp_use[0][0], crp_use[0][1])
        _, t_ = e_dma(p, "g", crp[0][:],
                      cT[e:e + 1, tc * 512:(tc + 1) * 512]
                      .partition_broadcast(128).squeeze(1), s_cr[b])
        cr_tgt[b] = t_

    xn_use = [None, None]
    crp_use = [None, None]
    xn_tgt = [None, None]
    cr_tgt = [None, None]
    w1_tgt = load_w1(0, 0, None)
    w3_tgt = [load_w3(0, 0, 0, None), None]
    w13_use = [None, None]
    w2_tgt = load_w2(0, 0, None)
    load_xn(0)
    load_cr(0)
    up_free = [None, None]
    dn_free = [None, None]
    hch_wr = [None, None]
    hch_rd = [None, None]
    ob_last = [None, None]
    regT = {}
    vread = [None, None]
    w13_new = [True]
    w2_new = [True]
    yield "steady"

    npair = [0]
    for ui, (e, kg, tc) in enumerate(units):
        sl = ui % 2
        kgi = ui // ntc
        kb = kgi % 2
        for half in range(2):
            if half == 0:
                if ui + 1 < NU:
                    load_xn(ui + 1)
                if tc == 0 and ui + ntc < NU:
                    e2, kg2, _ = units[ui + ntc]
                    w3_tgt[1 - kb] = load_w3(e2, kg2, 1 - kb,
                                             w13_use[1 - kb])
                if w13_new[0]:
                    p.wait("t", s_w13, w1_tgt)
                    p.wait("t", s_w3[kb], w3_tgt[kb])
                    w13_new[0] = False
                p.wait("t", s_xn[sl], xn_tgt[sl])
                if hch_rd[sl] is not None:
                    p.wait("v", hch_rd[sl][0], hch_rd[sl][1])
            first_v = (half == 0)
            for fb in (2 * half, 2 * half + 1):
                if up_free[0] is not None:
                    p.wait("t", up_free[0][0], up_free[0][1])
                for k in range(8):
                    e_mm(p, upp[0:128, 0:512],
                         wu1[:, k * 512 + fb * 128:k * 512 + (fb + 1) * 128],
                         xnc[sl][:, k * 512:(k + 1) * 512], start=(k == 0),
                         stop=(k == 7))
                tma = p.marker("t")
                if up_free[1] is not None:
                    p.wait("t", up_free[1][0], up_free[1][1])
                for k in range(8):
                    e_mm(p, upq[0:128, 0:512],
                         wu3[kb][:, k * 512 + fb * 128:k * 512 + (fb + 1) * 128],
                         xnc[sl][:, k * 512:(k + 1) * 512], start=(k == 0),
                         stop=(k == 7))
                tmb = p.marker("t")
                p.wait("s", tma[0], tma[1])
                if vread[fb % 2] is not None:
                    p.wait("s", vread[fb % 2][0], vread[fb % 2][1])
                e_act(p, asb[fb % 2][:], upp[0:128, 0:512], AF.Silu)
                sma = p.marker("s")
                p.wait("s", tmb[0], tmb[1])
                e_act(p, pqb[fb % 2][:], upq[0:128, 0:512], AF.Identity)
                smb = p.marker("s")
                up_free[0], up_free[1] = sma, smb
                p.wait("v", sma[0], sma[1])
                p.wait("v", smb[0], smb[1])
                if first_v:
                    p.wait("v", s_cr[sl], cr_tgt[sl])
                    first_v = False
                hd = hch[sl][:, fb * 512:(fb + 1) * 512]
                e_tt(p, hd, asb[fb % 2][:], pqb[fb % 2][:], ALU.mult)
                e_tt(p, hd, hd, crp[0][:], ALU.mult)
                vread[fb % 2] = p.marker("v")
            if half == 1:
                hch_wr[sl] = p.marker("v")
                crp_use[0] = hch_wr[sl]
                xn_use[sl] = p.marker("t")
                if ui + 1 < NU:
                    load_cr(ui + 1)
                if tc == ntc - 1:
                    w13_use[kb] = xn_use[sl]
                    w13_new[0] = True
                    if ui + 1 < NU:
                        e2, kg2, _ = units[ui + 1]
                        w1_tgt = load_w1(e2, kg2, p.marker("t"))
            yield ("up", ui, half)
        for half in range(2):
            if half == 0:
                p.wait("t", hch_wr[sl][0], hch_wr[sl][1])
                if w2_new[0]:
                    p.wait("t", s_w2, w2_tgt)
                    w2_new[0] = False
            for mbp in (2 * half, 2 * half + 1):
                ob = npair[0] % 2
                npair[0] += 1
                for sub in range(2):
                    mb = mbp * 2 + sub
                    dn = dnp[mb % 2]
                    if dn_free[mb % 2] is not None:
                        p.wait("t", dn_free[mb % 2][0], dn_free[mb % 2][1])
                    for k in range(NFB):
                        e_mm(p, dn[0:128, 0:512],
                             wd2[:, k * 1024 + mb * 128:
                                 k * 1024 + (mb + 1) * 128],
                             hch[sl][:, k * 512:(k + 1) * 512],
                             start=(k == 0), stop=(k == NFB - 1))
                    tmd = p.marker("t")
                    p.wait("s", tmd[0], tmd[1])
                    if ob_last[ob] is not None and sub == 0:
                        p.wait("s", s_ac[ob], ob_last[ob])
                    e_act(p, obp[ob][:, sub * 512:(sub + 1) * 512],
                          dn[0:128, 0:512], AF.Identity)
                    dn_free[mb % 2] = p.marker("s")
                sm = dn_free[1]
                p.wait("g", sm[0], sm[1])
                reg = (mbp, tc)
                if reg in regT:
                    rs, rt = regT[reg]
                    p.wait("g", rs, rt)
                    p.op("g", lambda eng, mbp=mbp, tc=tc, ob=ob:
                         eng.dma_start(
                             out=acc_d[mbp * 256:(mbp + 1) * 256,
                                       tc * 512:(tc + 1) * 512].rearrange(
                                 "(kc p) m -> p kc m", p=128),
                             in_=obp[ob][:].rearrange("p (kc m) -> p kc m",
                                                      kc=2),
                             accum_op=ALU.add), s_ac[ob], 16, kind="d")
                else:
                    p.op("g", lambda eng, mbp=mbp, tc=tc, ob=ob:
                         eng.dma_start(
                             out=acc_d[mbp * 256:(mbp + 1) * 256,
                                       tc * 512:(tc + 1) * 512].rearrange(
                                 "(kc p) m -> p kc m", p=128),
                             in_=obp[ob][:].rearrange("p (kc m) -> p kc m",
                                                      kc=2)), s_ac[ob], 16,
                         kind="d")
                p.dma_sems.add(s_ac[ob])
                ob_last[ob] = s_ac[ob].n
                regT[reg] = (s_ac[ob], s_ac[ob].n)
            if half == 1:
                hch_rd[sl] = p.marker("t")
                if tc == ntc - 1 and ui + 1 < NU:
                    e2, kg2, _ = units[ui + 1]
                    w2_tgt = load_w2(e2, kg2, p.marker("t"))
                    w2_new[0] = True
            yield ("dn", ui, half)
    if ob_last[0] is not None:
        p.wait("g", s_ac[0], ob_last[0])
    if ob_last[1] is not None:
        p.wait("g", s_ac[1], ob_last[1])
    p.close_scope(lfps)
    p.close_scope(lfsb)


def _L2f_final(ctx, handles):
    nc, p, io = ctx
    sIn, acc_d, x_out = handles["sin"], handles["acc"], handles["x3"]
    fsem = p.sem("fin")
    fst2 = [p.sem("osta"), p.sem("ostb")]
    sch = [p.sb("fch2", [128, 1024]) for _ in range(2)]
    hch = [p.sb("fhh2", [128, 1024]) for _ in range(2)]
    fl = {}
    fs_t = {}
    nhalf = T // 1024
    for dd in range(8):
        for half in range(nhalf):
            u = (dd * nhalf + half) % 2
            c0 = half * 1024
            key = dd * nhalf + half
            if key >= 2:
                pk = key - 2
                p.wait("y", fl[pk][0], fl[pk][1])
                p.wait("y", fst2[u], fs_t[pk])
            _, t1 = e_dma(p, "y", sch[u][:],
                          sIn[dd * 128:(dd + 1) * 128, c0:c0 + 1024], fsem)
            _, t2 = e_dma(p, "y", hch[u][:],
                          acc_d[dd * 128:(dd + 1) * 128, c0:c0 + 1024], fsem)
            p.wait("v", fsem, t2)
            e_tt(p, sch[u][:], sch[u][:], hch[u][:], ALU.add)
            vm = p.marker("v")
            p.wait("y", vm[0], vm[1])
            _, ft = e_dma(p, "y", x_out[dd * 128:(dd + 1) * 128, c0:c0 + 1024],
                          sch[u][:], fst2[u])
            fl[key] = vm
            fs_t[key] = ft


def build_L1f(ctx=None):
    E, F_ = 8, 1024
    dc = 256
    own = ctx is None
    if own:
        nc = bass.Bass("TRN2", target_bir_lowering=False)
        p = P(nc)
        io = None
    else:
        nc, p, io = ctx
    xn = _dram(nc, io, "xn", [D, T], F32, "ExternalInput")
    sIn = _dram(nc, io, "sin", [D, T], F32, "ExternalInput")
    g1d = _dram(nc, io, "g1", [D], F32, "ExternalInput")
    g2d = _dram(nc, io, "g2", [D], F32, "ExternalInput")
    WqT = _dram(nc, io, "WqT", [D, D], F32, "ExternalInput")
    WkcT = _dram(nc, io, "WkcT", [D, dc], F32, "ExternalInput")
    WvcT = _dram(nc, io, "WvcT", [D, dc], F32, "ExternalInput")
    WoT = _dram(nc, io, "WoT", [D, D], F32, "ExternalInput")
    WrT = _dram(nc, io, "WrT", [D, E], F32, "ExternalInput")
    ebias = _dram(nc, io, "ebias", [E], F32, "ExternalInput")
    sW1T = _dram(nc, io, "sW1T", [D, F_], F32, "ExternalInput")
    sW2T = _dram(nc, io, "sW2T", [F_, D], F32, "ExternalInput")
    sW3T = _dram(nc, io, "sW3T", [D, F_], F32, "ExternalInput")
    eW1T = _dram(nc, io, "eW1T", [E, D, F_], F32, "ExternalInput")
    eW2T = _dram(nc, io, "eW2T", [E, F_, D], F32, "ExternalInput")
    eW3T = _dram(nc, io, "eW3T", [E, D, F_], F32, "ExternalInput")
    iden = _dram(nc, io, "iden", [128, 128], F32, "ExternalInput")
    x_out = _dram(nc, io, "x2", [D, T], F32, "ExternalOutput")
    X2d = _dram(nc, io, "X2scr", [D, T], F32, "Internal")
    cT = _dram(nc, io, "cTscr", [E, T], BF16, "Internal")
    acc_d = _dram(nc, io, "accscr", [D, T], F32, "Internal")

    g1v, _, _ = load_bias_cols(p, g1d, 8, "g1v")
    g2v, gs, gt = load_bias_cols(p, g2d, 8, "g2v")
    p.wait("v", gs, gt)

    nx = [p.sb("nx1", [128, T], BF16) for _ in range(8)]
    ao_d = _dram(nc, io, "aoscr", [D, T], BF16, "Internal")
    lp = lin_pool(p, 8)
    tmp512 = p.sb("tmp512", [128, 512])
    sc1 = []
    np_ = norm_pool(p, scope=sc1)
    xch = [p.sb("xch1", [128, 512 * 8], scope=sc1) for _ in range(2)]

    def rms_apply(d, tc, t0, tw, xa, mu_bc, rstd_bc):
        e_stt(p, tmp512[:, 0:tw], xa, g1v[:, d:d + 1], rstd_bc,
              ALU.mult, ALU.mult)
        e_copy(p, nx[d][:, t0:t0 + tw], tmp512[:, 0:tw])

    norm_feat(p, np_, ("dram", xn), rms_apply, 1e-6, mean_center=False,
              xch=xch)
    p.close_scope(sc1)

    sc2 = []
    kc = [p.sb("kc1", [128, T], BF16, scope=sc2) for _ in range(2)]
    vt = p.sb("vt1", [128, (T // 128) * 258], BF16, scope=sc2)
    qh = [p.sb("qh1", [128, T], BF16, scope=sc2) for _ in range(2)]
    ao = [p.sb("ao1", [128, T], BF16, scope=sc2) for _ in range(2)]
    wv_sb = [p.sb("wv1", [128, dc], BF16, scope=sc2) for _ in range(8)]
    ones_t = p.sb("onet1", [1, T], BF16, scope=sc2)
    onesq = p.sb("onsq1", [1, 128], scope=sc2)
    o_scr = p.sb("oscr1", [128, 512 * 2], scope=sc2)
    osum = p.sb("osum1", [1, 512], scope=sc2)
    rec = p.sb("rec1", [1, 512], scope=sc2)
    state = dict(
        sc_ps=[p.pt("scp1", [128, 512], scope=sc2) for _ in range(2)],
        pbuf=[p.sb("pb1", [128, 512], BF16, scope=sc2) for _ in range(2)],
        ov_ps=[[p.pt("ovp1", [128, 512], scope=sc2)],
               [p.pt("ovp1b", [128, 512], scope=sc2)],
               [p.pt("ovp1c", [1, 512], scope=sc2)]],
    )
    e_memset(p, ones_t[:], 1.0)
    e_memset(p, onesq[:], 1.0)
    ev_kc = evac_to_tiles(p, kc, AF.Identity)
    linear_fm(p, lp, WkcT, nx, D, dc, ev_kc)
    wsem = p.sem("wvs")
    for k in range(8):
        e_dma(p, "g", wv_sb[k][:], WvcT[k * 128:(k + 1) * 128, :], wsem)
    p.wait("t", wsem, wsem.n)
    for tt in range(T // 128):
        e_memset(p, vt[:, tt * 258 + 256:tt * 258 + 258], 1.0)
    vm0 = p.marker("v")
    p.wait("s", vm0[0], vm0[1])

    def ev_v(tt, psum, tmark):
        p.wait("s", tmark[0], tmark[1])
        e_act(p, vt[:, tt * 258:tt * 258 + 256], psum[:, 0:256], AF.Identity)
        return p.marker("s")
    linear_tm(p, lp, wv_sb, nx, D, dc, ev_v)
    p.barrier("ytvsg")

    X2st = [p.sem("osta"), p.sem("ostb")]
    xnld = [p.sem("xla"), p.sem("xlb")]
    for h in range(4):
        ev_q = evac_to_tiles(p, qh, AF.Identity)
        linear_fm(p, lp, WqT[:, 256 * h:256 * h + 256], nx, D, 256, ev_q)
        p.barrier("ytvsg")

        def o_cb(tg, g0, gw, h=h):
            ov = state["ov_ps"]
            e_act(p, o_scr[:, 0:512], ov[0][0][:, :], AF.Identity)
            e_act(p, o_scr[:, 512:1024], ov[1][0][:, :], AF.Identity)
            e_act(p, osum[:, 0:512], ov[2][0][0:1, :], AF.Identity)
            sm = p.marker("s")
            p.wait("v", sm[0], sm[1])
            p.op("v", lambda eng: eng.reciprocal(out=rec[:, 0:512],
                                                 in_=osum[:, 0:512]))
            vm = p.marker("v")
            p.wait("t", vm[0], vm[1])
            for half in range(2):
                e_mm(p, ov[half][0][:, :], onesq[0:1, 0:128], rec[:, 0:512])
            tb = p.marker("t")
            p.wait("v", tb[0], tb[1])
            for half in range(2):
                e_tt(p, ao[half][:, g0:g0 + 512],
                     o_scr[:, half * 512:(half + 1) * 512],
                     ov[half][0][:, :], ALU.mult)
            vo = p.marker("v")
            p.wait("t", vo[0], vo[1])
            p.wait("s", vo[0], vo[1])

        attn_head(
            p,
            sc_chunks=[(lambda s0, k=0: kc[0][:, s0:s0 + 128],
                        lambda g0, gw: qh[0][:, g0:g0 + gw]),
                       (lambda s0: kc[1][:, s0:s0 + 128],
                        lambda g0, gw: qh[1][:, g0:g0 + gw])],
            get_vchunks=lambda sb_: [
                (vt[:, sb_ * 258:sb_ * 258 + 128], 0, 128),
                (vt[:, sb_ * 258 + 128:sb_ * 258 + 256], 1, 128),
                (vt[:, sb_ * 258 + 256:sb_ * 258 + 257], 2, 1)],
            o_cb=o_cb, state=state, tgsz=512, exp_scale=0.0625)
        hm = p.marker("v")
        p.wait("y", hm[0], hm[1])
        aos = p.sem("aosx")
        for half in range(2):
            e_dma(p, "y", ao_d[(2 * h + half) * 128:(2 * h + half + 1) * 128, :],
                  ao[half][:], aos)
        p.wait("v", aos, aos.n)
        p.wait("s", aos, aos.n)
        p.wait("t", aos, aos.n)
    p.barrier("ytvsg")
    p.close_scope(sc2)

    sc2b = []
    obt = [p.sb("obt1", [128, T], BF16, scope=sc2b) for _ in range(8)]
    obs = p.sem("obl")
    for k in range(8):
        e_dma(p, "y", obt[k][:], ao_d[k * 128:(k + 1) * 128, :], obs)
    p.wait("t", obs, obs.n)
    xrs = [p.sb("xr1", [128, 512], scope=sc2b) for _ in range(2)]
    sch2 = [p.sb("sc1h", [128, T], scope=sc2b) for _ in range(2)]
    xl_t = {}
    ev_done = {}
    st_done = {}

    NTC = T // 512

    def ev_X2(mb, tc, t0, tw, psum, tmark):
        u = (mb * NTC + tc) % 2
        uk = mb * NTC + tc
        if uk == 0:
            for k2 in (0, 1):
                _, xl_t[k2] = e_dma(p, "y", xrs[k2 % 2][:],
                                    xn[0:128, k2 * 512:(k2 + 1) * 512],
                                    xnld[k2 % 2])
        if uk + 2 <= 8 * NTC - 1:
            nk_ = uk + 2
            mb2, tc2 = nk_ // NTC, nk_ % NTC
            p.wait("y", ev_done[uk][0], ev_done[uk][1]) if uk in ev_done else None
        sch = sch2[mb % 2]
        p.wait("s", tmark[0], tmark[1])
        p.wait("s", xnld[u], xl_t[uk])
        if tc == 0 and mb >= 2:
            p.wait("s", X2st[mb % 2], st_done[mb - 2])
        e_act(p, sch[:, t0:t0 + tw], psum, AF.Identity)
        sm = p.marker("s")
        p.wait("v", sm[0], sm[1])
        e_tt(p, sch[:, t0:t0 + tw], sch[:, t0:t0 + tw],
             xrs[u][:, 0:tw], ALU.add)
        vm = p.marker("v")
        ev_done[uk] = vm
        if uk + 2 <= 8 * NTC - 1:
            nk_ = uk + 2
            mb2, tc2 = nk_ // NTC, nk_ % NTC
            p.wait("y", vm[0], vm[1])
            _, xl_t[nk_] = e_dma(p, "y", xrs[u][:],
                                 xn[mb2 * 128:(mb2 + 1) * 128,
                                    tc2 * 512:tc2 * 512 + 512], xnld[u])
        p.wait("y", vm[0], vm[1])
        e_dma(p, "y", X2d[mb * 128:(mb + 1) * 128, t0:t0 + tw],
              sch[:, t0:t0 + tw], X2st[mb % 2])
        st_done[mb] = X2st[mb % 2].n
        return vm

    linear_fm(p, lp, WoT, obt, D, D, ev_X2)
    p.barrier("ytvsg")
    p.close_scope(sc2b)

    sc3 = []
    np2_ = norm_pool(p, scope=sc3)
    xch2 = [p.sb("xch1b", [128, 512 * 8], scope=sc3) for _ in range(2)]

    def rms2_apply(d, tc, t0, tw, xa, mu_bc, rstd_bc):
        e_stt(p, tmp512[:, 0:tw], xa, g2v[:, d:d + 1], rstd_bc,
              ALU.mult, ALU.mult)
        e_copy(p, nx[d][:, t0:t0 + tw], tmp512[:, 0:tw])

    norm_feat(p, np2_, ("dram", X2d), rms2_apply, 1e-6, mean_center=False,
              xch=xch2)
    p.close_scope(sc3)

    lp2 = lin_pool(p, 8, np_ps=4)
    router_gates(p, lp2, nx, WrT, ebias, E, cT, "softmax", idn_dram=iden)

    hb = [p.sb("hb1", [128, T], BF16) for _ in range(8)]
    asb = [p.sb("asb1", [128, 512]) for _ in range(2)]
    crep = [p.sb("crep1", [128, T], BF16) for _ in range(2)]
    ob = [p.sb("ob1", [128, 512]) for _ in range(2)]
    wB = [p.sb("wB1", [128, 8 * 128], BF16) for _ in range(2)]
    crs = [p.sem("cra"), p.sem("crb")]
    w13s = [p.sem("w13a"), p.sem("w13b")]
    accs = [p.sem("acca"), p.sem("accb")]
    acc_t = [None, None]
    first_acc = [True]
    wA = lp2["wbuf"]
    crt = {}

    def moe_pass(e):
        W1 = sW1T if e < 0 else eW1T[e]
        W3 = sW3T if e < 0 else eW3T[e]
        W2 = sW2T if e < 0 else eW2T[e]
        if e >= 0:
            _, crt[e % 2] = e_dma(p, "g", crep[e % 2][:],
                                  cT[e:e + 1, :].partition_broadcast(128)
                                  .squeeze(1), crs[e % 2])
            p.wait("v", crs[e % 2], crt[e % 2])
        wt = {}
        for fb in range(2):
            _, t1 = e_dma(p, "g", wA[fb % 2][:].rearrange(
                "p (kc m) -> p kc m", kc=8),
                W1[:, fb * 128:(fb + 1) * 128].rearrange(
                    "(kc p) m -> p kc m", p=128), w13s[fb % 2])
            _, t2 = e_dma(p, "g", wB[fb % 2][:].rearrange(
                "p (kc m) -> p kc m", kc=8),
                W3[:, fb * 128:(fb + 1) * 128].rearrange(
                    "(kc p) m -> p kc m", p=128), w13s[fb % 2])
            wt[fb] = t2
        for fb in range(8):
            p.wait("t", w13s[fb % 2], wt[fb])
            for tc in range(T // 512):
                t0 = tc * 512
                sl = 2 * (tc % 2)
                pp, pq = lp2["ps"][sl], lp2["ps"][sl + 1]
                if lp2["prev"][sl] is not None:
                    p.wait("t", lp2["prev"][sl][0], lp2["prev"][sl][1])
                for k in range(8):
                    e_mm(p, pp[0:128, 0:512],
                         wA[fb % 2][0:128, k * 128:(k + 1) * 128],
                         nx[k][:, t0:t0 + 512], start=(k == 0), stop=(k == 7))
                tma = p.marker("t")
                if lp2["prev"][sl + 1] is not None:
                    p.wait("t", lp2["prev"][sl + 1][0],
                           lp2["prev"][sl + 1][1])
                for k in range(8):
                    e_mm(p, pq[0:128, 0:512],
                         wB[fb % 2][0:128, k * 128:(k + 1) * 128],
                         nx[k][:, t0:t0 + 512], start=(k == 0), stop=(k == 7))
                tmb = p.marker("t")
                p.wait("s", tma[0], tma[1])
                prevv = lp2["prev"][sl + 1]
                if prevv is not None:
                    p.wait("s", prevv[0], prevv[1])
                e_act(p, asb[tc % 2][:, 0:512], pp[0:128, 0:512], AF.Silu)
                sm_ = p.marker("s")
                lp2["prev"][sl] = sm_
                p.wait("v", tmb[0], tmb[1])
                p.wait("v", sm_[0], sm_[1])
                e_tt(p, hb[fb][:, t0:t0 + 512], asb[tc % 2][:, 0:512],
                     pq[0:128, 0:512], ALU.mult)
                if e >= 0:
                    e_tt(p, hb[fb][:, t0:t0 + 512], hb[fb][:, t0:t0 + 512],
                         crep[e % 2][:, t0:t0 + 512], ALU.mult)
                lp2["prev"][sl + 1] = p.marker("v")
            if fb + 2 < 8:
                gm = p.marker("t")
                p.wait("g", gm[0], gm[1])
                _, t1 = e_dma(p, "g", wA[fb % 2][:].rearrange(
                    "p (kc m) -> p kc m", kc=8),
                    W1[:, (fb + 2) * 128:(fb + 3) * 128].rearrange(
                        "(kc p) m -> p kc m", p=128), w13s[fb % 2])
                _, t2 = e_dma(p, "g", wB[fb % 2][:].rearrange(
                    "p (kc m) -> p kc m", kc=8),
                    W3[:, (fb + 2) * 128:(fb + 3) * 128].rearrange(
                        "(kc p) m -> p kc m", p=128), w13s[fb % 2])
                wt[fb + 2] = t2
        hvm = p.marker("v")
        p.wait("g", hvm[0], hvm[1])
        w2t = {}
        for mb in range(2):
            _, w2t[mb] = e_dma(p, "g", wB[mb % 2][:].rearrange(
                "p (kc m) -> p kc m", kc=8),
                W2[:, mb * 128:(mb + 1) * 128].rearrange(
                    "(kc p) m -> p kc m", p=128), w13s[mb % 2])
        for mb in range(8):
            p.wait("t", w13s[mb % 2], w2t[mb])
            for tc in range(T // 512):
                t0 = tc * 512
                uu = mb * (T // 512) + tc
                u = uu % 2
                u4 = uu % 4
                pp = lp2["ps"][u4]
                if lp2["prev"][u4] is not None:
                    p.wait("t", lp2["prev"][u4][0], lp2["prev"][u4][1])
                for k in range(8):
                    e_mm(p, pp[0:128, 0:512],
                         wB[mb % 2][0:128, k * 128:(k + 1) * 128],
                         hb[k][:, t0:t0 + 512], start=(k == 0), stop=(k == 7))
                tm = p.marker("t")
                p.wait("s", tm[0], tm[1])
                if acc_t[u] is not None:
                    p.wait("s", accs[u], acc_t[u])
                e_act(p, ob[u][:, 0:512], pp[0:128, 0:512], AF.Identity)
                sm = p.marker("s")
                lp2["prev"][u4] = sm
                p.wait("g", sm[0], sm[1])
                if first_acc[0]:
                    p.op("g", lambda eng, mb=mb, t0=t0, u=u: eng.dma_start(
                        out=acc_d[mb * 128:(mb + 1) * 128, t0:t0 + 512],
                        in_=ob[u][:, 0:512]), accs[u], 16, kind="d")
                else:
                    p.op("g", lambda eng, mb=mb, t0=t0, u=u: eng.dma_start(
                        out=acc_d[mb * 128:(mb + 1) * 128, t0:t0 + 512],
                        in_=ob[u][:, 0:512], accum_op=ALU.add), accs[u], 16,
                        kind="d")
                p.dma_sems.add(accs[u])
                acc_t[u] = accs[u].n
            if mb + 2 < 8:
                gm2 = p.marker("t")
                p.wait("g", gm2[0], gm2[1])
                _, w2t[mb + 2] = e_dma(p, "g", wB[mb % 2][:].rearrange(
                    "p (kc m) -> p kc m", kc=8),
                    W2[:, (mb + 2) * 128:(mb + 3) * 128].rearrange(
                        "(kc p) m -> p kc m", p=128), w13s[mb % 2])
        first_acc[0] = False
        gm3 = p.marker("t")
        p.wait("g", gm3[0], gm3[1])

    moe_pass(-1)
    for e in range(E):
        moe_pass(e)
    p.barrier("ytvsg")

    fsem = p.sem("fin")
    fst2 = [p.sem("osta"), p.sem("ostb")]
    sch = [p.sb("fc1", [128, 1024]) for _ in range(2)]
    hch = [p.sb("fh1", [128, 1024]) for _ in range(2)]
    xch3 = [p.sb("fx1", [128, 1024]) for _ in range(2)]
    fl = {}
    fs_t = {}
    nhalf = T // 1024
    for dd in range(8):
        for half in range(nhalf):
            u = (dd * nhalf + half) % 2
            c0 = half * 1024
            key = dd * nhalf + half
            if key >= 2:
                pk = key - 2
                p.wait("y", fl[pk][0], fl[pk][1])
                p.wait("y", fst2[u], fs_t[pk])
            _, t1 = e_dma(p, "y", sch[u][:],
                          sIn[dd * 128:(dd + 1) * 128, c0:c0 + 1024], fsem)
            _, t1b = e_dma(p, "y", xch3[u][:],
                           X2d[dd * 128:(dd + 1) * 128, c0:c0 + 1024], fsem)
            _, t2 = e_dma(p, "y", hch[u][:],
                          acc_d[dd * 128:(dd + 1) * 128, c0:c0 + 1024], fsem)
            p.wait("v", fsem, t2)
            e_tt(p, sch[u][:], sch[u][:], xch3[u][:], ALU.add)
            e_tt(p, sch[u][:], sch[u][:], hch[u][:], ALU.add)
            vm = p.marker("v")
            p.wait("y", vm[0], vm[1])
            _, ft = e_dma(p, "y", x_out[dd * 128:(dd + 1) * 128, c0:c0 + 1024],
                          sch[u][:], fst2[u])
            fl[key] = vm
            fs_t[key] = ft
    if own:
        final_wait_all(p)
        return p.finalize()


def build_L1a(ctx=None):
    DI = 512
    CH = 128
    own = ctx is None
    if own:
        nc = bass.Bass("TRN2", target_bir_lowering=False)
        p = P(nc)
        io = None
    else:
        nc, p, io = ctx
    x1 = _dram(nc, io, "x", [D, T], F32, "ExternalInput")
    Wq = _dram(nc, io, "Wq", [D, DI], F32, "ExternalInput")
    Wk = _dram(nc, io, "Wk", [D, DI], F32, "ExternalInput")
    Wv = _dram(nc, io, "Wv", [D, DI], F32, "ExternalInput")
    Wa = _dram(nc, io, "Wa", [D, DI], F32, "ExternalInput")
    ba = _dram(nc, io, "ba", [DI], F32, "ExternalInput")
    WoT = _dram(nc, io, "Wo", [DI, D], F32, "ExternalInput")
    gam = _dram(nc, io, "gamma", [D], F32, "ExternalInput")
    bet = _dram(nc, io, "beta", [D], F32, "ExternalInput")
    blk2 = _dram(nc, io, "blk2", [128, 2], F32, "ExternalInput")
    blk2b = _dram(nc, io, "blk2b", [2, 128], F32, "ExternalInput")
    triu = _dram(nc, io, "triu", [128, 128], F32, "ExternalInput")
    iden = _dram(nc, io, "iden", [128, 128], F32, "ExternalInput")
    xn_out = _dram(nc, io, "xn", [D, T], F32, "ExternalOutput")
    s_out = _dram(nc, io, "s", [D, T], F32, "ExternalOutput")

    gv, _, _ = load_bias_cols(p, gam, 8, "gv")
    bv, bs, bt = load_bias_cols(p, bet, 8, "bv")
    bav, bas, bat = load_bias_cols(p, ba, 4, "bav")
    p.wait("v", bs, bt)
    p.wait("s", bas, bat)

    lp = lin_pool(p, 8)
    sc1 = []
    np_ = norm_pool(p, scope=sc1)
    xch = [p.sb("xchA", [128, 512 * 8], scope=sc1) for _ in range(2)]
    xnf_ch = [p.sb("xnfA", [128, 512 * 8], scope=sc1) for _ in range(2)]
    stx = [p.sem("sxa"), p.sem("sxb")]
    st_marks = [None, None]

    def ln_apply(d, tc, t0, tw, xa, mu_bc, rstd_bc):
        ch = xnf_ch[tc % 2]
        if d == 0 and st_marks[tc % 2] is not None:
            p.wait("v", stx[tc % 2], st_marks[tc % 2])
        dst = ch[:, d * 512:d * 512 + tw]
        e_tt(p, dst, xa, mu_bc, ALU.subtract)
        e_tt(p, dst, dst, rstd_bc, ALU.mult)
        if d == 7:
            vm = p.marker("v")
            p.wait("y", vm[0], vm[1])
            for dd in range(8):
                e_dma(p, "y", xn_out[dd * 128:(dd + 1) * 128, t0:t0 + tw],
                      ch[:, dd * 512:dd * 512 + tw], stx[tc % 2])
            st_marks[tc % 2] = stx[tc % 2].n

    norm_feat(p, np_, ("dram", x1), ln_apply, 1e-5, xch=xch)
    p.close_scope(sc1)

    xnb2 = [p.sb("xnb2a", [128, T], BF16) for _ in range(8)]
    sc1b = []
    np1b = norm_pool(p, scope=sc1b)
    xchb = [p.sb("xchB", [128, 512 * 8], scope=sc1b) for _ in range(2)]
    tmpb = p.sb("tmpB", [128, 512], scope=sc1b)

    def ln2_apply(d, tc, t0, tw, xa, mu_bc, rstd_bc):
        e_tt(p, tmpb[:, 0:tw], xa, mu_bc, ALU.subtract)
        e_tt(p, tmpb[:, 0:tw], tmpb[:, 0:tw], rstd_bc, ALU.mult)
        e_ts(p, tmpb[:, 0:tw], tmpb[:, 0:tw], gv[:, d:d + 1], bv[:, d:d + 1],
             op0=ALU.mult, op1=ALU.add)
        e_copy(p, xnb2[d][:, t0:t0 + tw], tmpb[:, 0:tw])

    norm_feat(p, np1b, ("dram", xn_out), ln2_apply, 1e-5, xch=xchb)
    p.close_scope(sc1b)

    qh = [p.sb("qhA", [128, T], BF16) for _ in range(2)]
    kh = [p.sb("khA", [128, T], BF16) for _ in range(2)]
    la = [p.sb("laA", [128, T]) for _ in range(2)]
    vtA = p.sb("vtA", [128, (T // 128) * 256], BF16)
    ao = [p.sb("aoA", [128, T], BF16) for _ in range(4)]
    Sst = [p.sb("SstA", [64, 64]) for _ in range(4)]
    Sstb = [p.sb("SstbA", [64, 64], BF16) for _ in range(4)]
    cumt = [p.sb("cumA", [128, CH]) for _ in range(2)]
    ep = [p.sb("epA", [128, CH]) for _ in range(2)]
    em_ = [p.sb("emA", [128, CH]) for _ in range(2)]
    e2_ = [p.sb("e2A", [128, CH]) for _ in range(2)]
    qt_ = [p.sb("qtA", [64, CH], BF16) for _ in range(4)]
    kt_ = [p.sb("ktA", [64, CH], BF16) for _ in range(4)]
    k2_ = [p.sb("k2A", [64, CH], BF16) for _ in range(4)]
    pbA = p.sb("pbA", [128, CH], BF16)
    k2T = p.sb("k2TA", [128, 64], BF16)
    onesA = p.sb("onesA", [128, CH])
    blkA = p.sb("blkA", [128, 2])
    blkB = p.sb("blkB", [2, 128])
    triA = p.sb("triA", [128, 128], BF16)
    idnA = p.sb("idnA", [128, 128], BF16)
    sqA = p.sb("sqA", [128, 512])
    ssA = p.sb("ssA", [2, 512])
    wvA = [p.sb("wvA", [128, 256], BF16) for _ in range(8)]
    e_memset(p, onesA[:], 1.0)
    csem = p.sem("clc")
    csg = p.sem("cstg")
    e_dma(p, "y", blkA[:], blk2, csem)
    e_dma(p, "y", blkB[:], blk2b, csem)
    e_dma(p, "g", triA[:], triu, csg)
    e_dma(p, "g", idnA[:], iden, csg)
    p.wait("t", csem, csem.n)
    p.wait("v", csem, csem.n)
    p.wait("t", csg, csg.n)
    p.wait("v", csg, csg.n)
    scps = p.pt("scpsA", [128, CH])
    ops_ = p.pt("opsA", [64, CH])
    k2ps = p.pt("k2psA", [128, 64], BF16)
    dsps = p.pt("dspsA", [64, 64])
    bcps = p.pt("bcpsA", [128, 512])

    for half in range(2):
        c0col = half * 256
        ev_q = evac_to_tiles(p, qh, AF.Identity)
        linear_fm(p, lp, Wq[:, c0col:c0col + 256], xnb2, D, 256, ev_q)
        def ev_a(mb, tc, t0, tw, psum, tmark):
            p.wait("s", tmark[0], tmark[1])
            e_act(p, sqA[:, 0:tw], psum, AF.Sigmoid,
                  bias=bav[:, 2 * half + mb:2 * half + mb + 1])
            e_act(p, la[mb][:, t0:t0 + tw], sqA[:, 0:tw], AF.Ln)
            return p.marker("s")
        linear_fm(p, lp, Wa[:, c0col:c0col + 256], xnb2, D, 256, ev_a)
        prev_kv = [None]

        def ev_k(mb, tc, t0, tw, psum, tmark):
            p.wait("s", tmark[0], tmark[1])
            if prev_kv[0] is not None:
                p.wait("s", prev_kv[0][0], prev_kv[0][1])
            e_act(p, sqA[:, 0:tw], psum, AF.Square)
            sm = p.marker("s")
            p.wait("t", sm[0], sm[1])
            e_mm(p, bcps[0:2, 0:tw], blkA[0:128, 0:2], sqA[:, 0:tw])
            tm = p.marker("t")
            p.wait("v", tm[0], tm[1])
            e_copy(p, ssA[:, 0:tw], bcps[0:2, 0:tw])
            vm = p.marker("v")
            p.wait("s", vm[0], vm[1])
            e_act(p, ssA[:, 0:tw], ssA[:, 0:tw], AF.Ln)
            e_act(p, ssA[:, 0:tw], ssA[:, 0:tw], AF.Exp, scale=0.5)
            sm2 = p.marker("s")
            p.wait("v", sm2[0], sm2[1])
            e_ts(p, ssA[:, 0:tw], ssA[:, 0:tw], 1e-12, None, op0=ALU.max)
            p.op("v", lambda eng, tw=tw: eng.reciprocal(out=ssA[:, 0:tw],
                                                        in_=ssA[:, 0:tw]))
            vm2 = p.marker("v")
            p.wait("t", vm2[0], vm2[1])
            e_mm(p, bcps[:, 0:tw], blkB[0:2, :], ssA[0:2, 0:tw])
            tb = p.marker("t")
            p.wait("s", tb[0], tb[1])
            e_act(p, sqA[:, 0:tw], psum, AF.Identity)
            sm3 = p.marker("s")
            p.wait("v", sm3[0], sm3[1])
            e_tt(p, kh[mb][:, t0:t0 + tw], sqA[:, 0:tw], bcps[:, 0:tw],
                 ALU.mult)
            prev_kv[0] = p.marker("v")
            return prev_kv[0]
        linear_fm(p, lp, Wk[:, c0col:c0col + 256], xnb2, D, 256, ev_k)
        wsem = p.sem("wvs")
        for k in range(8):
            e_dma(p, "g", wvA[k][:],
                  Wv[k * 128:(k + 1) * 128, c0col:c0col + 256], wsem)
        p.wait("t", wsem, wsem.n)

        def ev_v(tt, psum, tmark):
            p.wait("s", tmark[0], tmark[1])
            e_act(p, vtA[:, tt * 256:tt * 256 + 256], psum[:, 0:256],
                  AF.Identity)
            return p.marker("s")
        linear_tm(p, lp, wvA, xnb2, D, 256, ev_v)
        for hl in range(4):
            e_memset(p, Sst[hl][:], 0.0)
            e_memset(p, Sstb[hl][:], 0.0)
        p.barrier("ytvsg")

        smark = {}
        for c in range(T // CH):
            t0 = c * CH
            u = c % 2
            for tl in range(2):
                p.op("v", lambda eng, tl=tl, t0=t0, u=u: eng.tensor_tensor_scan(
                    out=cumt[tl][:, 0:CH] if u == 0 else cumt[tl][:, 0:CH],
                    data0=onesA[:, 0:CH], data1=la[tl][:, t0:t0 + CH],
                    initial=0.0, op0=ALU.mult, op1=ALU.add))
                vm = p.marker("v")
                p.wait("s", vm[0], vm[1])
                e_act(p, ep[tl][:], cumt[tl][:], AF.Exp)
                sm = p.marker("s")
                p.wait("v", sm[0], sm[1])
                e_ts(p, em_[tl][:], cumt[tl][:], -60.0, None, op0=ALU.max)
                e_ts(p, e2_[tl][:], cumt[tl][:], cumt[tl][:, CH - 1:CH],
                     None, op0=ALU.subtract)
                vm2 = p.marker("v")
                p.wait("s", vm2[0], vm2[1])
                e_act(p, em_[tl][:], em_[tl][:], AF.Exp, scale=-1.0)
                e_act(p, e2_[tl][:], e2_[tl][:], AF.Exp, scale=-1.0)
                sm2 = p.marker("s")
                p.wait("v", sm2[0], sm2[1])
                for sh in range(2):
                    off = sh * 64
                    hl = 2 * tl + sh
                    e_tt(p, qt_[hl][:], qh[tl][off:off + 64, t0:t0 + CH],
                         ep[tl][off:off + 64, :], ALU.mult)
                    e_tt(p, kt_[hl][:], kh[tl][off:off + 64, t0:t0 + CH],
                         em_[tl][off:off + 64, :], ALU.mult)
                    e_tt(p, k2_[hl][:], kh[tl][off:off + 64, t0:t0 + CH],
                         e2_[tl][off:off + 64, :], ALU.mult)
            vql = p.marker("v")
            p.wait("t", vql[0], vql[1])
            for hl in range(4):
                tl, off = hl // 2, (hl % 2) * 64
                vsl = vtA[:, (t0 // 128) * 256 + hl * 64:
                          (t0 // 128) * 256 + hl * 64 + 64]
                e_mm(p, scps[:, 0:CH], kt_[hl][:, :], qt_[hl][:, :])
                tm = p.marker("t")
                p.wait("v", tm[0], tm[1])
                e_tt(p, pbA[:], scps[:, 0:CH], triA[:], ALU.mult)
                vmk = p.marker("v")
                p.wait("t", vmk[0], vmk[1])
                e_mm(p, ops_[0:64, 0:CH], vsl, pbA[:], start=True, stop=False)
                e_mm(p, ops_[0:64, 0:CH], Sstb[hl][:, :], qt_[hl][:, :],
                     start=False, stop=True)
                p.op("t", lambda eng, hl=hl: eng.transpose(
                    k2ps[:, 0:64], k2_[hl][:, :], idnA[0:64, 0:64]))
                tm2 = p.marker("t")
                p.wait("s", tm2[0], tm2[1])
                e_act(p, ao[2 * half + hl // 2][off:off + 64, t0:t0 + CH]
                      if False else sqA[0:64, 0:CH], ops_[0:64, 0:CH],
                      AF.Identity)
                e_act(p, k2T[:, 0:64], k2ps[:, 0:64], AF.Identity)
                sm3 = p.marker("s")
                p.wait("v", sm3[0], sm3[1])
                e_copy(p, ao[2 * half + hl // 2][off:off + 64, t0:t0 + CH],
                       sqA[0:64, 0:CH])
                p.wait("t", sm3[0], sm3[1])
                e_mm(p, dsps[:, :], k2T[:, 0:64], vsl)
                tm3 = p.marker("t")
                p.wait("v", tm3[0], tm3[1])
                e_ts(p, Sst[hl][:, :], Sst[hl][:, :],
                     ep[tl][off:off + 64, CH - 1:CH], None, op0=ALU.mult)
                e_tt(p, Sst[hl][:, :], Sst[hl][:, :], dsps[:, :], ALU.add)
                e_copy(p, Sstb[hl][:, :], Sst[hl][:, :])
                vS = p.marker("v")
                p.wait("t", vS[0], vS[1])
        p.barrier("ytvsg")

    xrs = [la[0][:, 0:512], la[0][:, 512:1024]]
    xrs2 = [la[1][:, 0:512], la[1][:, 512:1024]]
    sch2 = [qh[0][:, 0:1024].bitcast(F32), qh[1][:, 0:1024].bitcast(F32)]
    xnld = [p.sem("xla"), p.sem("xlb")]
    stS = [p.sem("osta"), p.sem("ostb")]
    xl_t = {}
    ev_done = {}
    st_done = {}
    NTC = T // 512

    def ev_s(mb, tc, t0, tw, psum, tmark):
        u = (mb * NTC + tc) % 2
        uk = mb * NTC + tc
        if uk == 0:
            for k2 in (0, 1):
                _, xl_t[k2] = e_dma(p, "y", xrs[k2 % 2],
                                    x1[0:128, k2 * 512:(k2 + 1) * 512],
                                    xnld[k2 % 2])
                e_dma(p, "y", xrs2[k2 % 2],
                      xn_out[0:128, k2 * 512:(k2 + 1) * 512], xnld[k2 % 2])
                xl_t[k2] = xnld[k2 % 2].n
        sch = sch2[u]
        p.wait("s", tmark[0], tmark[1])
        p.wait("s", xnld[u], xl_t[uk])
        if uk >= 2 and (uk - 2) in st_done:
            p.wait("s", stS[u], st_done[uk - 2])
        e_act(p, sch[:, 0:tw], psum, AF.Identity)
        sm = p.marker("s")
        p.wait("v", sm[0], sm[1])
        e_tt(p, sch[:, 0:tw], sch[:, 0:tw], xrs[u][:, 0:tw], ALU.add)
        e_tt(p, sch[:, 0:tw], sch[:, 0:tw], xrs2[u][:, 0:tw], ALU.add)
        vm = p.marker("v")
        ev_done[uk] = vm
        if uk + 2 <= 8 * NTC - 1:
            nk_ = uk + 2
            mb2, tc2 = nk_ // NTC, nk_ % NTC
            p.wait("y", vm[0], vm[1])
            _, _t = e_dma(p, "y", xrs[u][:],
                          x1[mb2 * 128:(mb2 + 1) * 128,
                             tc2 * 512:tc2 * 512 + 512], xnld[u])
            e_dma(p, "y", xrs2[u][:],
                  xn_out[mb2 * 128:(mb2 + 1) * 128,
                         tc2 * 512:tc2 * 512 + 512], xnld[u])
            xl_t[nk_] = xnld[u].n
        p.wait("y", vm[0], vm[1])
        e_dma(p, "y", s_out[mb * 128:(mb + 1) * 128, t0:t0 + tw],
              sch[:, 0:tw], stS[u])
        st_done[uk] = stS[u].n
        return vm

    linear_fm(p, lp, WoT, ao, DI, D, ev_s)
    if own:
        final_wait_all(p)
        return p.finalize()


def build_L2a(ctx=None, copar_start=None, copar_mid=None, copar=None,
              copar_end=None):
    DI2 = 2048
    NST = 16
    FFH = 2731
    own = ctx is None
    if own:
        nc = bass.Bass("TRN2", target_bir_lowering=False)
        p = P(nc)
        io = None
    else:
        nc, p, io = ctx
    x2 = _dram(nc, io, "x", [D, T], F32, "ExternalInput")
    nmw = _dram(nc, io, "norm_m_w", [D], F32, "ExternalInput")
    ipT = _dram(nc, io, "in_projT", [D, 2 * DI2], F32, "ExternalInput")
    cw = _dram(nc, io, "conv_w", [4, DI2], F32, "ExternalInput")
    cb = _dram(nc, io, "conv_b", [DI2], F32, "ExternalInput")
    xpBT = _dram(nc, io, "xpBT", [DI2, NST], F32, "ExternalInput")
    xpCT = _dram(nc, io, "xpCT", [DI2, NST], F32, "ExternalInput")
    xpdT = _dram(nc, io, "xpdT", [DI2, 1], F32, "ExternalInput")
    dtw = _dram(nc, io, "dt_w", [DI2], F32, "ExternalInput")
    dtb = _dram(nc, io, "dt_b", [DI2], F32, "ExternalInput")
    Alog = _dram(nc, io, "A_log", [DI2, NST], F32, "ExternalInput")
    Dskip = _dram(nc, io, "D_skip", [DI2], F32, "ExternalInput")
    opT = _dram(nc, io, "out_projT", [DI2, D], F32, "ExternalInput")
    naw = _dram(nc, io, "norm_attn_w", [D], F32, "ExternalInput")
    qwT = _dram(nc, io, "qwT", [D, D], F32, "ExternalInput")
    kwT = _dram(nc, io, "kwT", [D, 512], F32, "ExternalInput")
    vwT = _dram(nc, io, "vwT", [D, 512], F32, "ExternalInput")
    owT = _dram(nc, io, "owT", [D, D], F32, "ExternalInput")
    nfw = _dram(nc, io, "norm_ffn_w", [D], F32, "ExternalInput")
    gwT = _dram(nc, io, "gwT", [D, FFH], F32, "ExternalInput")
    uwT = _dram(nc, io, "uwT", [D, FFH], F32, "ExternalInput")
    dwT = _dram(nc, io, "dwT", [FFH, D], F32, "ExternalInput")
    trid = _dram(nc, io, "trid", [4, 128, 512], F32, "ExternalInput")
    x_out = _dram(nc, io, "s", [D, T], F32, "ExternalOutput")
    xn_d = _dram(nc, io, "xn", [D, T], F32, "ExternalOutput")
    zs_d = _dram(nc, io, "zscr", [DI2, T], BF16, "Internal")
    xs_d = _dram(nc, io, "xsscr", [DI2, T], BF16, "Internal")
    Bt_d = _dram(nc, io, "Btscr", [NST, T], F32, "Internal")
    Ct_d = _dram(nc, io, "Ctscr", [NST, T], F32, "Internal")
    dtr_d = _dram(nc, io, "dtrscr", [1, T], F32, "Internal")
    h1_d = _dram(nc, io, "h1scr", [D, T], F32, "Internal")
    h2_d = _dram(nc, io, "h2scr", [D, T], F32, "Internal")
    ao_d = _dram(nc, io, "ao2scr", [D, T], BF16, "Internal")
    ffa_d = _dram(nc, io, "ffascr", [D, T], F32, "Internal")

    gv0, gs0, gt0 = load_bias_cols(p, nmw, 8, "gnm")
    p.wait("v", gs0, gt0)
    lp = lin_pool(p, 16)
    sc_xr = []
    xr = [p.sb("xr2", [128, T], BF16, scope=sc_xr) for _ in range(8)]

    sc1 = []
    np1 = norm_pool(p, scope=sc1)
    xch = [p.sb("xch2", [128, 512 * 8], scope=sc1) for _ in range(2)]
    xnf_ch = [p.sb("xnf2", [128, 512 * 8], scope=sc1) for _ in range(2)]
    stx = [p.sem("sxa"), p.sem("sxb")]
    st_marks = [None, None]

    def ln_apply(d, tc, t0, tw, xa, mu_bc, rstd_bc):
        ch = xnf_ch[tc % 2]
        if d == 0 and st_marks[tc % 2] is not None:
            p.wait("v", stx[tc % 2], st_marks[tc % 2])
        dst = ch[:, d * 512:d * 512 + tw]
        e_tt(p, dst, xa, mu_bc, ALU.subtract)
        e_tt(p, dst, dst, rstd_bc, ALU.mult)
        if d == 7:
            vm = p.marker("v")
            p.wait("y", vm[0], vm[1])
            for dd in range(8):
                e_dma(p, "y", xn_d[dd * 128:(dd + 1) * 128, t0:t0 + tw],
                      ch[:, dd * 512:dd * 512 + tw], stx[tc % 2])
            st_marks[tc % 2] = stx[tc % 2].n

    norm_feat(p, np1, ("dram", x2), ln_apply, 1e-5, xch=xch)
    p.close_scope(sc1)
    sc1b = []
    np1b = norm_pool(p, scope=sc1b)
    xchb = [p.sb("xchb2", [128, 512 * 8], scope=sc1b) for _ in range(2)]
    tmpc = p.sb("tmpc2", [128, 512], scope=sc1b)

    def rms_apply(d, tc, t0, tw, xa, mu_bc, rstd_bc):
        e_stt(p, tmpc[:, 0:tw], xa, gv0[:, d:d + 1], rstd_bc, ALU.mult,
              ALU.mult)
        e_copy(p, xr[d][:, t0:t0 + tw], tmpc[:, 0:tw])

    norm_feat(p, np1b, ("dram", xn_d), rms_apply, 1e-6, mean_center=False,
              xch=xchb)
    p.close_scope(sc1b)
    if copar_start is not None:
        p.barrier("ytvsg")
        copar_start()

    sc2 = []
    cvw = p.sb("cvw2", [128, 4 * 16], scope=sc2)
    cvb = p.sb("cvb2", [128, 16], scope=sc2)
    xpB = p.sb("xpB2", [128, 16 * NST], BF16, scope=sc2)
    xpC = p.sb("xpC2", [128, 16 * NST], BF16, scope=sc2)
    xpd = p.sb("xpd2", [128, 16], BF16, scope=sc2)
    convf = p.sb("convf", [128, T], scope=sc2)
    convo = p.sb("convo", [128, T], scope=sc2)
    cacc = p.sb("cacc2", [128, T], scope=sc2)
    zch = p.sb("zch2", [128, T], BF16, scope=sc2)
    Bps = p.pt("Bps2", [NST, 512], scope=sc2)
    Cps = p.pt("Cps2", [NST, 512], scope=sc2)
    dps = p.pt("dps2", [1, 512], scope=sc2)
    Bsb = p.sb("Bsb2", [NST, T], scope=sc2)
    Csb = p.sb("Csb2", [NST, T], scope=sc2)
    dsb = p.sb("dsb2", [1, T], scope=sc2)
    cl = p.sem("clc")
    for k4 in range(4):
        for dt_ in range(16):
            e_dma(p, "y", cvw[:, k4 * 16 + dt_:k4 * 16 + dt_ + 1],
                  cw[k4, dt_ * 128:(dt_ + 1) * 128].unsqueeze(1), cl)
    for dt_ in range(16):
        e_dma(p, "y", cvb[:, dt_:dt_ + 1],
              cb[dt_ * 128:(dt_ + 1) * 128].unsqueeze(1), cl)
    clg = p.sem("cstg")
    for dt_ in range(16):
        e_dma(p, "g", xpB[:, dt_ * NST:(dt_ + 1) * NST],
              xpBT[dt_ * 128:(dt_ + 1) * 128, :], clg)
        e_dma(p, "g", xpC[:, dt_ * NST:(dt_ + 1) * NST],
              xpCT[dt_ * 128:(dt_ + 1) * 128, :], clg)
        e_dma(p, "g", xpd[:, dt_:dt_ + 1],
              xpdT[dt_ * 128:(dt_ + 1) * 128, :], clg)
    p.wait("v", cl, cl.n)
    p.wait("s", cl, cl.n)
    p.wait("t", clg, clg.n)
    Bacc = [None, None, None]
    stz = p.sem("stz")
    stxs = p.sem("stxs")

    def ev_ip(mb, tc, t0, tw, psum, tmark):
        if mb < 16:
            p.wait("s", tmark[0], tmark[1])
            e_act(p, convf[:, t0:t0 + tw], psum, AF.Identity)
            return p.marker("s")
        p.wait("s", tmark[0], tmark[1])
        e_act(p, zch[:, t0:t0 + tw], psum, AF.Silu)
        sm = p.marker("s")
        if tc == T // 512 - 1:
            p.wait("y", sm[0], sm[1])
            e_dma(p, "y", zs_d[(mb - 16) * 128:(mb - 15) * 128, :], zch[:],
                  stz)
            ym = stz.n
            p.wait("s", stz, ym)
        return sm

    def post_conv(mb):
        dcol = mb
        p.wait("v", stxs, stxs.n)
        e_ts(p, cacc[:, 0:T], convf[:, 0:T],
             cvw[:, 3 * 16 + dcol:3 * 16 + dcol + 1], None, op0=ALU.mult)
        for kk in range(3):
            sh = 3 - kk
            e_stt(p, cacc[:, sh:T], convf[:, 0:T - sh],
                  cvw[:, kk * 16 + dcol:kk * 16 + dcol + 1], cacc[:, sh:T],
                  ALU.mult, ALU.add)
        vm = p.marker("v")
        p.wait("s", vm[0], vm[1])
        e_act(p, convo[:, 0:T], cacc[:, 0:T], AF.Silu,
              bias=cvb[:, dcol:dcol + 1])
        sm = p.marker("s")
        p.wait("v", sm[0], sm[1])
        cb16 = cacc[:, 0:T].bitcast(BF16)
        e_copy(p, cb16[:, 0:T], convo[:, 0:T])
        vm2 = p.marker("v")
        p.wait("y", vm2[0], vm2[1])
        e_dma(p, "y", xs_d[mb * 128:(mb + 1) * 128, :], cb16[:, 0:T], stxs)
        p.wait("t", vm2[0], vm2[1])
        for tc in range(T // 512):
            t0 = tc * 512
            e_mm(p, Bps[:, :], xpB[:, mb * NST:(mb + 1) * NST],
                 cb16[:, t0:t0 + 512])
            e_mm(p, Cps[:, :], xpC[:, mb * NST:(mb + 1) * NST],
                 cb16[:, t0:t0 + 512])
            e_mm(p, dps[:, :], xpd[:, mb:mb + 1], cb16[:, t0:t0 + 512])
            tm = p.marker("t")
            p.wait("v", tm[0], tm[1])
            if mb == 0:
                e_copy(p, Bsb[:, t0:t0 + 512], Bps[:, :])
                e_copy(p, Csb[:, t0:t0 + 512], Cps[:, :])
                e_copy(p, dsb[0:1, t0:t0 + 512], dps[0:1, :])
            else:
                e_tt(p, Bsb[:, t0:t0 + 512], Bsb[:, t0:t0 + 512], Bps[:, :],
                     ALU.add)
                e_tt(p, Csb[:, t0:t0 + 512], Csb[:, t0:t0 + 512], Cps[:, :],
                     ALU.add)
                e_tt(p, dsb[0:1, t0:t0 + 512], dsb[0:1, t0:t0 + 512],
                     dps[0:1, :], ALU.add)
            vmq = p.marker("v")
            p.wait("t", vmq[0], vmq[1])
        cm = p.marker("t")
        p.wait("s", cm[0], cm[1])
        p.wait("v", cm[0], cm[1])

    for mb in range(16):
        ev_done_ip = []

        def ev_ip_x(mbx, tc, t0, tw, psum, tmark, mb=mb):
            p.wait("s", tmark[0], tmark[1])
            e_act(p, convf[:, t0:t0 + tw], psum, AF.Identity)
            return p.marker("s")
        ems = linear_fm(p, lp, ipT[:, mb * 128:(mb + 1) * 128], xr, D, 128,
                        ev_ip_x)
        for m_ in ems:
            if m_ is not None:
                p.wait("v", m_[0], m_[1])
        post_conv(mb)
    for mb in range(16):
        def ev_ip_z(mbx, tc, t0, tw, psum, tmark, mb=mb):
            p.wait("s", tmark[0], tmark[1])
            e_act(p, zch[:, t0:t0 + tw], psum, AF.Silu)
            sm = p.marker("s")
            if t0 + tw >= T:
                p.wait("y", sm[0], sm[1])
                e_dma(p, "y", zs_d[mb * 128:(mb + 1) * 128, :], zch[:], stz)
                p.wait("s", stz, stz.n)
            return sm
        linear_fm(p, lp, ipT[:, DI2 + mb * 128:DI2 + (mb + 1) * 128], xr, D,
                  128, ev_ip_z)
    vb = p.marker("v")
    p.wait("y", vb[0], vb[1])
    sB = p.sem("sB2")
    e_dma(p, "y", Bt_d[:, :], Bsb[:], sB)
    e_dma(p, "y", Ct_d[:, :], Csb[:], sB)
    e_dma(p, "y", dtr_d[:, :], dsb[:], sB)
    p.barrier("ytvsg")
    p.close_scope(sc2)
    p.close_scope(sc_xr)
    if copar_mid is not None:
        copar_mid()

    sc3 = []
    TCH = 1024
    NTCH = T // TCH
    Avec = p.sb("Avec2", [128, 16 * NST], scope=sc3)
    dskv = p.sb("dskv2", [128, 16], scope=sc3)
    dtrep = p.sb("dtrep", [128, TCH], scope=sc3)
    delta = p.sb("del2", [128, TCH], scope=sc3)
    ych = p.sb("ych2", [128, TCH], scope=sc3)
    xs_ch = [p.sb("xsch2", [128, TCH], BF16, scope=sc3) for _ in range(16)]
    zss = p.sb("zsch2", [128, TCH], BF16, scope=sc3)
    Brep = [p.sb("Brep2", [128, TCH], BF16, scope=sc3) for _ in range(NST)]
    Crep = [p.sb("Crep2", [128, TCH], BF16, scope=sc3) for _ in range(NST)]
    dA2 = [p.sb("dA2", [128, TCH], scope=sc3) for _ in range(2)]
    u2 = [p.sb("u2", [128, TCH], BF16, scope=sc3)]
    dxb = p.sb("dxb2", [128, TCH], BF16, scope=sc3)
    hsc = p.sb("hsc2", [128, TCH // 2], scope=sc3)
    hscb = p.sb("hscb2", [128, TCH], BF16, scope=sc3)
    tmp2 = p.sb("tmp22", [128, TCH // 2], scope=sc3)
    carry_all = p.sb("car2", [128, 16 * NST], scope=sc3)
    carry = [carry_all[:, i * NST:(i + 1) * NST] for i in range(16)]
    xnch = [delta[:, 0:512], delta[:, 512:1024]]
    schh = [ych[:, 0:512], ych[:, 512:1024]]
    sdl = p.sem("sdl")
    for dt_ in range(16):
        e_dma(p, "y", Avec[:, dt_ * NST:(dt_ + 1) * NST],
              Alog[dt_ * 128:(dt_ + 1) * 128, :], sdl)
        e_dma(p, "y", dskv[:, dt_:dt_ + 1],
              Dskip[dt_ * 128:(dt_ + 1) * 128].unsqueeze(1), sdl)
    dtwv, _, _ = load_bias_cols(p, dtw, 16, "dtwv", scope=sc3)
    dtbv, ds2, dt2 = load_bias_cols(p, dtb, 16, "dtbv", scope=sc3)
    p.wait("s", sdl, sdl.n)
    p.wait("v", sdl, sdl.n)
    p.wait("v", ds2, dt2)
    e_act(p, Avec[:], Avec[:], AF.Exp)
    smA = p.marker("s")
    p.wait("v", smA[0], smA[1])
    e_ts(p, Avec[:], Avec[:], -1.0, None, op0=ALU.mult)
    vmA = p.marker("v")
    p.wait("s", vmA[0], vmA[1])
    for dt_ in range(16):
        e_memset(p, carry[dt_][:], 0.0)
    p.barrier("ytvsg")
    dA_rd = [None, None]
    brs = [p.sem("brsa"), p.sem("brsb")]
    xss = [p.sem("xssa"), p.sem("xssb")]
    zsm = p.sem("zsm")
    drs = [p.sem("drsa"), p.sem("drsb")]
    yzm = [None]
    pchunk = {}
    for tch in range(NTCH):
        c0 = tch * TCH
        if pchunk:
            p.wait("g", pchunk["vsc"][0], pchunk["vsc"][1])
            p.wait("y", pchunk["tend"][0], pchunk["tend"][1])
            p.wait("y", pchunk["vsc"][0], pchunk["vsc"][1])
        for n in range(NST):
            e_dma(p, "g", Brep[n][:],
                  Bt_d[n:n + 1, c0:c0 + TCH].partition_broadcast(128)
                  .squeeze(1), brs[tch % 2])
        for n in range(NST):
            _, brt = e_dma(p, "g", Crep[n][:],
                           Ct_d[n:n + 1, c0:c0 + TCH].partition_broadcast(128)
                           .squeeze(1), brs[tch % 2])
        for dt_ in range(16):
            _, xst = e_dma(p, "y", xs_ch[dt_][:],
                           xs_d[dt_ * 128:(dt_ + 1) * 128, c0:c0 + TCH],
                           xss[tch % 2])
        _, drt = e_dma(p, "y", dtrep[:, 0:TCH],
                       dtr_d[0:1, c0:c0 + TCH].partition_broadcast(128)
                       .squeeze(1), drs[tch % 2])
        p.wait("v", xss[tch % 2], xst)
        p.wait("s", xss[tch % 2], xst)
        p.wait("v", brs[tch % 2], brt)
        p.wait("v", drs[tch % 2], drt)
        p.wait("s", drs[tch % 2], drt)
        if pchunk:
            for uu, (sem_, tgt_) in pchunk["h1"].items():
                p.wait("v", sem_, tgt_)
                p.wait("s", sem_, tgt_)
            p.wait("s", pchunk["vend"][0], pchunk["vend"][1])
            p.wait("v", pchunk["send"][0], pchunk["send"][1])
        zst = {}
        for dt_ in range(16):
            for hh in range(2):
                h0 = hh * (TCH // 2)
                e_ts(p, tmp2[:, 0:TCH // 2], dtrep[:, h0:h0 + TCH // 2],
                     dtwv[:, dt_:dt_ + 1], dtbv[:, dt_:dt_ + 1],
                     op0=ALU.mult, op1=ALU.add)
                vmd = p.marker("v")
                p.wait("s", vmd[0], vmd[1])
                e_softplus(p, delta, tmp2, hsc, n0=h0, nw=TCH // 2,
                           tmp0=True, in0=True)
                smh = p.marker("s")
                p.wait("v", smh[0], smh[1])
            spm = p.marker("s")
            p.wait("v", spm[0], spm[1])
            e_tt(p, dxb[:], delta[:], xs_ch[dt_][:], ALU.mult)
            vdx = p.marker("v")
            if yzm[0] is not None:
                p.wait("y", yzm[0][0], yzm[0][1])
            _, zst[dt_] = e_dma(p, "y", zss[:],
                                zs_d[dt_ * 128:(dt_ + 1) * 128, c0:c0 + TCH],
                                zsm)
            for n in range(NST):
                bA = n % 2
                p.wait("s", vdx[0], vdx[1])
                if dA_rd[bA] is not None:
                    p.wait("s", dA_rd[bA][0], dA_rd[bA][1])
                e_act(p, dA2[bA][:], delta[:], AF.Exp,
                      scale=Avec[:, dt_ * NST + n:dt_ * NST + n + 1])
                sm = p.marker("s")
                e_tt(p, u2[0][:], dxb[:], Brep[n][:], ALU.mult)
                p.wait("v", sm[0], sm[1])
                p.op("v", lambda eng, dt_=dt_, n=n, bA=bA:
                     eng.tensor_tensor_scan(
                         out=hscb[:], data0=dA2[bA][:], data1=u2[0][:],
                         initial=carry[dt_][:, n:n + 1], op0=ALU.mult,
                         op1=ALU.add))
                e_copy(p, carry[dt_][:, n:n + 1], hscb[:, TCH - 1:TCH])
                if n == 0:
                    e_tt(p, ych[:], hscb[:], Crep[n][:], ALU.mult)
                else:
                    e_tt(p, hscb[:], hscb[:], Crep[n][:], ALU.mult)
                    e_tt(p, ych[:], ych[:], hscb[:], ALU.add)
                dA_rd[bA] = p.marker("v")
                if copar is not None:
                    copar()
            e_stt(p, ych[:], xs_ch[dt_][:], dskv[:, dt_:dt_ + 1], ych[:],
                  ALU.mult, ALU.add)
            p.wait("v", zsm, zst[dt_])
            e_tt(p, xs_ch[dt_][:], ych[:], zss[:], ALU.mult)
            yzm[0] = p.marker("v")
        vsc = p.marker("v")
        ssc = p.marker("s")
        p.wait("t", vsc[0], vsc[1])
        h1st = [p.sem("h1sa"), p.sem("h1sb")]
        xnld = [p.sem("xla"), p.sem("xlb")]
        xt_l = {}
        evd = {}

        def ev_op(mb, tc, t0, tw, psum, tmark, c0=c0, h1st=h1st, xnld=xnld,
                  xt_l=xt_l, evd=evd):
            u = (mb * (TCH // 512) + tc) % 2
            uk = mb * (TCH // 512) + tc
            if uk == 0:
                p.wait("y", vsc[0], vsc[1])
                p.wait("y", ssc[0], ssc[1])
                for k2 in (0, 1):
                    _, xt_l[k2] = e_dma(p, "y", xnch[k2][:],
                                        xn_d[0:128, c0 + k2 * 512:
                                             c0 + (k2 + 1) * 512], xnld[k2])
            p.wait("s", tmark[0], tmark[1])
            p.wait("s", xnld[u], xt_l[uk])
            if uk >= 2 and (uk - 2) in evd:
                p.wait("s", h1st[u], evd[uk - 2])
            e_act(p, schh[u][:, 0:tw], psum, AF.Identity)
            sm = p.marker("s")
            p.wait("v", sm[0], sm[1])
            e_tt(p, schh[u][:, 0:tw], schh[u][:, 0:tw], xnch[u][:, 0:tw],
                 ALU.add)
            vm = p.marker("v")
            if uk + 2 <= 2 * (TCH // 512) * 8 // 2 - 1 and uk + 2 < 8 * (TCH // 512):
                nk_ = uk + 2
                mb2, tc2 = nk_ // (TCH // 512), nk_ % (TCH // 512)
                p.wait("y", vm[0], vm[1])
                _, xt_l[nk_] = e_dma(p, "y", xnch[u][:],
                                     xn_d[mb2 * 128:(mb2 + 1) * 128,
                                          c0 + tc2 * 512:c0 + tc2 * 512 + 512],
                                     xnld[u])
            p.wait("y", vm[0], vm[1])
            _, st_ = e_dma(p, "y", h1_d[mb * 128:(mb + 1) * 128,
                                        c0 + t0:c0 + t0 + tw],
                           schh[u][:, 0:tw], h1st[u])
            evd[uk] = st_
            return vm

        linear_fm(p, lp, opT, xs_ch, DI2, D, ev_op, n_free=TCH)
        nuk = 8 * (TCH // 512)
        pchunk = {
            "vsc": vsc, "tend": p.marker("t"), "vend": p.marker("v"),
            "send": p.marker("s"),
            "h1": {u_: (h1st[u_], evd[nuk - 2 + u_]) for u_ in (0, 1)},
        }
    p.barrier("ytvsg")
    p.close_scope(sc3)
    if copar_end is not None:
        copar_end()
    p.barrier("ytvsg")

    gv1, gs1, gt1 = load_bias_cols(p, naw, 8, "gna")
    p.wait("v", gs1, gt1)
    xa = [p.sb("xa2", [128, T], BF16) for _ in range(8)]
    sc4 = []
    np4 = norm_pool(p, scope=sc4)
    xch4 = [p.sb("xch42", [128, 512 * 8], scope=sc4) for _ in range(2)]
    tmp4 = p.sb("tmp42", [128, 512], scope=sc4)

    def rms_a(d, tc, t0, tw, xav, mu_bc, rstd_bc):
        e_stt(p, tmp4[:, 0:tw], xav, gv1[:, d:d + 1], rstd_bc, ALU.mult,
              ALU.mult)
        e_copy(p, xa[d][:, t0:t0 + tw], tmp4[:, 0:tw])

    norm_feat(p, np4, ("dram", h1_d), rms_a, 1e-6, mean_center=False,
              xch=xch4)
    p.close_scope(sc4)

    sc5 = []
    kt5 = [p.sb("kt5", [128, T], BF16, scope=sc5) for _ in range(4)]
    vt5 = p.sb("vt5", [128, (T // 128) * 516], BF16, scope=sc5)
    qh5 = [p.sb("qh5", [128, T], BF16, scope=sc5) for _ in range(2)]
    ah5 = [p.sb("ah5", [128, T], BF16, scope=sc5) for _ in range(2)]
    wv5 = [p.sb("wv5", [128, 512], BF16, scope=sc5) for _ in range(8)]
    tri5 = [p.sb("tri5", [128, 512], BF16, scope=sc5) for _ in range(4)]
    o_scr5 = p.sb("oscr5", [128, 1024], scope=sc5)
    osum5 = p.sb("osum5", [1, 512], scope=sc5)
    rec5 = p.sb("rec5", [1, 512], scope=sc5)
    onesq5 = p.sb("onsq5", [1, 128], scope=sc5)
    ones_t5 = p.sb("onet5", [1, T], BF16, scope=sc5)
    state5 = dict(
        sc_ps=[p.pt("scp5", [128, 512], scope=sc5) for _ in range(2)],
        pbuf=[p.sb("pb5", [128, 512], BF16, scope=sc5) for _ in range(2)],
        ov_ps=[[p.pt("ovp5a", [128, 512], scope=sc5)],
               [p.pt("ovp5b", [128, 512], scope=sc5)],
               [p.pt("ovp5c", [1, 512], scope=sc5)]],
    )
    e_memset(p, onesq5[:], 1.0)
    e_memset(p, ones_t5[:], 1.0)
    t5s = p.sem("cstg")
    for j in range(4):
        e_dma(p, "g", tri5[j][:], trid[j], t5s)
    ev_k5 = evac_to_tiles(p, kt5, AF.Identity)
    linear_fm(p, lp, kwT, xa, D, 512, ev_k5)
    wsem5 = p.sem("wvs")
    for k in range(8):
        e_dma(p, "g", wv5[k][:], vwT[k * 128:(k + 1) * 128, :], wsem5)
    p.wait("t", wsem5, wsem5.n)
    p.wait("t", t5s, t5s.n)
    p.wait("v", t5s, t5s.n)
    for tt in range(T // 128):
        for kv in range(2):
            e_memset(p, vt5[:, tt * 516 + kv * 258 + 256:
                           tt * 516 + kv * 258 + 258], 1.0)
    vm5 = p.marker("v")
    p.wait("s", vm5[0], vm5[1])

    def ev_v5(tt, psum, tmark):
        p.wait("s", tmark[0], tmark[1])
        for kv in range(2):
            e_act(p, vt5[:, tt * 516 + kv * 258:tt * 516 + kv * 258 + 256],
                  psum[:, kv * 256:kv * 256 + 256], AF.Identity)
        return p.marker("s")
    linear_tm(p, lp, wv5, xa, D, 512, ev_v5)
    p.barrier("ytvsg")

    ao5st = p.sem("aosx")
    for h in range(4):
        kv = h // 2
        ev_q5 = evac_to_tiles(p, qh5, AF.Identity)
        linear_fm(p, lp, qwT[:, 256 * h:256 * h + 256], xa, D, 256, ev_q5)
        p.barrier("ytvsg")

        def causal_sbs(tg, h=h):
            return list(range((tg + 1) * 4))

        def mask_cb(sb_, tg, pb, em, h=h):
            base = tg * 4
            if sb_ < base:
                return None
            p.wait("v", em[0], em[1])
            e_tt(p, pb[:, 0:512], pb[:, 0:512], tri5[sb_ - base][:], ALU.mult)
            return p.marker("v")

        def o_cb5(tg, g0, gw, h=h):
            ov = state5["ov_ps"]
            e_act(p, o_scr5[:, 0:512], ov[0][0][:, :], AF.Identity)
            e_act(p, o_scr5[:, 512:1024], ov[1][0][:, :], AF.Identity)
            e_act(p, osum5[:, 0:512], ov[2][0][0:1, :], AF.Identity)
            sm = p.marker("s")
            p.wait("v", sm[0], sm[1])
            p.op("v", lambda eng: eng.reciprocal(out=rec5[:, 0:512],
                                                 in_=osum5[:, 0:512]))
            vm = p.marker("v")
            p.wait("t", vm[0], vm[1])
            for half in range(2):
                e_mm(p, ov[half][0][:, :], onesq5[0:1, 0:128], rec5[:, 0:512])
            tb = p.marker("t")
            p.wait("v", tb[0], tb[1])
            for half in range(2):
                e_tt(p, ah5[half][:, g0:g0 + 512],
                     o_scr5[:, half * 512:(half + 1) * 512],
                     ov[half][0][:, :], ALU.mult)
            vo = p.marker("v")
            p.wait("t", vo[0], vo[1])
            p.wait("s", vo[0], vo[1])

        attn_head(
            p,
            sc_chunks=[(lambda s0, kv=kv: kt5[2 * kv][:, s0:s0 + 128],
                        lambda g0, gw: qh5[0][:, g0:g0 + gw]),
                       (lambda s0, kv=kv: kt5[2 * kv + 1][:, s0:s0 + 128],
                        lambda g0, gw: qh5[1][:, g0:g0 + gw])],
            get_vchunks=lambda sb_, kv=kv: [
                (vt5[:, sb_ * 516 + kv * 258:sb_ * 516 + kv * 258 + 128], 0, 128),
                (vt5[:, sb_ * 516 + kv * 258 + 128:sb_ * 516 + kv * 258 + 256], 1, 128),
                (vt5[:, sb_ * 516 + kv * 258 + 256:sb_ * 516 + kv * 258 + 257], 2, 1)],
            o_cb=o_cb5, state=state5, tgsz=512, causal_sbs=causal_sbs,
            mask_cb=mask_cb, exp_scale=0.0625)
        hm = p.marker("v")
        p.wait("y", hm[0], hm[1])
        for half in range(2):
            e_dma(p, "y", ao_d[(2 * h + half) * 128:(2 * h + half + 1) * 128, :],
                  ah5[half][:], ao5st)
        p.wait("v", ao5st, ao5st.n)
        p.wait("s", ao5st, ao5st.n)
        p.wait("t", ao5st, ao5st.n)
    p.barrier("ytvsg")
    p.close_scope(sc5)

    sc6 = []
    obt6 = [p.sb("obt6", [128, T], BF16, scope=sc6) for _ in range(8)]
    xrs6 = [p.sb("xr6", [128, 512], scope=sc6) for _ in range(2)]
    sch6 = [p.sb("sc6", [128, 512], scope=sc6) for _ in range(2)]
    ob6 = p.sem("obl")
    for k in range(8):
        e_dma(p, "y", obt6[k][:], ao_d[k * 128:(k + 1) * 128, :], ob6)
    p.wait("t", ob6, ob6.n)
    h2st = [p.sem("osta"), p.sem("ostb")]
    xnl6 = [p.sem("xla"), p.sem("xlb")]
    xt6 = {}
    evd6 = {}
    NTC = T // 512

    def ev_h2(mb, tc, t0, tw, psum, tmark):
        u = (mb * NTC + tc) % 2
        uk = mb * NTC + tc
        if uk == 0:
            for k2 in (0, 1):
                _, xt6[k2] = e_dma(p, "y", xrs6[k2 % 2][:],
                                   h1_d[0:128, k2 * 512:(k2 + 1) * 512],
                                   xnl6[k2 % 2])
        p.wait("s", tmark[0], tmark[1])
        p.wait("s", xnl6[u], xt6[uk])
        if uk >= 2 and (uk - 2) in evd6:
            p.wait("s", h2st[u], evd6[uk - 2])
        e_act(p, sch6[u][:, 0:tw], psum, AF.Identity)
        sm = p.marker("s")
        p.wait("v", sm[0], sm[1])
        e_tt(p, sch6[u][:, 0:tw], sch6[u][:, 0:tw], xrs6[u][:, 0:tw], ALU.add)
        vm = p.marker("v")
        if uk + 2 <= 8 * NTC - 1:
            nk_ = uk + 2
            mb2, tc2 = nk_ // NTC, nk_ % NTC
            p.wait("y", vm[0], vm[1])
            _, xt6[nk_] = e_dma(p, "y", xrs6[u][:],
                                h1_d[mb2 * 128:(mb2 + 1) * 128,
                                     tc2 * 512:tc2 * 512 + 512], xnl6[u])
        p.wait("y", vm[0], vm[1])
        _, st_ = e_dma(p, "y", h2_d[mb * 128:(mb + 1) * 128, t0:t0 + tw],
                       sch6[u][:, 0:tw], h2st[u])
        evd6[uk] = st_
        return vm

    linear_fm(p, lp, owT, obt6, D, D, ev_h2)
    p.barrier("ytvsg")
    p.close_scope(sc6)

    gv2, gs2, gt2 = load_bias_cols(p, nfw, 8, "gnf")
    p.wait("v", gs2, gt2)
    xf = xa
    sc7 = []
    np7 = norm_pool(p, scope=sc7)
    xch7 = [p.sb("xch72", [128, 512 * 8], scope=sc7) for _ in range(2)]
    tmp7 = p.sb("tmp72", [128, 512], scope=sc7)

    def rms_f(d, tc, t0, tw, xav, mu_bc, rstd_bc):
        e_stt(p, tmp7[:, 0:tw], xav, gv2[:, d:d + 1], rstd_bc, ALU.mult,
              ALU.mult)
        e_copy(p, xf[d][:, t0:t0 + tw], tmp7[:, 0:tw])

    norm_feat(p, np7, ("dram", h2_d), rms_f, 1e-6, mean_center=False,
              xch=xch7)
    p.close_scope(sc7)

    sc8 = []
    lpF = lin_pool(p, 16, scope=sc8, np_ps=4)
    hb8 = [p.sb("hb8", [128, T], BF16, scope=sc8) for _ in range(8)]
    asb8 = [p.sb("asb8", [128, 512], scope=sc8) for _ in range(2)]
    wB8 = [p.sb("wB8", [128, 8 * 128], BF16, scope=sc8) for _ in range(2)]
    ob8 = [p.sb("ob8", [128, 512], scope=sc8) for _ in range(2)]
    w13s8 = [p.sem("w13a"), p.sem("w13b")]
    accs8 = [p.sem("acca"), p.sem("accb")]
    acc_t8 = [None, None]
    first8 = [True]
    wA8 = lpF["wbuf"]
    KGS = [(0, 8), (8, 8), (16, 6)]
    for (kg0, kgn) in KGS:
        wt8 = {}

        def load8(fbl, gate):
            fb = kg0 + fbl
            f0 = fb * 128
            fw = min(128, FFH - f0)
            if gate is not None:
                p.wait("g", gate[0], gate[1])
            _, t1 = e_dma(p, "g", wA8[fbl % 2][:, 0:8 * fw].rearrange(
                "p (kc m) -> p kc m", kc=8),
                gwT[:, f0:f0 + fw].rearrange("(kc p) m -> p kc m", p=128),
                w13s8[fbl % 2])
            _, t2 = e_dma(p, "g", wB8[fbl % 2][:, 0:8 * fw].rearrange(
                "p (kc m) -> p kc m", kc=8),
                uwT[:, f0:f0 + fw].rearrange("(kc p) m -> p kc m", p=128),
                w13s8[fbl % 2])
            wt8[fbl] = t2

        load8(0, None)
        if kgn > 1:
            load8(1, None)
        for fbl in range(kgn):
            fb = kg0 + fbl
            f0 = fb * 128
            fw = min(128, FFH - f0)
            p.wait("t", w13s8[fbl % 2], wt8[fbl])
            for tc in range(T // 512):
                t0 = tc * 512
                sl = 2 * (tc % 2)
                pp, pq = lpF["ps"][sl], lpF["ps"][sl + 1]
                if lpF["prev"][sl] is not None:
                    p.wait("t", lpF["prev"][sl][0], lpF["prev"][sl][1])
                for k in range(8):
                    e_mm(p, pp[0:fw, 0:512],
                         wA8[fbl % 2][0:128, k * fw:(k + 1) * fw],
                         xf[k][:, t0:t0 + 512], start=(k == 0), stop=(k == 7))
                tma = p.marker("t")
                if lpF["prev"][sl + 1] is not None:
                    p.wait("t", lpF["prev"][sl + 1][0],
                           lpF["prev"][sl + 1][1])
                for k in range(8):
                    e_mm(p, pq[0:fw, 0:512],
                         wB8[fbl % 2][0:128, k * fw:(k + 1) * fw],
                         xf[k][:, t0:t0 + 512], start=(k == 0), stop=(k == 7))
                tmb = p.marker("t")
                p.wait("s", tma[0], tma[1])
                prevv = lpF["prev"][sl + 1]
                if prevv is not None:
                    p.wait("s", prevv[0], prevv[1])
                e_act(p, asb8[tc % 2][0:fw, 0:512], pp[0:fw, 0:512], AF.Silu)
                sm_ = p.marker("s")
                lpF["prev"][sl] = sm_
                p.wait("v", tmb[0], tmb[1])
                p.wait("v", sm_[0], sm_[1])
                e_tt(p, hb8[fbl][0:fw, t0:t0 + 512], asb8[tc % 2][0:fw, 0:512],
                     pq[0:fw, 0:512], ALU.mult)
                lpF["prev"][sl + 1] = p.marker("v")
            if fbl + 2 < kgn:
                gm = p.marker("t")
                p.wait("g", gm[0], gm[1])
                load8(fbl + 2, gm)
        hvm = p.marker("v")
        p.wait("g", hvm[0], hvm[1])
        w2t8 = {}
        krows = min(kgn * 128, FFH - kg0 * 128)
        nk8 = (krows + 127) // 128

        def loadw2(mb, gate):
            if gate is not None:
                p.wait("g", gate[0], gate[1])
            sn = None
            for k in range(nk8):
                k0 = kg0 * 128 + k * 128
                kw = min(128, FFH - k0)
                _, sn = e_dma(p, "g", wB8[mb % 2][0:kw, k * 128:(k + 1) * 128],
                              dwT[k0:k0 + kw, mb * 128:(mb + 1) * 128],
                              w13s8[mb % 2])
            w2t8[mb] = sn

        loadw2(0, None)
        loadw2(1, None)
        for mb in range(8):
            p.wait("t", w13s8[mb % 2], w2t8[mb])
            for tc in range(T // 512):
                t0 = tc * 512
                uu = mb * (T // 512) + tc
                u = uu % 2
                u4 = uu % 4
                pp = lpF["ps"][u4]
                if lpF["prev"][u4] is not None:
                    p.wait("t", lpF["prev"][u4][0], lpF["prev"][u4][1])
                for k in range(nk8):
                    kw = min(128, FFH - (kg0 * 128 + k * 128))
                    e_mm(p, pp[0:128, 0:512],
                         wB8[mb % 2][0:kw, k * 128:(k + 1) * 128],
                         hb8[k][0:kw, t0:t0 + 512], start=(k == 0),
                         stop=(k == nk8 - 1))
                tm = p.marker("t")
                p.wait("s", tm[0], tm[1])
                if acc_t8[u] is not None:
                    p.wait("s", accs8[u], acc_t8[u])
                e_act(p, ob8[u][:, 0:512], pp[0:128, 0:512], AF.Identity)
                sm = p.marker("s")
                lpF["prev"][u4] = sm
                p.wait("g", sm[0], sm[1])
                if first8[0]:
                    p.op("g", lambda eng, mb=mb, t0=t0, u=u: eng.dma_start(
                        out=ffa_d[mb * 128:(mb + 1) * 128, t0:t0 + 512],
                        in_=ob8[u][:, 0:512]), accs8[u], 16, kind="d")
                else:
                    p.op("g", lambda eng, mb=mb, t0=t0, u=u: eng.dma_start(
                        out=ffa_d[mb * 128:(mb + 1) * 128, t0:t0 + 512],
                        in_=ob8[u][:, 0:512], accum_op=ALU.add), accs8[u], 16,
                        kind="d")
                p.dma_sems.add(accs8[u])
                acc_t8[u] = accs8[u].n
            if mb + 2 < 8:
                gm2 = p.marker("t")
                loadw2(mb + 2, gm2)
        first8[0] = False
    p.barrier("ytvsg")
    p.close_scope(sc8)

    fsem = p.sem("fin")
    fst2 = [p.sem("osta"), p.sem("ostb")]
    sch = [p.sb("fcL2", [128, 1024]) for _ in range(2)]
    hch = [p.sb("fhL2", [128, 1024]) for _ in range(2)]
    xch9 = [p.sb("fxL2", [128, 1024]) for _ in range(2)]
    fl = {}
    fs_t = {}
    nhalf = T // 1024
    for dd in range(8):
        for half in range(nhalf):
            u = (dd * nhalf + half) % 2
            c0 = half * 1024
            key = dd * nhalf + half
            if key >= 2:
                pk = key - 2
                p.wait("y", fl[pk][0], fl[pk][1])
                p.wait("y", fst2[u], fs_t[pk])
            e_dma(p, "y", sch[u][:], x2[dd * 128:(dd + 1) * 128,
                                        c0:c0 + 1024], fsem)
            e_dma(p, "y", xch9[u][:], h2_d[dd * 128:(dd + 1) * 128,
                                           c0:c0 + 1024], fsem)
            _, t2 = e_dma(p, "y", hch[u][:], ffa_d[dd * 128:(dd + 1) * 128,
                                                   c0:c0 + 1024], fsem)
            p.wait("v", fsem, t2)
            e_tt(p, sch[u][:], sch[u][:], xch9[u][:], ALU.add)
            e_tt(p, sch[u][:], sch[u][:], hch[u][:], ALU.add)
            vm = p.marker("v")
            p.wait("y", vm[0], vm[1])
            _, ft = e_dma(p, "y", x_out[dd * 128:(dd + 1) * 128, c0:c0 + 1024],
                          sch[u][:], fst2[u])
            fl[key] = vm
            fs_t[key] = ft
    if own:
        final_wait_all(p)
        return p.finalize()


def build_L0a(ctx=None):
    DI0 = 2048
    NH0 = 256
    NST0 = 16
    own = ctx is None
    if own:
        nc = bass.Bass("TRN2", target_bir_lowering=False)
        p = P(nc)
        io = None
    else:
        nc, p, io = ctx
    x0 = _dram(nc, io, "x", [D, T], F32, "ExternalInput")
    lnw = _dram(nc, io, "ln_w", [D], F32, "ExternalInput")
    lnb = _dram(nc, io, "ln_b", [D], F32, "ExternalInput")
    WinZ = _dram(nc, io, "WinZ", [D, DI0], F32, "ExternalInput")
    WinX = _dram(nc, io, "WinX", [D, DI0], F32, "ExternalInput")
    WinB = _dram(nc, io, "WinB", [D, NST0], F32, "ExternalInput")
    WinC = _dram(nc, io, "WinC", [D, NST0], F32, "ExternalInput")
    WinD = _dram(nc, io, "WinD", [D, NH0], F32, "ExternalInput")
    cwx = _dram(nc, io, "cwx", [4, DI0], F32, "ExternalInput")
    cwB = _dram(nc, io, "cwB", [4, NST0], F32, "ExternalInput")
    cwC = _dram(nc, io, "cwC", [4, NST0], F32, "ExternalInput")
    cbx = _dram(nc, io, "cbx", [DI0], F32, "ExternalInput")
    cbB = _dram(nc, io, "cbB", [NST0], F32, "ExternalInput")
    cbC = _dram(nc, io, "cbC", [NST0], F32, "ExternalInput")
    Alog = _dram(nc, io, "A_log", [NH0], F32, "ExternalInput")
    Dprep = _dram(nc, io, "Dp_rep", [DI0], F32, "ExternalInput")
    dtbias = _dram(nc, io, "dt_bias", [NH0], F32, "ExternalInput")
    rmsw = _dram(nc, io, "rms_w", [DI0], F32, "ExternalInput")
    WoutT = _dram(nc, io, "W_out", [DI0, D], F32, "ExternalInput")
    xn_out = _dram(nc, io, "xn", [D, T], F32, "ExternalOutput")
    s_out = _dram(nc, io, "s", [D, T], F32, "ExternalOutput")
    zs_d = _dram(nc, io, "zscr0", [DI0, T], BF16, "Internal")
    xs_d = _dram(nc, io, "xsscr0", [DI0, T], BF16, "Internal")
    Bt_d = _dram(nc, io, "Btscr0", [NST0, T], F32, "Internal")
    Ct_d = _dram(nc, io, "Ctscr0", [NST0, T], F32, "Internal")
    dA_d = _dram(nc, io, "dAscr0", [NH0, T], F32, "Internal")
    dt_d = _dram(nc, io, "dtscr0", [NH0, T], F32, "Internal")

    gv, _, _ = load_bias_cols(p, lnw, 8, "g0v")
    bv, bs0, bt0 = load_bias_cols(p, lnb, 8, "b0v")
    p.wait("v", bs0, bt0)
    lp = lin_pool(p, 16, np_ps=4)

    sc_x = []
    xn2 = [p.sb("xn20", [128, T], BF16, scope=sc_x) for _ in range(8)]
    sc1 = []
    np1 = norm_pool(p, scope=sc1)
    xch = [p.sb("xch0", [128, 512 * 8], scope=sc1) for _ in range(2)]
    xnf_ch = [p.sb("xnf0", [128, 512 * 8], scope=sc1) for _ in range(2)]
    stx = [p.sem("sxa"), p.sem("sxb")]
    st_marks = [None, None]

    def ln_apply(d, tc, t0, tw, xa, mu_bc, rstd_bc):
        ch = xnf_ch[tc % 2]
        if d == 0 and st_marks[tc % 2] is not None:
            p.wait("v", stx[tc % 2], st_marks[tc % 2])
        dst = ch[:, d * 512:d * 512 + tw]
        e_tt(p, dst, xa, mu_bc, ALU.subtract)
        e_tt(p, dst, dst, rstd_bc, ALU.mult)
        if d == 7:
            vm = p.marker("v")
            p.wait("y", vm[0], vm[1])
            for dd in range(8):
                e_dma(p, "y", xn_out[dd * 128:(dd + 1) * 128, t0:t0 + tw],
                      ch[:, dd * 512:dd * 512 + tw], stx[tc % 2])
            st_marks[tc % 2] = stx[tc % 2].n

    norm_feat(p, np1, ("dram", x0), ln_apply, 1e-5, xch=xch)
    p.close_scope(sc1)
    sc1b = []
    np1b = norm_pool(p, scope=sc1b)
    xchb = [p.sb("xchb0", [128, 512 * 8], scope=sc1b) for _ in range(2)]
    tmpb = p.sb("tmpb0", [128, 512], scope=sc1b)

    def ln2_apply(d, tc, t0, tw, xa, mu_bc, rstd_bc):
        e_tt(p, tmpb[:, 0:tw], xa, mu_bc, ALU.subtract)
        e_tt(p, tmpb[:, 0:tw], tmpb[:, 0:tw], rstd_bc, ALU.mult)
        e_ts(p, tmpb[:, 0:tw], tmpb[:, 0:tw], gv[:, d:d + 1], bv[:, d:d + 1],
             op0=ALU.mult, op1=ALU.add)
        e_copy(p, xn2[d][:, t0:t0 + tw], tmpb[:, 0:tw])

    norm_feat(p, np1b, ("dram", xn_out), ln2_apply, 1e-5, xch=xchb)
    p.close_scope(sc1b)

    sc2 = []
    cvwx = p.sb("cvwx0", [128, 4 * 16], scope=sc2)
    cvbx = p.sb("cvbx0", [128, 16], scope=sc2)
    convf = p.sb("convf0", [128, T], scope=sc2)
    convo = p.sb("convo0", [128, T], scope=sc2)
    cacc = p.sb("cacc0", [128, T], scope=sc2)
    zch = p.sb("zch0", [128, T], BF16, scope=sc2)
    BCw = p.sb("BCw0", [16, 10], scope=sc2)
    Bpre = p.sb("Bpre0", [16, T], scope=sc2)
    Bacc = p.sb("Bacc0", [16, T], scope=sc2)
    dcb = [p.sb("dcb0", [128, 512], scope=sc2) for _ in range(3)]
    Avc = p.sb("Avc0", [128, 2], scope=sc2)
    dbv = p.sb("dbv0", [128, 2], scope=sc2)
    cl = p.sem("clc")
    for k4 in range(4):
        for dt_ in range(16):
            e_dma(p, "y", cvwx[:, k4 * 16 + dt_:k4 * 16 + dt_ + 1],
                  cwx[k4, dt_ * 128:(dt_ + 1) * 128].unsqueeze(1), cl)
    for dt_ in range(16):
        e_dma(p, "y", cvbx[:, dt_:dt_ + 1],
              cbx[dt_ * 128:(dt_ + 1) * 128].unsqueeze(1), cl)
    for k4 in range(4):
        e_dma(p, "y", BCw[:, k4:k4 + 1], cwB[k4].unsqueeze(1), cl)
        e_dma(p, "y", BCw[:, 4 + k4:5 + k4], cwC[k4].unsqueeze(1), cl)
    e_dma(p, "y", BCw[:, 8:9], cbB.unsqueeze(1), cl)
    for j in range(2):
        e_dma(p, "y", Avc[:, j:j + 1],
              Alog[j * 128:(j + 1) * 128].unsqueeze(1), cl)
        e_dma(p, "y", dbv[:, j:j + 1],
              dtbias[j * 128:(j + 1) * 128].unsqueeze(1), cl)
    p.wait("v", cl, cl.n)
    p.wait("s", cl, cl.n)
    e_act(p, Avc[:], Avc[:], AF.Exp)
    smA = p.marker("s")
    p.wait("v", smA[0], smA[1])
    e_ts(p, Avc[:], Avc[:], -1.0, None, op0=ALU.mult)
    vmA = p.marker("v")
    p.wait("s", vmA[0], vmA[1])

    stz = p.sem("stz")
    stxs = p.sem("stxs")

    for mb in range(16):
        def ev_z(mbx, tc, t0, tw, psum, tmark, mb=mb):
            p.wait("s", tmark[0], tmark[1])
            e_act(p, zch[:, t0:t0 + tw], psum, AF.Silu)
            sm = p.marker("s")
            if t0 + tw >= T:
                p.wait("y", sm[0], sm[1])
                e_dma(p, "y", zs_d[mb * 128:(mb + 1) * 128, :], zch[:], stz)
                p.wait("s", stz, stz.n)
            return sm
        linear_fm(p, lp, WinZ[:, mb * 128:(mb + 1) * 128], xn2, D, 128, ev_z)

    for mb in range(16):
        def ev_x(mbx, tc, t0, tw, psum, tmark):
            p.wait("s", tmark[0], tmark[1])
            e_act(p, convf[:, t0:t0 + tw], psum, AF.Identity)
            return p.marker("s")
        ems = linear_fm(p, lp, WinX[:, mb * 128:(mb + 1) * 128], xn2, D, 128,
                        ev_x)
        for m_ in ems:
            if m_ is not None:
                p.wait("v", m_[0], m_[1])
        dcol = mb
        p.wait("v", stxs, stxs.n)
        e_ts(p, cacc[:, 0:T], convf[:, 0:T],
             cvwx[:, 3 * 16 + dcol:3 * 16 + dcol + 1], None, op0=ALU.mult)
        for kk in range(3):
            sh = 3 - kk
            e_stt(p, cacc[:, sh:T], convf[:, 0:T - sh],
                  cvwx[:, kk * 16 + dcol:kk * 16 + dcol + 1], cacc[:, sh:T],
                  ALU.mult, ALU.add)
        vm = p.marker("v")
        p.wait("s", vm[0], vm[1])
        e_act(p, convo[:, 0:T], cacc[:, 0:T], AF.Silu,
              bias=cvbx[:, dcol:dcol + 1])
        sm = p.marker("s")
        p.wait("v", sm[0], sm[1])
        cb16 = cacc[:, 0:T].bitcast(BF16)
        e_copy(p, cb16[:, 0:T], convo[:, 0:T])
        vm2 = p.marker("v")
        p.wait("y", vm2[0], vm2[1])
        e_dma(p, "y", xs_d[mb * 128:(mb + 1) * 128, :], cb16[:, 0:T], stxs)

    for (Wsrc, cOff, cbcol, dst) in ((WinB, 0, 8, Bt_d), (WinC, 4, 9, Ct_d)):
        if cbcol == 9:
            e_dma(p, "y", BCw[:, 9:10], cbC.unsqueeze(1), cl)
            p.wait("s", cl, cl.n)
            p.wait("v", cl, cl.n)

        def ev_bc(mbx, tc, t0, tw, psum, tmark):
            p.wait("s", tmark[0], tmark[1])
            e_act(p, Bpre[:, t0:t0 + tw], psum[0:16, 0:tw], AF.Identity)
            return p.marker("s")
        ems = linear_fm(p, lp, Wsrc, xn2, D, NST0, ev_bc)
        for m_ in ems:
            if m_ is not None:
                p.wait("v", m_[0], m_[1])
        e_ts(p, Bacc[:, 0:T], Bpre[:, 0:T], BCw[0:16, cOff + 3:cOff + 4],
             None, op0=ALU.mult)
        for kk in range(3):
            sh = 3 - kk
            e_stt(p, Bacc[:, sh:T], Bpre[:, 0:T - sh],
                  BCw[0:16, cOff + kk:cOff + kk + 1], Bacc[:, sh:T],
                  ALU.mult, ALU.add)
        vm = p.marker("v")
        p.wait("s", vm[0], vm[1])
        e_act(p, Bpre[:, 0:T], Bacc[:, 0:T], AF.Silu,
              bias=BCw[0:16, cbcol:cbcol + 1])
        sm = p.marker("s")
        p.wait("y", sm[0], sm[1])
        sbc = p.sem("sB2")
        e_dma(p, "y", dst[:, :], Bpre[0:16, 0:T], sbc)
        p.wait("s", sbc, sbc.n)
        p.wait("v", sbc, sbc.n)

    sdt = p.sem("sdt0")
    for mb in range(2):
        def ev_dt(mbx, tc, t0, tw, psum, tmark, mb=mb):
            p.wait("s", tmark[0], tmark[1])
            p.wait("s", sdt, sdt.n)
            e_act(p, dcb[0][:, 0:tw], psum, AF.Identity,
                  bias=dbv[:, mb:mb + 1])
            sm0 = p.marker("s")
            p.wait("v", sm0[0], sm0[1])
            e_softplus(p, dcb[1], dcb[0], dcb[2], nw=tw)
            vm = p.marker("v")
            p.wait("s", vm[0], vm[1])
            p.wait("y", vm[0], vm[1])
            e_dma(p, "y", dt_d[mb * 128:(mb + 1) * 128, t0:t0 + tw],
                  dcb[1][:, 0:tw], sdt)
            e_act(p, dcb[2][:, 0:tw], dcb[1][:, 0:tw], AF.Exp,
                  scale=Avc[:, mb:mb + 1])
            sm = p.marker("s")
            p.wait("y", sm[0], sm[1])
            e_dma(p, "y", dA_d[mb * 128:(mb + 1) * 128, t0:t0 + tw],
                  dcb[2][:, 0:tw], sdt)
            return sm
        linear_fm(p, lp, WinD[:, mb * 128:(mb + 1) * 128], xn2, D, 128,
                  ev_dt)
    p.wait("s", sdt, sdt.n)
    p.wait("v", sdt, sdt.n)
    p.barrier("ytvsg")
    p.close_scope(sc2)
    p.close_scope(sc_x)

    sc3 = []
    TCH = 1024
    NTCH = T // TCH
    dArep = [p.sb("dArep0", [128, TCH], scope=sc3) for _ in range(2)]
    dtrep = p.sb("dtrep0", [128, TCH], scope=sc3)
    gch = [p.sb("gch0", [128, TCH], scope=sc3) for _ in range(16)]
    xs_ch = [p.sb("xsch0", [128, TCH], BF16, scope=sc3) for _ in range(16)]
    zss = p.sb("zss0", [128, TCH], BF16, scope=sc3)
    Brep = [p.sb("Brep0", [128, TCH], BF16, scope=sc3) for _ in range(NST0)]
    Crep = [p.sb("Crep0", [128, TCH], BF16, scope=sc3) for _ in range(NST0)]
    u2 = [p.sb("u20", [128, TCH], BF16, scope=sc3)]
    dxb0 = p.sb("dxb0", [128, TCH], BF16, scope=sc3)
    hscb = p.sb("hscb0", [128, TCH], BF16, scope=sc3)
    hsc = p.sb("hsc0", [128, TCH], scope=sc3)
    tmp2 = p.sb("tmp20", [128, TCH], scope=sc3)
    carry = [p.sb("car0", [128, NST0], scope=sc3) for _ in range(16)]
    Dpv = p.sb("Dpv0", [128, 16], scope=sc3)
    rwv = p.sb("rwv0", [128, 16], scope=sc3)
    xnch = [tmp2[:, 0:512], tmp2[:, 512:1024]]
    x0ch = [hsc[:, 0:512], hsc[:, 512:1024]]
    schh = [dArep[0][:, 0:512], dArep[0][:, 512:1024]]
    np3 = norm_pool(p, scope=sc3, slim=True)
    sdl = p.sem("sdl")
    for dt_ in range(16):
        e_dma(p, "y", Dpv[:, dt_:dt_ + 1],
              Dprep[dt_ * 128:(dt_ + 1) * 128].unsqueeze(1), sdl)
        e_dma(p, "y", rwv[:, dt_:dt_ + 1],
              rmsw[dt_ * 128:(dt_ + 1) * 128].unsqueeze(1), sdl)
        e_memset(p, carry[dt_][:], 0.0)
    p.wait("v", sdl, sdl.n)
    p.wait("s", sdl, sdl.n)
    p.barrier("ytvsg")
    brs = [p.sem("brsa"), p.sem("brsb")]
    xss = [p.sem("xssa"), p.sem("xssb")]
    zsm = p.sem("zsm")
    drs = [p.sem("drsa"), p.sem("drsb")]
    drs2 = p.sem("sdl")
    brt_d = {}

    def loadBC(tc2):
        cb0 = tc2 * TCH
        for n in range(NST0):
            e_dma(p, "g", Brep[n][:],
                  Bt_d[n:n + 1, cb0:cb0 + TCH].partition_broadcast(128)
                  .squeeze(1), brs[tc2 % 2])
        for n in range(NST0):
            _, bt2 = e_dma(p, "g", Crep[n][:],
                           Ct_d[n:n + 1, cb0:cb0 + TCH]
                           .partition_broadcast(128).squeeze(1),
                           brs[tc2 % 2])
        brt_d[tc2] = bt2

    for tch in range(NTCH):
        c0 = tch * TCH

        def load_dA(dt2, gate, c0=c0):
            b = dt2 % 2
            hl2 = dt2 * 16
            if gate is not None:
                p.wait("y", gate[0], gate[1])
            _, d1 = e_dma(p, "y", dArep[b][:],
                          dA_d[hl2:hl2 + 16, c0:c0 + TCH].unsqueeze(1)
                          .to_broadcast([16, 8, TCH]), drs[b])
            return d1

        def load_dt(dt2, gate, c0=c0):
            hl2 = dt2 * 16
            if gate is not None:
                p.wait("y", gate[0], gate[1])
            _, d2 = e_dma(p, "y", dtrep[:],
                          dt_d[hl2:hl2 + 16, c0:c0 + TCH].unsqueeze(1)
                          .to_broadcast([16, 8, TCH]), drs2)
            return d2
        if tch == 0:
            loadBC(0)
        for dt_ in range(16):
            _, xst = e_dma(p, "y", xs_ch[dt_][:],
                           xs_d[dt_ * 128:(dt_ + 1) * 128, c0:c0 + TCH],
                           xss[tch % 2])
        p.wait("v", xss[tch % 2], xst)
        p.wait("s", xss[tch % 2], xst)
        p.wait("v", brs[tch % 2], brt_d[tch])
        zst = {}
        yzm = [None]
        dr_t = {0: load_dA(0, None)}
        dt_t = {0: load_dt(0, None)}
        dr_use = [None, None]
        for dt_ in range(16):
            b = dt_ % 2
            if dt_ + 1 < 16:
                dr_t[dt_ + 1] = load_dA(dt_ + 1, dr_use[(dt_ + 1) % 2])
            p.wait("v", drs2, dt_t[dt_])
            e_tt(p, dxb0[:], dtrep[:], xs_ch[dt_][:], ALU.mult)
            vdx = p.marker("v")
            if dt_ + 1 < 16:
                dt_t[dt_ + 1] = load_dt(dt_ + 1, vdx)
            p.wait("v", drs[b], dr_t[dt_])
            if yzm[0] is not None:
                p.wait("y", yzm[0][0], yzm[0][1])
            _, zst[dt_] = e_dma(p, "y", zss[:],
                                zs_d[dt_ * 128:(dt_ + 1) * 128, c0:c0 + TCH],
                                zsm)
            for n in range(NST0):
                e_tt(p, u2[0][:], dxb0[:], Brep[n][:], ALU.mult)
                p.op("v", lambda eng, dt_=dt_, n=n, b=b:
                     eng.tensor_tensor_scan(
                         out=hscb[:], data0=dArep[b][:], data1=u2[0][:],
                         initial=carry[dt_][:, n:n + 1], op0=ALU.mult,
                         op1=ALU.add))
                e_copy(p, carry[dt_][:, n:n + 1], hscb[:, TCH - 1:TCH])
                if n == 0:
                    e_tt(p, gch[dt_][:], hscb[:], Crep[n][:], ALU.mult)
                else:
                    e_tt(p, hscb[:], hscb[:], Crep[n][:], ALU.mult)
                    e_tt(p, gch[dt_][:], gch[dt_][:], hscb[:], ALU.add)
            e_stt(p, gch[dt_][:], xs_ch[dt_][:], Dpv[:, dt_:dt_ + 1],
                  gch[dt_][:], ALU.mult, ALU.add)
            p.wait("v", zsm, zst[dt_])
            e_tt(p, gch[dt_][:], gch[dt_][:], zss[:], ALU.mult)
            yzm[0] = p.marker("v")
            dr_use[b] = yzm[0]
        vvm = p.marker("v")
        p.wait("s", vvm[0], vvm[1])
        p.wait("t", vvm[0], vvm[1])
        if tch + 1 < NTCH:
            p.wait("g", vvm[0], vvm[1])
            loadBC(tch + 1)

        def rms_apply(d, tc2, t0, tw, xa, mu_bc, rstd_bc):
            e_stt(p, tmp2[:, 0:tw], xa, rwv[:, d:d + 1], rstd_bc, ALU.mult,
                  ALU.mult)
            e_copy(p, xs_ch[d][:, t0:t0 + tw], tmp2[:, 0:tw])

        norm_feat(p, np3, ("sb", gch), rms_apply, 1e-5, nd=16,
                  mean_center=False, n_free=TCH, n_feat=DI0)
        h1st = [p.sem("h1sa"), p.sem("h1sb")]
        xnld = [p.sem("xla"), p.sem("xlb")]
        xt_l = {}
        evd = {}

        def ev_out(mb, tc, t0, tw, psum, tmark, c0=c0, h1st=h1st, xnld=xnld,
                   xt_l=xt_l, evd=evd):
            u = (mb * (TCH // 512) + tc) % 2
            uk = mb * (TCH // 512) + tc
            if uk == 0:
                for k2 in (0, 1):
                    e_dma(p, "y", xnch[k2][:],
                          xn_out[0:128, c0 + k2 * 512:c0 + (k2 + 1) * 512],
                          xnld[k2])
                    _, xt_l[k2] = e_dma(p, "y", x0ch[k2][:],
                                        x0[0:128, c0 + k2 * 512:
                                           c0 + (k2 + 1) * 512], xnld[k2])
            p.wait("s", tmark[0], tmark[1])
            p.wait("s", xnld[u], xt_l[uk])
            if uk >= 2 and (uk - 2) in evd:
                p.wait("s", h1st[u], evd[uk - 2])
            e_act(p, schh[u][:, 0:tw], psum, AF.Identity)
            sm = p.marker("s")
            p.wait("v", sm[0], sm[1])
            e_tt(p, schh[u][:, 0:tw], schh[u][:, 0:tw], xnch[u][:, 0:tw],
                 ALU.add)
            e_tt(p, schh[u][:, 0:tw], schh[u][:, 0:tw], x0ch[u][:, 0:tw],
                 ALU.add)
            vm = p.marker("v")
            if uk + 2 < 8 * (TCH // 512):
                nk_ = uk + 2
                mb2, tc2 = nk_ // (TCH // 512), nk_ % (TCH // 512)
                p.wait("y", vm[0], vm[1])
                e_dma(p, "y", xnch[u][:],
                      xn_out[mb2 * 128:(mb2 + 1) * 128,
                             c0 + tc2 * 512:c0 + tc2 * 512 + 512], xnld[u])
                _, xt_l[nk_] = e_dma(p, "y", x0ch[u][:],
                                     x0[mb2 * 128:(mb2 + 1) * 128,
                                        c0 + tc2 * 512:c0 + tc2 * 512 + 512],
                                     xnld[u])
            p.wait("y", vm[0], vm[1])
            _, st_ = e_dma(p, "y", s_out[mb * 128:(mb + 1) * 128,
                                         c0 + t0:c0 + t0 + tw],
                           schh[u][:, 0:tw], h1st[u])
            evd[uk] = st_
            return vm

        linear_fm(p, lp, WoutT, xs_ch, DI0, D, ev_out, n_free=TCH)
        p.barrier("ytvsg")
    p.close_scope(sc3)
    if own:
        final_wait_all(p)
        return p.finalize()


_NC_CACHE = {}


def _get_nc(name, builder):
    if name not in _NC_CACHE:
        _NC_CACHE[name] = builder()
    return _NC_CACHE[name]


def _runT(name, builder, per_core, shared, outs):
    nc = _get_nc(name, builder)
    in_maps = []
    for c in range(B):
        m = dict(shared)
        for k, v in per_core.items():
            m[k] = v[c]
        in_maps.append(m)
    res = run_bass_kernel_spmd(nc, in_maps, core_ids=list(range(B)))
    return {o: [np.asarray(res.results[c][o]) for c in range(B)] for o in outs}


def kernel_unfused(**inputs):
    C = np.ascontiguousarray
    f32 = np.float32
    g = lambda k: np.asarray(inputs[k])
    x = g("L0_attn_x")
    xs = [C(x[b].T).astype(f32, copy=False) for b in range(B)]
    iden = np.eye(128, dtype=f32)
    DI0, NST0, NH0 = 2048, 16, 256

    Win = g("L0_attn_W_in")
    cw0 = g("L0_attn_conv_w")
    cb0 = g("L0_attn_conv_b")
    sh0 = {
        "ln_w": g("L0_attn_ln_w"), "ln_b": g("L0_attn_ln_b"),
        "WinZ": C(Win[:, :DI0]), "WinX": C(Win[:, DI0:2 * DI0]),
        "WinB": C(Win[:, 2 * DI0:2 * DI0 + NST0]),
        "WinC": C(Win[:, 2 * DI0 + NST0:2 * DI0 + 2 * NST0]),
        "WinD": C(Win[:, -NH0:]),
        "cwx": C(cw0[:, 0, :DI0]), "cwB": C(cw0[:, 0, DI0:DI0 + NST0]),
        "cwC": C(cw0[:, 0, DI0 + NST0:]),
        "cbx": C(cb0[:DI0]), "cbB": C(cb0[DI0:DI0 + NST0]),
        "cbC": C(cb0[DI0 + NST0:]),
        "A_log": g("L0_attn_A_log"),
        "Dp_rep": C(np.repeat(g("L0_attn_Dp"), 8)),
        "dt_bias": g("L0_attn_dt_bias"), "rms_w": g("L0_attn_rms_w"),
        "W_out": g("L0_attn_W_out"),
    }
    r = _runT("L0a", build_L0a, {"x": xs}, sh0, ["xn", "s"])
    xn_c, s_c = r["xn"], r["s"]
    sh = {"weight": g("L0_ffn_weight"), "lng": g("L0_ffn_ln_gamma"),
          "lnb": g("L0_ffn_ln_beta")}
    r = _runT("L0f", build_L0f, {"xn": xn_c, "sin": s_c}, sh, ["x1"])
    xs = r["x1"]

    blk2 = np.zeros((128, 2), f32)
    blk2[0:64, 0] = 1.0
    blk2[64:128, 1] = 1.0
    sh = {"Wq": g("L1_attn_Wq"), "Wk": g("L1_attn_Wk"), "Wv": g("L1_attn_Wv"),
          "Wa": g("L1_attn_Wa"), "ba": g("L1_attn_ba"), "Wo": g("L1_attn_Wo"),
          "gamma": g("L1_attn_gamma"), "beta": g("L1_attn_beta"),
          "blk2": blk2, "blk2b": C(blk2.T),
          "triu": C(np.triu(np.ones((128, 128), f32))), "iden": iden}
    r = _runT("L1a", build_L1a, {"x": xs}, sh, ["xn", "s"])
    xn_c, s_c = r["xn"], r["s"]
    sh = {"g1": g("L1_ffn_g1"), "g2": g("L1_ffn_g2"),
          "WqT": C(g("L1_ffn_Wq").T), "WkcT": C(g("L1_ffn_Wkc").T),
          "WvcT": C(g("L1_ffn_Wvc").T), "WoT": C(g("L1_ffn_Wo").T),
          "WrT": C(g("L1_ffn_Wr").T), "ebias": g("L1_ffn_expert_bias"),
          "sW1T": C(g("L1_ffn_sW1").T), "sW2T": C(g("L1_ffn_sW2").T),
          "sW3T": C(g("L1_ffn_sW3").T),
          "eW1T": C(g("L1_ffn_eW1").transpose(0, 2, 1)),
          "eW2T": C(g("L1_ffn_eW2").transpose(0, 2, 1)),
          "eW3T": C(g("L1_ffn_eW3").transpose(0, 2, 1)), "iden": iden}
    r = _runT("L1f", build_L1f, {"xn": xn_c, "sin": s_c}, sh, ["x2"])
    xs = r["x2"]

    xpw = g("L2_attn_x_proj_w")
    trid = np.zeros((4, 128, 512), f32)
    for j in range(4):
        si = np.arange(128)[:, None]
        ti = np.arange(512)[None, :]
        trid[j] = ((j * 128 + si) <= ti).astype(f32)
    sh = {"norm_m_w": g("L2_attn_norm_m_w"),
          "in_projT": C(g("L2_attn_in_proj_w").T),
          "conv_w": C(g("L2_attn_conv_w")[:, 0, :].T),
          "conv_b": g("L2_attn_conv_b"),
          "xpBT": C(xpw[:16].T), "xpCT": C(xpw[16:32].T),
          "xpdT": C(xpw[32:33].T), "dt_w": C(g("L2_attn_dt_w")[:, 0]),
          "dt_b": g("L2_attn_dt_b"), "A_log": g("L2_attn_A_log"),
          "D_skip": g("L2_attn_D_skip"),
          "out_projT": C(g("L2_attn_out_proj_w").T),
          "norm_attn_w": g("L2_attn_norm_attn_w"),
          "qwT": C(g("L2_attn_q_w").T), "kwT": C(g("L2_attn_k_w").T),
          "vwT": C(g("L2_attn_v_w").T), "owT": C(g("L2_attn_o_w").T),
          "norm_ffn_w": g("L2_attn_norm_ffn_w"),
          "gwT": C(g("L2_attn_gate_w").T), "uwT": C(g("L2_attn_up_w").T),
          "dwT": C(g("L2_attn_down_w").T), "trid": trid}
    r = _runT("L2a", build_L2a, {"x": xs}, sh, ["xn", "s"])
    xn_c, s_c = r["xn"], r["s"]
    sh = {"router_w": g("L2_ffn_router_w"), "router_b": g("L2_ffn_router_b"),
          "w1": g("L2_ffn_w1"), "w3": g("L2_ffn_w3"), "w2": g("L2_ffn_w2"),
          "iden": iden}
    r = _runT("L2f", build_L2f, {"xn": xn_c, "sin": s_c}, sh, ["x3"])
    xs = r["x3"]

    sh = {"w_qkv": g("L3_attn_w_qkv"), "b_qkv": g("L3_attn_b_qkv"),
          "w_proj": g("L3_attn_w_proj"), "b_proj": g("L3_attn_b_proj")}
    r = _runT("L3a", build_L3a, {"x": xs}, sh, ["xn", "s"])
    xn_c, s_c = r["xn"], r["s"]
    sh = {"qT": C(g("L3_ffn_qweights").transpose(0, 2, 1)),
          "scT": C(np.repeat(g("L3_ffn_scales"), 16, axis=2)
                   .transpose(0, 2, 1)),
          "biases": g("L3_ffn_biases"),
          "laT": C(g("L3_ffn_lora_a").transpose(0, 2, 1)),
          "lbT": C(g("L3_ffn_lora_b").transpose(0, 2, 1)),
          "lng": g("L3_ffn_ln_gamma"), "lnb": g("L3_ffn_ln_beta")}
    r = _runT("L3f", build_L3f, {"xn": xn_c, "sin": s_c}, sh, ["x4"])
    out = np.stack([np.asarray(v, f32).T for v in r["x4"]])
    return np.ascontiguousarray(out)


def build_fused(chain_n=8):
    nc = bass.Bass("TRN2", target_bir_lowering=False)
    p = P(nc)
    ins_map = []
    x_in = nc.dram_tensor("x", [D, T], F32, kind="ExternalInput").ap()
    out = nc.dram_tensor("out", [D, T], F32, kind="ExternalOutput").ap()
    chain = [("L0a", build_L0a), ("L0f", build_L0f),
             ("L1a", build_L1a), ("L1f", build_L1f),
             ("L2a", build_L2a), ("L2f", build_L2f),
             ("L3a", build_L3a), ("L3f", build_L3f)][:chain_n]
    cur = x_in
    l2f_state = {}
    for i, (tag, bld) in enumerate(chain):
        is_attn = tag.endswith("a")
        io = {"__tag__": tag + "_"}
        if is_attn:
            io["x"] = cur
        else:
            io["xn"] = xn_ap
            io["sin"] = s_ap
        if i == chain_n - 1:
            io[{"L0a": "s", "L0f": "x1", "L1a": "s", "L1f": "x2",
                "L2a": "s", "L2f": "x3", "L3a": "s", "L3f": "x4"}[tag]] = out
        sc = []
        p.scope_stack.append(sc)
        if tag == "L2a" and "L2f" in [t for t, _ in chain]:
            io2f = {"__tag__": "L2f_"}
            if i + 1 == chain_n - 1:
                io2f["x3"] = out
            handles2f = {}
            genbox = [None]

            def l2f_start(io=io, io2f=io2f):
                io2f["xn"] = io["xn"]
                io2f["sin"] = io["s"]
                genbox[0] = _L2f_stream((nc, p, io2f), handles2f)
                next(genbox[0])

            def l2f_mid():
                next(genbox[0])

            def l2f_step():
                next(genbox[0], None)

            def l2f_end():
                while True:
                    if next(genbox[0], "__done__") == "__done__":
                        break

            bld((nc, p, io), copar_start=l2f_start, copar_mid=l2f_mid,
                copar=l2f_step, copar_end=l2f_end)
            l2f_state["io"] = io2f
            l2f_state["handles"] = handles2f
        elif tag == "L2f" and l2f_state:
            io = l2f_state["io"]
            _L2f_final((nc, p, io), l2f_state["handles"])
        else:
            bld((nc, p, io))
        p.barrier("ytvsg")
        p.scope_stack.pop()
        p.close_scope(sc)
        for nm in io.get("__ins__", []):
            ins_map.append(nm)
        if is_attn:
            xn_ap = io["xn"]
            s_ap = io["s"]
        else:
            cur = io[{"L0f": "x1", "L1f": "x2", "L2f": "x3", "L3f": "x4"}[tag]]
    p.sem_prefix = ""
    final_wait_all(p)
    return p.finalize(), ins_map


_FUSED = {}


def kernel(**inputs):
    C = np.ascontiguousarray
    f32 = np.float32
    if "nc" not in _FUSED:
        _FUSED["nc"], _FUSED["ins"] = build_fused()
    nc, ins = _FUSED["nc"], _FUSED["ins"]
    shared, per_core = _prep_all(inputs)
    in_maps = []
    for c in range(B):
        m = {"x": per_core[c]}
        for dram_name, (tag, local) in [(dn, (dn.split("_", 1)[0], dn.split("_", 1)[1])) for dn in [n for n, _ in ins]] if False else []:
            pass
        for dram_name, local in ins:
            m[dram_name] = shared[dram_name]
        in_maps.append(m)
    res = run_bass_kernel_spmd(nc, in_maps, core_ids=list(range(B)))
    outp = np.stack([np.asarray(res.results[c]["out"], f32).T for c in range(B)])
    return np.ascontiguousarray(outp)


def _prep_all(inputs):
    C = np.ascontiguousarray
    f32 = np.float32
    g = lambda k: np.asarray(inputs[k])
    x = g("L0_attn_x")
    per_core = [C(x[b].T).astype(f32, copy=False) for b in range(B)]
    iden = np.eye(128, dtype=f32)
    DI0, NST0, NH0 = 2048, 16, 256
    Win = g("L0_attn_W_in")
    cw0 = g("L0_attn_conv_w")
    cb0 = g("L0_attn_conv_b")
    blk2 = np.zeros((128, 2), f32)
    blk2[0:64, 0] = 1.0
    blk2[64:128, 1] = 1.0
    xpw = g("L2_attn_x_proj_w")
    trid = np.zeros((4, 128, 512), f32)
    for j in range(4):
        si = np.arange(128)[:, None]
        ti = np.arange(512)[None, :]
        trid[j] = ((j * 128 + si) <= ti).astype(f32)
    sh = {}
    def put(tag, d):
        for k, v in d.items():
            sh[tag + "_" + k] = v
    put("L0a", {
        "ln_w": g("L0_attn_ln_w"), "ln_b": g("L0_attn_ln_b"),
        "WinZ": C(Win[:, :DI0]), "WinX": C(Win[:, DI0:2 * DI0]),
        "WinB": C(Win[:, 2 * DI0:2 * DI0 + NST0]),
        "WinC": C(Win[:, 2 * DI0 + NST0:2 * DI0 + 2 * NST0]),
        "WinD": C(Win[:, -NH0:]),
        "cwx": C(cw0[:, 0, :DI0]), "cwB": C(cw0[:, 0, DI0:DI0 + NST0]),
        "cwC": C(cw0[:, 0, DI0 + NST0:]),
        "cbx": C(cb0[:DI0]), "cbB": C(cb0[DI0:DI0 + NST0]),
        "cbC": C(cb0[DI0 + NST0:]),
        "A_log": g("L0_attn_A_log"), "Dp_rep": C(np.repeat(g("L0_attn_Dp"), 8)),
        "dt_bias": g("L0_attn_dt_bias"), "rms_w": g("L0_attn_rms_w"),
        "W_out": g("L0_attn_W_out")})
    put("L0f", {"weight": g("L0_ffn_weight"), "lng": g("L0_ffn_ln_gamma"),
                "lnb": g("L0_ffn_ln_beta")})
    put("L1a", {"Wq": g("L1_attn_Wq"), "Wk": g("L1_attn_Wk"),
                "Wv": g("L1_attn_Wv"), "Wa": g("L1_attn_Wa"),
                "ba": g("L1_attn_ba"), "Wo": g("L1_attn_Wo"),
                "gamma": g("L1_attn_gamma"), "beta": g("L1_attn_beta"),
                "blk2": blk2, "blk2b": C(blk2.T),
                "triu": C(np.triu(np.ones((128, 128), f32))), "iden": iden})
    put("L1f", {"g1": g("L1_ffn_g1"), "g2": g("L1_ffn_g2"),
                "WqT": C(g("L1_ffn_Wq").T), "WkcT": C(g("L1_ffn_Wkc").T),
                "WvcT": C(g("L1_ffn_Wvc").T), "WoT": C(g("L1_ffn_Wo").T),
                "WrT": C(g("L1_ffn_Wr").T), "ebias": g("L1_ffn_expert_bias"),
                "sW1T": C(g("L1_ffn_sW1").T), "sW2T": C(g("L1_ffn_sW2").T),
                "sW3T": C(g("L1_ffn_sW3").T),
                "eW1T": C(g("L1_ffn_eW1").transpose(0, 2, 1)),
                "eW2T": C(g("L1_ffn_eW2").transpose(0, 2, 1)),
                "eW3T": C(g("L1_ffn_eW3").transpose(0, 2, 1)), "iden": iden})
    put("L2a", {"norm_m_w": g("L2_attn_norm_m_w"),
                "in_projT": C(g("L2_attn_in_proj_w").T),
                "conv_w": C(g("L2_attn_conv_w")[:, 0, :].T),
                "conv_b": g("L2_attn_conv_b"),
                "xpBT": C(xpw[:16].T), "xpCT": C(xpw[16:32].T),
                "xpdT": C(xpw[32:33].T), "dt_w": C(g("L2_attn_dt_w")[:, 0]),
                "dt_b": g("L2_attn_dt_b"), "A_log": g("L2_attn_A_log"),
                "D_skip": g("L2_attn_D_skip"),
                "out_projT": C(g("L2_attn_out_proj_w").T),
                "norm_attn_w": g("L2_attn_norm_attn_w"),
                "qwT": C(g("L2_attn_q_w").T), "kwT": C(g("L2_attn_k_w").T),
                "vwT": C(g("L2_attn_v_w").T), "owT": C(g("L2_attn_o_w").T),
                "norm_ffn_w": g("L2_attn_norm_ffn_w"),
                "gwT": C(g("L2_attn_gate_w").T), "uwT": C(g("L2_attn_up_w").T),
                "dwT": C(g("L2_attn_down_w").T), "trid": trid})
    put("L2f", {"router_w": g("L2_ffn_router_w"),
                "router_b": g("L2_ffn_router_b"), "w1": g("L2_ffn_w1"),
                "w3": g("L2_ffn_w3"), "w2": g("L2_ffn_w2"), "iden": iden})
    put("L3a", {"w_qkv": g("L3_attn_w_qkv"), "b_qkv": g("L3_attn_b_qkv"),
                "w_proj": g("L3_attn_w_proj"), "b_proj": g("L3_attn_b_proj")})
    put("L3f", {"qT": C(g("L3_ffn_qweights").transpose(0, 2, 1)),
                "scT": C(np.repeat(g("L3_ffn_scales"), 16, axis=2)
                         .transpose(0, 2, 1)),
                "biases": g("L3_ffn_biases"),
                "laT": C(g("L3_ffn_lora_a").transpose(0, 2, 1)),
                "lbT": C(g("L3_ffn_lora_b").transpose(0, 2, 1)),
                "lng": g("L3_ffn_ln_gamma"), "lnb": g("L3_ffn_ln_beta")})
    return sh, per_core
